# version 15
# baseline (speedup 1.0000x reference)
import math
import numpy as np
import ml_dtypes
import concourse.bass as bass
import concourse.mybir as mybir
from concourse.bass_utils import run_bass_kernel_spmd
from concourse.alu_op_type import AluOpType
from bass_rust import ReduceOp
from concourse import library_config

F32 = mybir.dt.float32
BF16 = mybir.dt.bfloat16
FP8 = mybir.dt.float8e4
AF = mybir.ActivationFunctionType
AX = mybir.AxisListType
OP = AluOpType
P = 128


def full_cfg():
    return dict(D=2048, H=16, T=4096, TO=2048, QW=512, DFF=8192)


class Ctx:
    def __init__(self, nc):
        self.nc = nc
        self.sem = {}
        self.cnt = {}

    def add(self, name):
        self.sem[name] = self.nc.alloc_semaphore(name=name)
        self.cnt[name] = 0

    def inc(self, instr, name, n=1):
        instr.then_inc(self.sem[name], n)
        self.cnt[name] += n
        return self.cnt[name]

    def dma(self, eng, out, in_, name):
        eng.dma_start(out=out, in_=in_).then_inc(self.sem[name], 16)
        self.cnt[name] += 16
        return self.cnt[name]

    def wait(self, eng, name, target=None):
        eng.wait_ge(self.sem[name], self.cnt[name] if target is None else target)


class DBuf:

    def __init__(self, ctx, semnames):
        self.ctx = ctx
        self.names = list(semnames)
        self.n = len(self.names)
        self.tgt = {}

    def issue(self, eng, i, pairs):
        nm = self.names[i % self.n]
        for out, in_ in pairs:
            self.tgt[i] = self.ctx.dma(eng, out, in_, nm)
        return self.tgt[i]

    def wait(self, eng, i):
        eng.wait_ge(self.ctx.sem[self.names[i % self.n]], self.tgt[i])

    def wait_all(self, eng):
        for nm in self.names:
            self.ctx.wait(eng, nm)


def bcast_ap(t, inner):
    a = t if isinstance(t, bass.AP) else t.ap()
    return bass.AP(a.tensor, a.offset, [[a.ap[0][0], a.ap[0][1]]] +
                   [list(x) for x in a.ap[1:]] + [[0, inner]])


def build_program(cfg, phases=("w1", "w2", "a", "b", "d1", "d2")):
    D, H, T, TO, QW, DFF = (cfg[k] for k in ("D", "H", "T", "TO", "QW", "DFF"))
    FC = D // P
    NTT = T // P
    NTO = TO // P
    NSLOT = TO // QW
    ZONE = 2 * QW // P
    NI = DFF // 512
    NIS = DFF // P
    OCW = min(D, 512)
    NOCH = D // OCW
    QT = QW // P
    NDG = max(D // 512, 1)
    DG = min(D, 512)
    NCH = T // (4 * P)
    NCHO = TO // (4 * P)
    NG = D // 32
    NGI = 512 // 32
    assert D % P == 0 and H * P == D and QW % P == 0 and QW <= 512

    nc = bass.Bass("TRN2", target_bir_lowering=False)

    x_batch = nc.dram_tensor("x_batch", [T, D], F32, kind="ExternalInput").ap()
    x_own = nc.dram_tensor("x_own", [TO, D], F32, kind="ExternalInput").ap()
    w_in = {n: nc.dram_tensor("w" + n, [D, D], F32, kind="ExternalInput").ap()
            for n in "qkvo"}
    gate_w = nc.dram_tensor("gate_w", [DFF, D], F32, kind="ExternalInput").ap()
    up_w = nc.dram_tensor("up_w", [DFF, D], F32, kind="ExternalInput").ap()
    down_w = nc.dram_tensor("down_w", [D, DFF], F32, kind="ExternalInput").ap()
    cmask = nc.dram_tensor("cmask", [NSLOT, ZONE, P, QW], FP8,
                           kind="ExternalInput").ap()
    y_out = nc.dram_tensor("y_out", [TO, D], F32, kind="ExternalOutput").ap()

    s_wT = {n: nc.dram_tensor(f"s_w{n}T", [FC, P, D], BF16, kind="Internal").ap()
            for n in "qkvo"}
    s_kh = nc.dram_tensor("s_kh", [H, P, T], BF16, kind="Internal").ap()
    s_kl = nc.dram_tensor("s_kl", [H, P, T], BF16, kind="Internal").ap()
    s_v = nc.dram_tensor("s_v", [H, NTT, P, 2 * P], BF16, kind="Internal").ap()
    s_qh = nc.dram_tensor("s_qh", [H, P, TO], BF16, kind="Internal").ap()
    s_ql = nc.dram_tensor("s_ql", [H, P, TO], BF16, kind="Internal").ap()
    s_xp = nc.dram_tensor("s_xp", [NTO, P, D], F32, kind="Internal").ap()
    s_ax = nc.dram_tensor("s_ax", [FC, NTO, P, 2 * P], BF16, kind="Internal").ap()
    s_wg = nc.dram_tensor("s_wg", [FC, NI, P, 1024], BF16, kind="Internal").ap()
    s_wu = nc.dram_tensor("s_wu", [FC, NI, P, 1024], BF16, kind="Internal").ap()
    s_wd = nc.dram_tensor("s_wd", [NIS, NOCH, P, 2 * OCW], BF16, kind="Internal").ap()
    s_h = nc.dram_tensor("s_h", [NIS, NTO, P, 2 * P], BF16, kind="Internal").ap()
    s_fma = nc.dram_tensor("s_fma", [T // 512, 2, FC, P, 512], BF16,
                           kind="Internal").ap()
    s_fmq = nc.dram_tensor("s_fmq", [TO // 512, 2, FC, P, 512], BF16,
                           kind="Internal").ap()

    ctx = Ctx(nc)
    for s in ([f"q{i}" for i in range(6)] + [f"g{i}" for i in range(6)] +
              ["dS", "tMM", "tTR", "vMain", "vEvac", "vAux", "aMain", "gMain"]):
        ctx.add(s)

    ident = nc.sbuf_tensor("ident", [P, P], BF16).__enter__()
    ones_bf = nc.sbuf_tensor("ones_bf", [P, 1], BF16).__enter__()
    gam = {n: nc.sbuf_tensor(f"gam_{n}", [P, 1], F32).__enter__() for n in "qkvo"}
    invg = {n: nc.sbuf_tensor(f"invg_{n}", [P, 1], F32).__enter__() for n in "qkvo"}
    _CONSTS = {"neg_half": -0.5, "neg40": -40.0, "eps": 1e-5}
    cpool = nc.sbuf_tensor("cpool", [P, len(_CONSTS)], F32).__enter__()
    CP = {k: cpool[:, i:i + 1] for i, k in enumerate(_CONSTS)}
    onesf_c = nc.sbuf_tensor("onesf_c", [P, 1], F32).__enter__()
    onesf_r = nc.sbuf_tensor("onesf_r", [1, P], F32).__enter__()
    grow = nc.sbuf_tensor("grow", [1, 2], F32).__enter__()
    masks_cm = nc.sbuf_tensor("masks_sb", [P, NSLOT * ZONE * QW], FP8)
    masks_sb = masks_cm.__enter__()

    with nc.Block() as blk:
        @blk.vector
        def _(v):
            v.memset(ident[:], 0.0)
            v.memset(ones_bf[:], 1.0)
            v.memset(onesf_c[:], 1.0)
            v.memset(onesf_r[:], 1.0)
            for i, val in enumerate(_CONSTS.values()):
                v.memset(cpool[:, i:i + 1], val)

        @blk.sync
        def _(s):
            for sl in range(NSLOT):
                for z in range(ZONE):
                    ctx.dma(s, masks_sb[:, (sl * ZONE + z) * QW:
                                        (sl * ZONE + z + 1) * QW], cmask[sl, z], "dS")
            ctx.wait(s, "dS")

    with nc.Block() as blk:
        @blk.gpsimd
        def _(g):
            g.affine_select(out=ident[:], in_=ident[:], compare_op=OP.not_equal,
                            fill=1.0, base=0, pattern=[[-1, P]], channel_multiplier=1)

    def emit_bitlinear_prep(w_ap, wT_ap, gam_ap, invg_ap, tag, extra_scale=None):
        ntile = D // P
        nsub = D // P
        with (
            nc.sbuf_tensor(f"w1_wbuf_{tag}", [P, 2, D], F32) as wbuf,
            nc.sbuf_tensor(f"w1_abuf_{tag}", [P, D], F32) as abuf,
            nc.sbuf_tensor(f"w1_bbuf_{tag}", [P, D], F32) as bbuf,
            nc.sbuf_tensor(f"w1_part_{tag}", [P, ntile], F32) as part,
            nc.sbuf_tensor(f"w1_col_{tag}", [P, 1], F32) as col,
        ):
            lds = DBuf(ctx, ["q0", "q1"])
            vb = ctx.cnt["vMain"]
            with nc.Block() as blk:
                @blk.sync
                def _(s):
                    for i in range(ntile):
                        if i >= 2:
                            s.wait_ge(ctx.sem["vMain"], vb + i - 1)
                        lds.issue(s, i, [(wbuf[:, i % 2], w_ap[i * P:(i + 1) * P, :])])

                @blk.vector
                def _(v):
                    for i in range(ntile):
                        lds.wait(v, i)
                        ctx.inc(v.tensor_reduce(part[:, i:i + 1], wbuf[:, i % 2],
                                                AX.X, OP.add,
                                                apply_absolute_value=True), "vMain")
            with nc.Block() as blk:
                @blk.vector
                def _(v):
                    v.tensor_reduce(col[:], part[:], AX.X, OP.add)
            with nc.psum_tensor(f"w1_psg_{tag}", [P, 512], F32) as psg:
                with nc.Block() as blk:
                    @blk.tensor
                    def _(t):
                        t.matmul(psg[0:1, 0:1], col[:], onesf_c[:], start=True,
                                 stop=True)
                with nc.Block() as blk:
                    @blk.vector
                    def _(v):
                        v.tensor_scalar(grow[0:1, 0:1], psg[0:1, 0:1],
                                        1.0 / (D * D), 1e-5, OP.mult, OP.max)
                        v.drain()
                        v.reciprocal(grow[0:1, 1:2], grow[0:1, 0:1])
                        if extra_scale is not None:
                            v.drain()
                            v.tensor_scalar(grow[0:1, 0:1], grow[0:1, 0:1],
                                            extra_scale, None, OP.mult)
                with nc.Block() as blk:
                    @blk.tensor
                    def _(t):
                        t.matmul(psg[:, 0:2], onesf_r[:], grow[:], start=True,
                                 stop=True)
                with nc.Block() as blk:
                    @blk.vector
                    def _(v):
                        v.tensor_copy(gam_ap[:], psg[:, 0:1])
                        v.tensor_copy(invg_ap[:], psg[:, 1:2])

            lds2 = DBuf(ctx, ["q0", "q1"])
            spl = DBuf(ctx, ["g0", "g1", "g2", "g3"])
            ab = ctx.cnt["aMain"]
            vqb = ctx.cnt["vMain"]
            veb = ctx.cnt["vEvac"]
            tb_ = ctx.cnt["tTR"]
            with (
                nc.sbuf_tensor(f"w1_sgn_{tag}", [P, 2, D], BF16) as sgn,
                nc.sbuf_tensor(f"w1_rl_{tag}", [P, 2, D], BF16) as rl,
                nc.sbuf_tensor(f"w1_q_{tag}", [P, 2, D], BF16) as qb,
                nc.sbuf_tensor(f"w1_tb_{tag}", [P, 4, 512], BF16) as tbuf,
                nc.psum_tensor(f"w1_ps_{tag}", [P, 2, 1024], BF16) as ps,
                nc.Block() as blk,
            ):
                @blk.sync
                def _(s):
                    for i in range(ntile):
                        if i >= 2:
                            s.wait_ge(ctx.sem["aMain"], ab + i - 1)
                        lds2.issue(s, i, [(wbuf[:, i % 2], w_ap[i * P:(i + 1) * P, :])])

                @blk.gpsimd
                def _(g):
                    for i in range(ntile):
                        for jg in range(nsub // 4):
                            kg = i * (nsub // 4) + jg
                            g.wait_ge(ctx.sem["vEvac"], veb + kg + 1)
                            dstg = wT_ap[jg * 4:(jg + 1) * 4, :,
                                         i * P:(i + 1) * P].rearrange(
                                             "j p c -> p j c")
                            spl.issue(g, kg, [(dstg, tbuf[:, kg % 4])])
                    spl.wait_all(g)

                @blk.scalar
                def _(sc):
                    for i in range(ntile):
                        lds2.wait(sc, i)
                        if i >= 2:
                            sc.wait_ge(ctx.sem["vMain"], vqb + i - 1)
                        sc.activation(abuf[:], wbuf[:, i % 2], AF.Abs,
                                      scale=invg_ap[:])
                        sc.drain()
                        sc.activation(bbuf[:], abuf[:], AF.Sign, bias=CP["neg_half"])
                        sc.drain()
                        sc.activation(rl[:, i % 2], bbuf[:], AF.Relu)
                        sc.drain()
                        ctx.inc(sc.activation(sgn[:, i % 2], wbuf[:, i % 2], AF.Sign),
                                "aMain")

                @blk.tensor
                def _(t):
                    for i in range(ntile):
                        t.wait_ge(ctx.sem["vMain"], vqb + i + 1)
                        for jg in range(nsub // 4):
                            kg = i * (nsub // 4) + jg
                            if kg >= 2:
                                t.wait_ge(ctx.sem["vEvac"], veb + kg - 1)
                            for jj in range(4):
                                j = jg * 4 + jj
                                ins = t.transpose(
                                    ps[:, kg % 2, jj * P:(jj + 1) * P],
                                    qb[:, i % 2, j * P:(j + 1) * P], ident[:])
                            ctx.inc(ins, "tTR")

                @blk.vector
                def _(v):
                    ng = nsub // 4
                    for i in range(ntile + 1):
                        if i < ntile:
                            v.wait_ge(ctx.sem["aMain"], ab + i + 1)
                            if i >= 2:
                                v.wait_ge(ctx.sem["tTR"], tb_ + (i - 1) * ng)
                            ctx.inc(v.tensor_tensor(qb[:, i % 2], rl[:, i % 2],
                                                    sgn[:, i % 2], OP.mult), "vMain")
                        if i >= 1:
                            for jg in range(ng):
                                kg = (i - 1) * ng + jg
                                v.wait_ge(ctx.sem["tTR"], tb_ + kg + 1)
                                if kg >= 4:
                                    spl.wait(v, kg - 4)
                                ctx.inc(v.tensor_copy(tbuf[:, kg % 4],
                                                      ps[:, kg % 2, 0:512]),
                                        "vEvac")

    if "w1" in phases:
        for tag in "qkvo":
            emit_bitlinear_prep(w_in[tag], s_wT[tag], gam[tag], invg[tag], tag,
                                extra_scale=(1.0 / math.sqrt(P) if tag == "q"
                                             else None))

    def emit_mx_weight_prep(w_ap, nrow, ncol, dst, dst_kind, tag):
        ntile = nrow // P
        nsub = ncol // P
        ngrp = ncol // 32
        pair = dst_kind == "gu"
        ndq = 2 if pair else 1
        nwb = 4 if pair else 2
        lds = DBuf(ctx, ["q0", "q1"])
        spl = DBuf(ctx, ["q2", "q3", "q4", "q5"])
        ab = ctx.cnt["aMain"]
        vqb = ctx.cnt["vMain"]
        veb = ctx.cnt["vEvac"]
        tb_ = ctx.cnt["tTR"]
        gb = ctx.cnt["gMain"]
        with (
            nc.sbuf_tensor(f"w2_wbuf_{tag}", [P, nwb, ncol], F32) as wbuf,
            nc.sbuf_tensor(f"w2_ma_{tag}", [P, 2 if pair else 1, 2, ngrp],
                           F32) as ma,
            nc.sbuf_tensor(f"w2_r2_{tag}", [P, 2 if pair else 1, 2, ngrp],
                           F32) as r2,
            nc.sbuf_tensor(f"w2_q8_{tag}", [P, 2 if pair else 1, ncol],
                           FP8) as q8,
            nc.sbuf_tensor(f"w2_dq_{tag}", [P, ndq, ncol], F32) as dq,
            nc.sbuf_tensor(f"w2_hi_{tag}", [P, 2, ncol], BF16) as hib,
            nc.sbuf_tensor(f"w2_lo_{tag}", [P, 2, ncol], BF16) as lob,
            nc.sbuf_tensor(f"w2_tb_{tag}", [P, 4, 1024], BF16) as tbuf,
            nc.psum_tensor(f"w2_ps_{tag}", [P, 2, 1024], BF16) as ps,
            nc.Block() as blk,
        ):
            gsz = max(g for g in (8, 4, 2, 1) if nsub % g == 0)
            ng = nsub // gsz

            def dst_group(rt, jg, piece):
                if dst_kind == "gu":
                    ic, pos = (rt * P) // 512, rt % 4
                    off = (0 if piece == 0 else 512) + pos * P
                    d = dst[jg * gsz:(jg + 1) * gsz, ic, :, off:off + P]
                else:
                    och, pos = (rt * P) // OCW, rt % (OCW // P)
                    off = (0 if piece == 0 else OCW) + pos * P
                    d = dst[jg * gsz:(jg + 1) * gsz, och, :, off:off + P]
                return d.rearrange("j p c -> p j c")

            @blk.sync
            def _(s):
                for i in range(ntile):
                    if i >= 2:
                        lds.wait(s, i - 2)
                    if i >= nwb:
                        s.wait_ge(ctx.sem["vMain"], vqb + i - nwb + 1)
                    lds.issue(s, i, [(wbuf[:, i % nwb],
                                      w_ap[i * P:(i + 1) * P, :])])

            @blk.scalar
            def _(sc):
                def evacs(i):
                    for piece in range(2):
                        for jg in range(ng):
                            kg = (i * 2 + piece) * ng + jg
                            sc.wait_ge(ctx.sem["tTR"], tb_ + kg + 1)
                            if kg >= 4:
                                spl.wait(sc, kg - 4)
                            ctx.inc(sc.activation(tbuf[:, kg % 4, 0:gsz * P],
                                                  ps[:, kg % 2, 0:gsz * P],
                                                  AF.Identity), "vEvac")
                            if kg >= 1:
                                sc.wait_ge(ctx.sem["vEvac"], veb + kg)
                                ig, r = divmod(kg - 1, 2 * ng)
                                spl.issue(sc, kg - 1,
                                          [(dst_group(ig, r % ng, r // ng),
                                            tbuf[:, (kg - 1) % 4, 0:gsz * P])])
                for i in range(ntile):
                    sc.wait_ge(ctx.sem["vMain"], vqb + i + 1)
                    if i >= 2:
                        sc.wait_ge(ctx.sem["tTR"], tb_ + 2 * ng * (i - 1))
                    ctx.inc(sc.activation(hib[:, i % 2], dq[:, i % ndq],
                                          AF.Identity), "aMain")
                    if i >= 1:
                        evacs(i - 1)
                evacs(ntile - 1)
                kg = 2 * ng * ntile - 1
                sc.wait_ge(ctx.sem["vEvac"], veb + kg + 1)
                ig, r = divmod(kg, 2 * ng)
                spl.issue(sc, kg, [(dst_group(ig, r % ng, r // ng),
                                    tbuf[:, kg % 4, 0:gsz * P])])
                spl.wait_all(sc)

            @blk.tensor
            def _(t):
                for i in range(ntile):
                    t.wait_ge(ctx.sem["aMain"], ab + i + 1)
                    for piece, srcb in ((0, hib), (1, lob)):
                        if piece == 1:
                            t.wait_ge(ctx.sem["gMain"], gb + i + 1)
                        for jg in range(ng):
                            kg = (i * 2 + piece) * ng + jg
                            if kg >= 2:
                                t.wait_ge(ctx.sem["vEvac"], veb + kg - 1)
                            for jj in range(gsz):
                                j = jg * gsz + jj
                                ins = t.transpose(
                                    ps[:, kg % 2, jj * P:(jj + 1) * P],
                                    srcb[:, i % 2, j * P:(j + 1) * P], ident[:])
                            ctx.inc(ins, "tTR")

            @blk.gpsimd
            def _(g):
                for i in range(ntile):
                    g.wait_ge(ctx.sem["vMain"], vqb + i + 1)
                    g.wait_ge(ctx.sem["aMain"], ab + i + 1)
                    if i >= 2:
                        g.wait_ge(ctx.sem["tTR"], tb_ + 2 * ng * (i - 1))
                    ctx.inc(g.tensor_tensor(lob[:, i % 2], dq[:, i % ndq],
                                            hib[:, i % 2], OP.subtract), "gMain")

            @blk.vector
            def _(v):
                step = 2 if pair else 1
                for i0 in range(0, ntile, step):
                    tiles = list(range(i0, min(i0 + step, ntile)))
                    pj = (i0 // 2) % 2

                    def mslot(i):
                        return (pj, i % 2) if pair else (0, i % 2)

                    for i in tiles:
                        lds.wait(v, i)
                        w3 = wbuf[:, i % nwb].rearrange("p (g b) -> p g b", b=32)
                        v.tensor_reduce(ma[:, mslot(i)[0], mslot(i)[1]], w3,
                                        AX.X, OP.max, apply_absolute_value=True)
                    v.drain()
                    if pair and len(tiles) == 2:
                        mboth = ma[:, pj].rearrange("p s g -> p (s g)")
                        rboth = r2[:, pj].rearrange("p s g -> p (s g)")
                        v.tensor_scalar(mboth, mboth, 2.0 / 448.0, 2e-30,
                                        OP.mult, OP.max)
                        v.drain()
                        v.reciprocal(rboth, mboth)
                    else:
                        i = tiles[0]
                        sl = mslot(i)
                        v.tensor_scalar(ma[:, sl[0], sl[1]], ma[:, sl[0], sl[1]],
                                        2.0 / 448.0, 2e-30, OP.mult, OP.max)
                        v.drain()
                        v.reciprocal(r2[:, sl[0], sl[1]], ma[:, sl[0], sl[1]])
                    v.drain()
                    for i in tiles:
                        w3 = wbuf[:, i % nwb].rearrange("p (g b) -> p g b", b=32)
                        sl = mslot(i)
                        v.tensor_tensor(
                            q8[:, i % 2 if pair else 0].rearrange(
                                "p (g b) -> p g b", b=32), w3,
                            bcast_ap(r2[:, sl[0], sl[1]], 32), OP.mult)
                    v.drain()
                    for i in tiles:
                        if i >= ndq:
                            v.wait_ge(ctx.sem["aMain"], ab + i - ndq + 1)
                            v.wait_ge(ctx.sem["gMain"], gb + i - ndq + 1)
                        sl = mslot(i)
                        ctx.inc(v.tensor_tensor(
                            dq[:, i % ndq].rearrange("p (g b) -> p g b", b=32),
                            q8[:, i % 2 if pair else 0].rearrange(
                                "p (g b) -> p g b", b=32),
                            bcast_ap(ma[:, sl[0], sl[1]], 32), OP.mult), "vMain")

    if "w2" in phases:
        emit_mx_weight_prep(gate_w, DFF, D, s_wg, "gu", "g")
        emit_mx_weight_prep(up_w, DFF, D, s_wu, "gu", "u")
        emit_mx_weight_prep(down_w, D, DFF, s_wd, "d", "d")

    def spidx(qc, kk, np_):
        return qc * NDG * np_ + kk

    def emit_ln_spill(x_ap, nchunk, fm_dst, tagp):
        NT4 = 4
        NGRP = D // 512
        ngf = FC // 4
        NTILE = nchunk * NT4
        lds = DBuf(ctx, ["q0", "q1"])
        spl = DBuf(ctx, ["q2", "q3"])
        ab = ctx.cnt["aMain"]
        vb = ctx.cnt["vMain"]
        xb = ctx.cnt["vAux"]
        veb = ctx.cnt["vEvac"]
        tb_ = ctx.cnt["tTR"]
        gb = ctx.cnt["gMain"]
        order = [("a", 0), ("a", 1)] + \
            [x for t in range(NT4 - 2) for x in (("b", t), ("a", t + 2))] + \
            [("b", NT4 - 2), ("b", NT4 - 1)]
        act_order = [("s", 0), ("s", 1)] + \
            [x for t in range(NT4 - 2) for x in (("x", t), ("s", t + 2))] + \
            [("x", NT4 - 2), ("x", NT4 - 1)]
        pos_nm, pos_lo = {}, {}
        for qc in range(nchunk):
            for n_, (kind_, t_) in enumerate(order):
                (pos_nm if kind_ == "a" else pos_lo)[qc * NT4 + t_] = \
                    vb + qc * 8 + n_ + 1
        pos_sq, pos_x = {}, {}
        for qc in range(nchunk):
            for n_, (kind_, t_) in enumerate(act_order):
                (pos_sq if kind_ == "s" else pos_x)[qc * NT4 + t_] = \
                    ab + qc * 8 + n_ + 1
        pos_ag = {gi: xb + gi // NT4 * 4 + gi % NT4 + 1 for gi in range(NTILE)}
        pos_xh = {gi: gb + gi + 1 for gi in range(NTILE)}

        with (
            nc.sbuf_tensor(f"a_x_{tagp}", [P, 2, D], F32) as xbuf,
            nc.sbuf_tensor(f"a_stat_{tagp}", [P, 4, NGRP * 6], F32) as bns,
            nc.sbuf_tensor(f"a_mv_{tagp}", [P, 4, 2], F32) as mv,
            nc.sbuf_tensor(f"a_rv_{tagp}", [P, 4], F32) as rv,
            nc.sbuf_tensor(f"a_sq_{tagp}", [P, 4], F32) as sq,
            nc.sbuf_tensor(f"a_nm_{tagp}", [P, 4], F32) as nm,
            nc.sbuf_tensor(f"a_xln_{tagp}", [P, D], F32) as xlnb,
            nc.sbuf_tensor(f"a_xh_{tagp}", [P, 2, D], BF16) as xh,
            nc.sbuf_tensor(f"a_xl_{tagp}", [P, 2, D], BF16) as xl,
            nc.sbuf_tensor(f"a_fmh_{tagp}", [P, 2, FC, 4 * P], BF16) as fmh,
            nc.sbuf_tensor(f"a_fml_{tagp}", [P, 2, FC, 4 * P], BF16) as fml,
            nc.psum_tensor(f"a_pst_{tagp}", [P, 2, 1024], BF16) as pst,
            nc.Block() as blk,
        ):
            @blk.sync
            def _(s):
                def spills(qc):
                    for piece, buf in ((0, fmh), (1, fml)):
                        idx = qc * 2 + piece
                        if idx >= 2:
                            spl.wait(s, idx - 2)
                        s.wait_ge(ctx.sem["vEvac"],
                                  veb + (qc + 1) * NT4 * 2 * ngf)
                        spl.issue(s, idx,
                                  [(fm_dst[qc, piece].rearrange(
                                      "f p c -> p f c"), buf[:, qc % 2])])
                for qc in range(nchunk):
                    for t4 in range(NT4):
                        gi = qc * NT4 + t4
                        if gi >= 2:
                            lds.wait(s, gi - 2)
                            s.wait_ge(ctx.sem["aMain"], pos_x[gi - 2])
                        lds.issue(s, gi, [(xbuf[:, gi % 2],
                                           x_ap[gi * P:(gi + 1) * P, :])])
                    if qc >= 1:
                        spills(qc - 1)
                spills(nchunk - 1)

            @blk.scalar
            def _(sc):
                for qc in range(nchunk):
                    for kind, t4 in act_order:
                        gi = qc * NT4 + t4
                        if kind == "s":
                            sc.wait_ge(ctx.sem["vAux"], pos_ag[gi])
                            if qc >= 1:
                                sc.wait_ge(ctx.sem["vMain"],
                                           pos_nm[gi - NT4])
                            ctx.inc(sc.activation(sq[:, t4:t4 + 1],
                                                  mv[:, t4, 1:2], AF.Sqrt,
                                                  bias=CP["eps"]), "aMain")
                        else:
                            sc.wait_ge(ctx.sem["vMain"], pos_nm[gi])
                            if gi >= 1:
                                sc.wait_ge(ctx.sem["vMain"], pos_lo[gi - 1])
                                sc.wait_ge(ctx.sem["gMain"], pos_xh[gi - 1])
                            ctx.inc(sc.activation(xlnb[:], xbuf[:, gi % 2],
                                                  AF.Identity,
                                                  bias=nm[:, t4:t4 + 1],
                                                  scale=rv[:, t4:t4 + 1]),
                                    "aMain")
                            if qc >= 2 and t4 == 0:
                                spl.wait(sc, (qc - 2) * 2)
                                spl.wait(sc, (qc - 2) * 2 + 1)
                            for piece in range(2):
                                for jg in range(ngf):
                                    kg = gi * 2 * ngf + piece * ngf + jg
                                    sc.wait_ge(ctx.sem["tTR"], tb_ + kg + 1)
                                    dstp = fmh if piece == 0 else fml
                                    ctx.inc(sc.activation(
                                        dstp[:, qc % 2, jg * 4:(jg + 1) * 4,
                                             t4 * P:(t4 + 1) * P],
                                        pst[:, kg % 2, 0:512],
                                        AF.Identity), "vEvac")

            @blk.gpsimd
            def _(g):
                for gi in range(NTILE):
                    g.wait_ge(ctx.sem["aMain"], pos_x[gi])
                    if gi >= 2:
                        g.wait_ge(ctx.sem["tTR"], tb_ + (gi - 1) * 2 * ngf)
                    ctx.inc(g.tensor_copy(xh[:, gi % 2], xlnb[:]), "gMain")

            @blk.tensor
            def _(t):
                for gi in range(NTILE):
                    qc, t4 = divmod(gi, NT4)
                    t.wait_ge(ctx.sem["gMain"], pos_xh[gi])
                    for piece, srcb in ((0, xh), (1, xl)):
                        if piece == 1:
                            t.wait_ge(ctx.sem["vMain"], pos_lo[gi])
                        for jg in range(ngf):
                            kg = gi * 2 * ngf + piece * ngf + jg
                            if kg >= 2:
                                t.wait_ge(ctx.sem["vEvac"], veb + kg - 1)
                            for jj in range(4):
                                j = jg * 4 + jj
                                ins = t.transpose(
                                    pst[:, kg % 2, jj * P:(jj + 1) * P],
                                    srcb[:, gi % 2, j * P:(j + 1) * P],
                                    ident[:])
                            ctx.inc(ins, "tTR")

            @blk.vector
            def _(v):
                for qc in range(nchunk):
                    for kind, t4 in order:
                        gi = qc * NT4 + t4
                        if kind == "a":
                            lds.wait(v, gi)
                            if qc >= 1:
                                v.wait_ge(ctx.sem["aMain"], pos_x[gi - NT4])
                            for gr in range(NGRP):
                                v.bn_stats(bns[:, t4, gr * 6:gr * 6 + 6],
                                           xbuf[:, gi % 2,
                                                gr * 512:(gr + 1) * 512])
                            v.drain()
                            ctx.inc(v.bn_aggr(mv[:, t4], bns[:, t4]), "vAux")
                            v.wait_ge(ctx.sem["aMain"], pos_sq[gi])
                            v.reciprocal(rv[:, t4:t4 + 1], sq[:, t4:t4 + 1])
                            v.drain()
                            ctx.inc(v.scalar_tensor_tensor(
                                nm[:, t4:t4 + 1], mv[:, t4, 0:1], -1.0,
                                rv[:, t4:t4 + 1], OP.mult, OP.mult), "vMain")
                        else:
                            v.wait_ge(ctx.sem["gMain"], pos_xh[gi])
                            if gi >= 2:
                                v.wait_ge(ctx.sem["tTR"],
                                          tb_ + (gi - 1) * 2 * ngf)
                            ctx.inc(v.tensor_tensor(xl[:, gi % 2], xlnb[:],
                                                    xh[:, gi % 2],
                                                    OP.subtract), "vMain")

        with nc.Block() as blk:
            @blk.sync
            def _(s):
                for d in (lds, spl):
                    d.wait_all(s)

    def emit_proj_all(nchunk, mode, fm_src, tagp):
        np_ = 8 if mode == "kv" else 4
        lds = DBuf(ctx, ["q0", "q1"])
        wk_s = DBuf(ctx, ["q2", "q3"])
        wv_s = DBuf(ctx, ["q4", "q5"])
        spk = DBuf(ctx, ["g0", "g1"])
        spv = DBuf(ctx, ["g2", "g3"])
        tb2 = ctx.cnt["tMM"]
        ab2 = ctx.cnt["aMain"]
        vb2 = ctx.cnt["vMain"]
        CU = NDG * np_
        with (
            nc.sbuf_tensor(f"a_fmh_{tagp}", [P, 2, FC, 4 * P], BF16) as fmh,
            nc.sbuf_tensor(f"a_fml_{tagp}", [P, 2, FC, 4 * P], BF16) as fml,
            nc.sbuf_tensor(f"a_wkb_{tagp}", [P, 2, FC, DG], BF16) as wkb,
            nc.sbuf_tensor(f"a_wvb_{tagp}", [P, 2, FC, DG], BF16) as wvb,
            nc.sbuf_tensor(f"a_keh_{tagp}", [P, 2, DG], BF16) as keh,
            nc.sbuf_tensor(f"a_kel_{tagp}", [P, 2, DG], BF16) as kel,
            nc.psum_tensor(f"a_psk_{tagp}", [P, 2, DG], F32) as psk,
            nc.Block() as blk,
        ):
            @blk.sync
            def _(s):
                for qc in range(nchunk):
                    if qc >= 2:
                        lds.wait(s, qc - 2)
                        s.wait_ge(ctx.sem["tMM"], tb2 + (qc - 1) * CU)
                    lds.issue(s, qc, [(fmh[:, qc % 2],
                                       fm_src[qc, 0].rearrange(
                                           "f p c -> p f c")),
                                      (fml[:, qc % 2],
                                       fm_src[qc, 1].rearrange(
                                           "f p c -> p f c"))])
                    for dg in range(NDG):
                        i = qc * NDG + dg
                        if i >= 2:
                            s.wait_ge(ctx.sem["tMM"], tb2 + np_ * (i - 1))
                        wk_s.issue(s, i,
                                   [(wkb[:, i % 2, f],
                                     s_wT["k" if mode == "kv" else "q"]
                                     [f, :, dg * DG:(dg + 1) * DG])
                                    for f in range(FC)])
                        if mode == "kv":
                            wv_s.issue(s, i,
                                       [(wvb[:, i % 2, f],
                                         s_wT["v"][f, :, dg * DG:(dg + 1) * DG])
                                        for f in range(FC)])

            @blk.gpsimd
            def _(g):
                for qc in range(nchunk):
                    for dg in range(NDG):
                        for u in range(np_):
                            kk = dg * np_ + u
                            K = qc * CU + kk
                            g.wait_ge(ctx.sem["vMain"], vb2 + K + 1)
                            if mode == "q":
                                h = dg * 4 + u
                                spk.issue(g, spidx(qc, kk, np_),
                                          [(s_qh[h, :, qc * 512:(qc + 1) * 512],
                                            keh[:, K % 2]),
                                           (s_ql[h, :, qc * 512:(qc + 1) * 512],
                                            kel[:, K % 2])])
                            elif u < 4:
                                h = dg * 4 + u
                                spk.issue(g, spidx(qc, kk, np_),
                                          [(s_kh[h, :, qc * 512:(qc + 1) * 512],
                                            keh[:, K % 2]),
                                           (s_kl[h, :, qc * 512:(qc + 1) * 512],
                                            kel[:, K % 2])])
                            else:
                                tt4 = u - 4
                                ttg = qc * 4 + tt4
                                pairs = []
                                for hh in range(4):
                                    h = dg * 4 + hh
                                    pairs.append(
                                        (s_v[h, ttg, :, 0:P],
                                         keh[:, K % 2, hh * P:(hh + 1) * P]))
                                    pairs.append(
                                        (s_v[h, ttg, :, P:2 * P],
                                         kel[:, K % 2, hh * P:(hh + 1) * P]))
                                spv.issue(g, spidx(qc, kk, np_), pairs)

            def spill_wait(eng, qc, kk):
                gk = qc * CU + kk
                if gk < 2:
                    return
                kkp, qcp = divmod(gk - 2, CU)[1], (gk - 2) // CU
                st = spk if (mode == "q" or kkp % np_ < 4) else spv
                st.wait(eng, spidx(qcp, kkp, np_))

            @blk.tensor
            def _(t):
                for qc in range(nchunk):
                    lds.wait(t, qc)
                    for dg in range(NDG):
                        i = qc * NDG + dg
                        wk_s.wait(t, i)
                        if mode == "kv":
                            wv_s.wait(t, i)
                        for u in range(np_):
                            kk = dg * np_ + u
                            K = qc * CU + kk
                            if K >= 2:
                                t.wait_ge(ctx.sem["vMain"], vb2 + K - 1)
                            last = None
                            if mode == "q" or u < 4:
                                dch = u
                                for f in range(FC):
                                    lt = wkb[:, i % 2, f,
                                             dch * P:(dch + 1) * P]
                                    t.matmul(psk[:, K % 2], lt,
                                             fmh[:, qc % 2, f],
                                             start=(f == 0), stop=False)
                                    last = t.matmul(psk[:, K % 2], lt,
                                                    fml[:, qc % 2, f],
                                                    start=False,
                                                    stop=(f == FC - 1))
                            else:
                                tt4 = u - 4
                                for f in range(FC):
                                    t.matmul(psk[:, K % 2],
                                             fmh[:, qc % 2, f,
                                                 tt4 * P:(tt4 + 1) * P],
                                             wvb[:, i % 2, f],
                                             start=(f == 0), stop=False)
                                    last = t.matmul(
                                        psk[:, K % 2],
                                        fml[:, qc % 2, f,
                                            tt4 * P:(tt4 + 1) * P],
                                        wvb[:, i % 2, f], start=False,
                                        stop=(f == FC - 1))
                            ctx.inc(last, "tMM")

            @blk.scalar
            def _(sc):
                for qc in range(nchunk):
                    for kk in range(CU):
                        K = qc * CU + kk
                        sc.wait_ge(ctx.sem["tMM"], tb2 + K + 1)
                        spill_wait(sc, qc, kk)
                        gm = gam[("q" if mode == "q" else
                                  ("k" if kk % np_ < 4 else "v"))]
                        ctx.inc(sc.activation(keh[:, K % 2], psk[:, K % 2],
                                              AF.Copy, scale=gm[:]), "aMain")

            @blk.vector
            def _(v):
                for qc in range(nchunk):
                    for kk in range(CU):
                        K = qc * CU + kk
                        v.wait_ge(ctx.sem["aMain"], ab2 + K + 1)
                        spill_wait(v, qc, kk)
                        gm = gam[("q" if mode == "q" else
                                  ("k" if kk % np_ < 4 else "v"))]
                        ctx.inc(v.scalar_tensor_tensor(
                            kel[:, K % 2], psk[:, K % 2], gm[:],
                            keh[:, K % 2], OP.mult, OP.subtract), "vMain")

        with nc.Block() as blk:
            @blk.sync
            def _(s):
                for d in (lds, wk_s, wv_s, spk, spv):
                    d.wait_all(s)

    if "a" in phases:
        emit_ln_spill(x_batch, NCH, s_fma, "a")
        emit_ln_spill(x_own, NCHO, s_fmq, "a2")
        emit_proj_all(NCH, "kv", s_fma, "pa")
        emit_proj_all(NCHO, "q", s_fmq, "pa2")

    def mmseq(Nkt):
        seq = []
        for kt in range(Nkt + 2):
            if kt < Nkt:
                seq.append(("S", kt))
            if kt >= 2:
                seq.append(("A", kt - 2))
        posS = {kt: seq.index(("S", kt)) + 1 for kt in range(Nkt)}
        posA = {kt: seq.index(("A", kt)) + 1 for kt in range(Nkt)}
        return seq, posS, posA

    def emit_attention(s_idx, avh, avl):
        Es = (2 * s_idx + 2) * QW
        Nkt = Es // P
        zone0 = Nkt - ZONE
        seq, posS, posA = mmseq(Nkt)
        L = len(seq)
        kb = DBuf(ctx, ["q0", "q1"])
        vb_ = DBuf(ctx, ["q2", "q3"])
        qb_ = DBuf(ctx, ["q4", "q5"])
        tS0 = ctx.cnt["tMM"]
        vC0 = ctx.cnt["vEvac"]
        eR0 = ctx.cnt["vAux"]
        aE0 = ctx.cnt["aMain"]
        hD0 = ctx.cnt["vMain"]
        tB0 = ctx.cnt["tTR"]
        zcum = {}
        lastz = {}
        zprev = {}
        zc = vC0
        for h in range(H):
            for kt in range(Nkt):
                kidx = h * Nkt + kt
                if kt >= zone0:
                    zc += 1
                    zcum[kidx] = zc
                    zprev[kidx] = lastz.get(kidx % 3)
                    lastz[kidx % 3] = kidx

        def t_S(h, kt):
            return tS0 + h * L + posS[kt]

        def t_A(h, kt):
            return tS0 + h * L + posA[kt]

        with (
            nc.sbuf_tensor(f"b_k_{s_idx}", [P, 2, 2, Es], BF16) as kbf,
            nc.sbuf_tensor(f"b_v_{s_idx}", [P, 2, (Es // P) * 2 * P], BF16) as vbf,
            nc.sbuf_tensor(f"b_q_{s_idx}", [P, 2, 2, QW], BF16) as qbf,
            nc.sbuf_tensor(f"b_ein_{s_idx}", [P, 3, QW], F32) as ein,
            nc.sbuf_tensor(f"b_ef_{s_idx}", [P, 3, QW], F32) as ef,
            nc.sbuf_tensor(f"b_eh_{s_idx}", [P, 3, QW], BF16) as ehb,
            nc.sbuf_tensor(f"b_el_{s_idx}", [P, 3, QW], BF16) as elb,
            nc.sbuf_tensor(f"b_rc_{s_idx}", [1, QW], F32) as rc,
            nc.sbuf_tensor(f"b_rcb_{s_idx}", [P, QW], F32) as rcb,
            nc.sbuf_tensor(f"b_avt_{s_idx}", [P, QW], F32) as avt,
            nc.psum_tensor(f"b_ps_{s_idx}", [P, 3, 512], F32) as psS,
            nc.psum_tensor(f"b_pav_{s_idx}", [P, 2, 512], F32) as psAV,
            nc.psum_tensor(f"b_pcs_{s_idx}", [P, 2, 512], F32) as psCS,
            nc.psum_tensor(f"b_pb_{s_idx}", [P, 512], F32) as psB,
            nc.Block() as blk,
        ):
            @blk.sync
            def _(s):
                for h in range(H):
                    if h >= 2:
                        s.wait_ge(ctx.sem["tMM"], tS0 + (h - 1) * L)
                    kb.issue(s, h, [(kbf[:, h % 2, 0, :], s_kh[h, :, 0:Es]),
                                    (kbf[:, h % 2, 1, :], s_kl[h, :, 0:Es])])
                    ntt = Es // P
                    half = max(1, ntt // 2)
                    pairs = []
                    for j0 in range(0, ntt, half):
                        j1 = min(ntt, j0 + half)
                        src_v = s_v[h, j0:j1].rearrange("t p c -> p t c")
                        dst_v = vbf[:, h % 2, j0 * 2 * P:j1 * 2 * P].rearrange(
                            "p (t c) -> p t c", c=2 * P)
                        pairs.append((dst_v, src_v))
                    vb_.issue(s, h, pairs)
                    qb_.issue(s, h, [(qbf[:, h % 2, 0, :],
                                      s_qh[h, :, s_idx * QW:(s_idx + 1) * QW]),
                                     (qbf[:, h % 2, 1, :],
                                      s_ql[h, :, s_idx * QW:(s_idx + 1) * QW])])

            @blk.tensor
            def _(t):
                for h in range(H):
                    kb.wait(t, h)
                    vb_.wait(t, h)
                    qb_.wait(t, h)
                    if h >= 2:
                        t.wait_ge(ctx.sem["vMain"], hD0 + 3 * (h - 2) + 2)
                    for kind, kt in seq:
                        kidx = h * Nkt + kt
                        if kind == "S":
                            kprev = kidx - 3
                            if kprev >= 0:
                                hp, ktp = divmod(kprev, Nkt)
                                if ktp >= zone0:
                                    t.wait_ge(ctx.sem["vEvac"], zcum[kprev])
                                else:
                                    t.wait_ge(ctx.sem["aMain"],
                                              aE0 + 2 * kprev + 2)
                            kh = kbf[:, h % 2, 0, kt * P:(kt + 1) * P]
                            kl = kbf[:, h % 2, 1, kt * P:(kt + 1) * P]
                            t.matmul(psS[:, kt % 3, 0:QW], kh, qbf[:, h % 2, 0, :],
                                     start=True, stop=False)
                            t.matmul(psS[:, kt % 3, 0:QW], kh, qbf[:, h % 2, 1, :],
                                     start=False, stop=False)
                            ctx.inc(t.matmul(psS[:, kt % 3, 0:QW], kl,
                                             qbf[:, h % 2, 0, :],
                                             start=False, stop=True), "tMM")
                        else:
                            k0 = kt
                            ki = h * Nkt + k0
                            t.wait_ge(ctx.sem["vAux"], eR0 + ki + 1)
                            vh = vbf[:, h % 2, k0 * 2 * P:k0 * 2 * P + P]
                            vl = vbf[:, h % 2, k0 * 2 * P + P:(k0 + 1) * 2 * P]
                            st = (k0 == 0)
                            sp = (k0 == Nkt - 1)
                            t.matmul(psAV[:, h % 2, 0:QW], vh, ehb[:, ki % 3],
                                     start=st, stop=False)
                            t.matmul(psAV[:, h % 2, 0:QW], vh, elb[:, ki % 3],
                                     start=False, stop=False)
                            t.matmul(psAV[:, h % 2, 0:QW], vl, ehb[:, ki % 3],
                                     start=False, stop=sp)
                            t.matmul(psCS[0:1, h % 2, 0:QW], ones_bf[:], ehb[:, ki % 3],
                                     start=st, stop=False)
                            ctx.inc(t.matmul(psCS[0:1, h % 2, 0:QW], ones_bf[:],
                                             elb[:, ki % 3], start=False, stop=sp),
                                    "tMM")
                    t.wait_ge(ctx.sem["vMain"], hD0 + 3 * h + 1)
                    ctx.inc(t.matmul(psB[:, 0:QW], onesf_r[:], rc[:], start=True,
                                     stop=True), "tTR")

            @blk.scalar
            def _(sc):
                for h in range(H):
                    for kt in range(Nkt):
                        kidx = h * Nkt + kt
                        if kidx >= 3:
                            sc.wait_ge(ctx.sem["vAux"], eR0 + kidx - 2)
                            hp, ktp = divmod(kidx - 3, Nkt)
                            sc.wait_ge(ctx.sem["tMM"], t_A(hp, ktp))
                        if kt >= zone0:
                            sc.wait_ge(ctx.sem["vEvac"], zcum[kidx])
                            src = ein[:, kidx % 3]
                        else:
                            sc.wait_ge(ctx.sem["tMM"], t_S(h, kt))
                            src = psS[:, kt % 3, 0:QW]
                        ctx.inc(sc.activation(ef[:, kidx % 3], src, AF.Exp,
                                              bias=CP["neg40"]), "aMain")
                        ctx.inc(sc.activation(ehb[:, kidx % 3], src, AF.Exp,
                                              bias=CP["neg40"]), "aMain")

            @blk.vector
            def _(v):
                for h in range(H):
                    for kt in range(Nkt):
                        kidx = h * Nkt + kt
                        if kt >= zone0:
                            v.wait_ge(ctx.sem["tMM"], t_S(h, kt))
                            if zprev.get(kidx) is not None:
                                v.wait_ge(ctx.sem["aMain"],
                                          aE0 + 2 * zprev[kidx] + 2)
                            z = kt - zone0
                            moff = (s_idx * ZONE + z) * QW
                            ctx.inc(v.tensor_tensor(ein[:, kidx % 3],
                                                    psS[:, kt % 3, 0:QW],
                                                    masks_sb[:, moff:moff + QW],
                                                    OP.add), "vEvac")
                        v.wait_ge(ctx.sem["aMain"], aE0 + 2 * kidx + 2)
                        if kidx >= 3:
                            hp, ktp = divmod(kidx - 3, Nkt)
                            v.wait_ge(ctx.sem["tMM"], t_A(hp, ktp))
                        ctx.inc(v.tensor_tensor(elb[:, kidx % 3], ef[:, kidx % 3],
                                                ehb[:, kidx % 3], OP.subtract),
                                "vAux")
                    v.wait_ge(ctx.sem["tMM"], tS0 + (h + 1) * L)
                    ctx.inc(v.reciprocal(rc[:], psCS[0:1, h % 2, 0:QW]), "vMain")
                    v.wait_ge(ctx.sem["tTR"], tB0 + h + 1)
                    v.tensor_copy(rcb[:], psB[:, 0:QW])
                    v.drain()
                    ctx.inc(v.tensor_tensor(avt[:], psAV[:, h % 2, 0:QW], rcb[:],
                                            OP.mult), "vMain")
                    v.drain()
                    v.tensor_copy(avh[:, h * QW:(h + 1) * QW], avt[:])
                    v.drain()
                    ctx.inc(v.tensor_tensor(avl[:, h * QW:(h + 1) * QW], avt[:],
                                            avh[:, h * QW:(h + 1) * QW],
                                            OP.subtract), "vMain")

        with nc.Block() as blk:
            @blk.sync
            def _(s):
                for d in (kb, vb_, qb_):
                    d.wait_all(s)

    def emit_outproj_c(s_idx, avh, avl, xp):
        wo_s = DBuf(ctx, ["q2", "q3"])
        xo_s = DBuf(ctx, ["q4", "q5"])
        spx = DBuf(ctx, ["g0", "g1"])
        spa = DBuf(ctx, ["g2", "g3", "g4", "g5"])
        NGRP = D // 512
        ngf = FC // 4
        tb2 = ctx.cnt["tMM"]
        vb2 = ctx.cnt["vMain"]
        with (
            nc.sbuf_tensor(f"c_wo_{s_idx}", [P, 2, FC, DG], BF16) as wob,
            nc.psum_tensor(f"c_pso_{s_idx}", [P, 2, DG], F32) as psO,
            nc.Block() as blk,
        ):
            @blk.scalar
            def _(sc):
                for qs in range(QT):
                    xo_s.issue(sc, qs,
                               [(xp[:, qs],
                                 x_own[(s_idx * QT + qs) * P:
                                       (s_idx * QT + qs + 1) * P, :])])
                for dg in range(NDG):
                    if dg >= 2:
                        sc.wait_ge(ctx.sem["tMM"], tb2 + QT * (dg - 1))
                    wo_s.issue(sc, dg,
                               [(wob[:, dg % 2, f],
                                 s_wT["o"][f, :, dg * DG:(dg + 1) * DG])
                                for f in range(FC)])

            @blk.tensor
            def _(t):
                for dg in range(NDG):
                    wo_s.wait(t, dg)
                    for qs in range(QT):
                        kk = dg * QT + qs
                        if kk >= 2:
                            t.wait_ge(ctx.sem["vMain"], vb2 + kk - 1)
                        last = None
                        for f in range(FC):
                            lt_h = avh[:, f * QW + qs * P:f * QW + (qs + 1) * P]
                            lt_l = avl[:, f * QW + qs * P:f * QW + (qs + 1) * P]
                            t.matmul(psO[:, kk % 2], lt_h, wob[:, dg % 2, f],
                                     start=(f == 0), stop=False)
                            last = t.matmul(psO[:, kk % 2], lt_l, wob[:, dg % 2, f],
                                            start=False, stop=(f == FC - 1))
                        ctx.inc(last, "tMM")

            @blk.vector
            def _(v):
                for dg in range(NDG):
                    for qs in range(QT):
                        kk = dg * QT + qs
                        v.wait_ge(ctx.sem["tMM"], tb2 + kk + 1)
                        if dg == 0:
                            xo_s.wait(v, qs)
                        ctx.inc(v.scalar_tensor_tensor(
                            xp[:, qs, dg * DG:(dg + 1) * DG], psO[:, kk % 2],
                            gam["o"][:], xp[:, qs, dg * DG:(dg + 1) * DG],
                            OP.mult, OP.add), "vMain")

        ab = ctx.cnt["aMain"]
        xb3 = ctx.cnt["vAux"]
        veb = ctx.cnt["vEvac"]
        tb3 = ctx.cnt["tTR"]
        vb3 = ctx.cnt["vMain"]
        stages = []
        for qs in range(QT):
            for st in ("bn", "nmrec", "red", "tsr", "q8", "dq", "axl"):
                stages.append((st, qs))
        pos_nm, pos_q8, pos_dq, pos_axl, pos_aggr = {}, {}, {}, {}, {}
        vm, va = vb3, xb3
        for st, qs in stages:
            if st == "bn":
                va += 1
                pos_aggr[qs] = va
            elif st in ("nmrec", "q8", "dq", "axl"):
                vm += 1
                {"nmrec": pos_nm, "q8": pos_q8, "dq": pos_dq,
                 "axl": pos_axl}[st][qs] = vm
        pos_sq = {qs: ab + 3 * qs + 1 for qs in range(QT)}
        pos_xln = {qs: ab + 3 * qs + 2 for qs in range(QT)}
        pos_axh = {qs: ab + 3 * qs + 3 for qs in range(QT)}
        pos_t = {(qs, kg): tb3 + qs * 2 * ngf + kg + 1
                 for qs in range(QT) for kg in range(2 * ngf)}
        pos_e = {(qs, kg): veb + qs * 2 * ngf + kg + 1
                 for qs in range(QT) for kg in range(2 * ngf)}
        with (
            nc.sbuf_tensor(f"c_bns_{s_idx}", [P, NGRP * 6], F32) as bns,
            nc.sbuf_tensor(f"c_mv_{s_idx}", [P, QT, 2], F32) as mv,
            nc.sbuf_tensor(f"c_rv_{s_idx}", [P, QT], F32) as rv,
            nc.sbuf_tensor(f"c_sq_{s_idx}", [P, QT], F32) as sq,
            nc.sbuf_tensor(f"c_nm_{s_idx}", [P, QT], F32) as nm,
            nc.sbuf_tensor(f"c_xln_{s_idx}", [P, 2, D], F32) as xlnb,
            nc.sbuf_tensor(f"c_ma_{s_idx}", [P, 2, NG], F32) as ma,
            nc.sbuf_tensor(f"c_r2_{s_idx}", [P, 2, NG], F32) as r2,
            nc.sbuf_tensor(f"c_q8_{s_idx}", [P, D], FP8) as q8,
            nc.sbuf_tensor(f"c_dq_{s_idx}", [P, D], F32) as dq,
            nc.sbuf_tensor(f"c_axh_{s_idx}", [P, D], BF16) as axh,
            nc.sbuf_tensor(f"c_axl_{s_idx}", [P, D], BF16) as axl,
            nc.sbuf_tensor(f"c_stg_{s_idx}", [P, 4, 512], BF16) as stg,
            nc.psum_tensor(f"c_pst_{s_idx}", [P, 2, 1024], BF16) as pst,
            nc.Block() as blk,
        ):
            @blk.gpsimd
            def _(g):
                for qs in range(QT):
                    tile = s_idx * QT + qs
                    spx.issue(g, tile, [(s_xp[tile], xp[:, qs])])
                for qs in range(QT):
                    tile = s_idx * QT + qs
                    for kg in range(2 * ngf):
                        gk = tile * 2 * ngf + kg
                        g.wait_ge(ctx.sem["vEvac"], pos_e[(qs, kg)])
                        jg, piece = kg % ngf, kg // ngf
                        dstg = s_ax[jg * 4:(jg + 1) * 4, tile, :,
                                    piece * P:(piece + 1) * P].rearrange(
                                        "j p c -> p j c")
                        spa.issue(g, gk, [(dstg, stg[:, gk % 4])])

            @blk.scalar
            def _(sc):
                def evacs(qs):
                    tile = s_idx * QT + qs
                    for kg in range(2 * ngf):
                        qk = qs * 2 * ngf + kg
                        gk = tile * 2 * ngf + kg
                        sc.wait_ge(ctx.sem["tTR"], pos_t[(qs, kg)])
                        if qk >= 4:
                            spa.wait(sc, gk - 4)
                        ctx.inc(sc.activation(stg[:, gk % 4],
                                              pst[:, qk % 2, 0:512],
                                              AF.Identity), "vEvac")
                for qs in range(QT):
                    sc.wait_ge(ctx.sem["vAux"], pos_aggr[qs])
                    ctx.inc(sc.activation(sq[:, qs:qs + 1], mv[:, qs, 1:2],
                                          AF.Sqrt, bias=CP["eps"]), "aMain")
                    sc.wait_ge(ctx.sem["vMain"], pos_nm[qs])
                    if qs >= 2:
                        sc.wait_ge(ctx.sem["vMain"], pos_q8[qs - 2])
                    ctx.inc(sc.activation(xlnb[:, qs % 2], xp[:, qs],
                                          AF.Identity, bias=nm[:, qs:qs + 1],
                                          scale=rv[:, qs:qs + 1]), "aMain")
                    if qs >= 1:
                        evacs(qs - 1)
                    sc.wait_ge(ctx.sem["vMain"], pos_dq[qs])
                    if qs >= 1:
                        sc.wait_ge(ctx.sem["tTR"], pos_t[(qs - 1, 2 * ngf - 1)])
                    ctx.inc(sc.activation(axh[:], dq[:], AF.Identity), "aMain")
                evacs(QT - 1)

            @blk.tensor
            def _(t):
                for qs in range(QT):
                    t.wait_ge(ctx.sem["aMain"], pos_axh[qs])
                    for piece, src in ((0, axh), (1, axl)):
                        if piece == 1:
                            t.wait_ge(ctx.sem["vMain"], pos_axl[qs])
                        for jg in range(ngf):
                            kg = piece * ngf + jg
                            qk = qs * 2 * ngf + kg
                            if qk >= 2:
                                t.wait_ge(ctx.sem["vEvac"], veb + qk - 1)
                            for jj in range(4):
                                j = jg * 4 + jj
                                ins = t.transpose(
                                    pst[:, qk % 2, jj * P:(jj + 1) * P],
                                    src[:, j * P:(j + 1) * P], ident[:])
                            ctx.inc(ins, "tTR")

            @blk.vector
            def _(v):
                for st, qs in stages:
                    if st == "bn":
                        for gi in range(NGRP):
                            v.bn_stats(bns[:, gi * 6:gi * 6 + 6],
                                       xp[:, qs, gi * 512:(gi + 1) * 512])
                        v.drain()
                        ctx.inc(v.bn_aggr(mv[:, qs], bns[:]), "vAux")
                    elif st == "nmrec":
                        v.wait_ge(ctx.sem["aMain"], pos_sq[qs])
                        v.reciprocal(rv[:, qs:qs + 1], sq[:, qs:qs + 1])
                        v.drain()
                        ctx.inc(v.scalar_tensor_tensor(
                            nm[:, qs:qs + 1], mv[:, qs, 0:1], -1.0,
                            rv[:, qs:qs + 1], OP.mult, OP.mult), "vMain")
                    elif st == "red":
                        v.wait_ge(ctx.sem["aMain"], pos_xln[qs])
                        x3 = xlnb[:, qs % 2].rearrange("p (g b) -> p g b", b=32)
                        v.tensor_reduce(ma[:, qs % 2], x3, AX.X, OP.max,
                                        apply_absolute_value=True)
                        v.drain()
                    elif st == "tsr":
                        v.tensor_scalar(ma[:, qs % 2], ma[:, qs % 2],
                                        2.0 / 448.0, 2e-30, OP.mult, OP.max)
                        v.drain()
                        v.reciprocal(r2[:, qs % 2], ma[:, qs % 2])
                        v.drain()
                    elif st == "q8":
                        x3 = xlnb[:, qs % 2].rearrange("p (g b) -> p g b", b=32)
                        ctx.inc(v.tensor_tensor(
                            q8.rearrange("p (g b) -> p g b", b=32), x3,
                            bcast_ap(r2[:, qs % 2], 32), OP.mult), "vMain")
                        v.drain()
                    elif st == "dq":
                        if qs >= 1:
                            v.wait_ge(ctx.sem["aMain"], pos_axh[qs - 1])
                        ctx.inc(v.tensor_tensor(
                            dq.rearrange("p (g b) -> p g b", b=32),
                            q8.rearrange("p (g b) -> p g b", b=32),
                            bcast_ap(ma[:, qs % 2], 32), OP.mult), "vMain")
                    elif st == "axl":
                        v.wait_ge(ctx.sem["aMain"], pos_axh[qs])
                        if qs >= 1:
                            v.wait_ge(ctx.sem["tTR"],
                                      pos_t[(qs - 1, 2 * ngf - 1)])
                        ctx.inc(v.tensor_tensor(axl[:], dq[:], axh[:],
                                                OP.subtract), "vMain")

        with nc.Block() as blk:
            @blk.sync
            def _(s):
                for d in (wo_s, xo_s, spx, spa):
                    d.wait_all(s)

    for s_idx in (range(NSLOT) if "b" in phases else ()):
        with (
            nc.sbuf_tensor(f"b_avh_{s_idx}", [P, H * QW], BF16) as avh,
            nc.sbuf_tensor(f"b_avl_{s_idx}", [P, H * QW], BF16) as avl,
        ):
            emit_attention(s_idx, avh, avl)
            with nc.sbuf_tensor(f"b_xp_{s_idx}", [P, QT, D], F32) as xp:
                emit_outproj_c(s_idx, avh, avl, xp)

    def emit_d1():
        wg_s = DBuf(ctx, ["q0", "q1"])
        wu_s = DBuf(ctx, ["q2", "q3"])
        ax_s = DBuf(ctx, ["q4", "q5"])
        sph = DBuf(ctx, ["g0", "g1", "g2", "g3"])
        NITER = NI * NTO
        with (
            nc.sbuf_tensor("d1_wg", [P, 2, FC, 1024], BF16) as wgb,
            nc.sbuf_tensor("d1_wu", [P, 2, FC, 1024], BF16) as wub,
            nc.sbuf_tensor("d1_ax", [P, 2, FC, 2 * P], BF16) as axb,
            nc.sbuf_tensor("d1_gs", [P, 512], F32) as gs,
            nc.sbuf_tensor("d1_h", [P, 512], F32) as hb,
            nc.sbuf_tensor("d1_ma", [P, NGI], F32) as ma,
            nc.sbuf_tensor("d1_r2", [P, NGI], F32) as r2,
            nc.sbuf_tensor("d1_q8", [P, 512], FP8) as q8,
            nc.sbuf_tensor("d1_dq", [P, 512], F32) as dq,
            nc.sbuf_tensor("d1_hh", [P, 512], BF16) as hh,
            nc.sbuf_tensor("d1_hl", [P, 512], BF16) as hl,
            nc.sbuf_tensor("d1_stg", [P, 2, 512], BF16) as stg,
            nc.psum_tensor("d1_psg", [P, 2, 512], F32) as psG,
            nc.psum_tensor("d1_psu", [P, 2, 512], F32) as psU,
            nc.psum_tensor("d1_pst", [P, 2, 1024], BF16) as pst,
            nc.Block() as blk,
        ):
            tb_ = ctx.cnt["tMM"]
            tr_ = ctx.cnt["tTR"]
            ab = ctx.cnt["aMain"]
            vb = ctx.cnt["vMain"]
            veb = ctx.cnt["vEvac"]

            @blk.sync
            def _(s):
                for ic in range(NI):
                    if ic >= 2:
                        s.wait_ge(ctx.sem["tMM"], tb_ + 2 * NTO * (ic - 1))
                    wg_s.issue(s, ic, [(wgb[:, ic % 2, f], s_wg[f, ic])
                                       for f in range(FC)])
                    wu_s.issue(s, ic, [(wub[:, ic % 2, f], s_wu[f, ic])
                                       for f in range(FC)])
                    for tt in range(NTO):
                        it = ic * NTO + tt
                        if it >= 2:
                            s.wait_ge(ctx.sem["tMM"], tb_ + 2 * (it - 1))
                        half = FC // 2
                        pairs = []
                        for j0 in range(0, FC, half):
                            src = s_ax[j0:j0 + half, tt].rearrange("f p c -> p f c")
                            pairs.append((axb[:, it % 2, j0:j0 + half], src))
                        ax_s.issue(s, it, pairs)

            @blk.gpsimd
            def _(g):
                for it in range(NITER):
                    ic, tt = divmod(it, NTO)
                    for piece in range(2):
                        kg = it * 2 + piece
                        g.wait_ge(ctx.sem["vEvac"], veb + kg + 1)
                        dstg = s_h[ic * 4:(ic + 1) * 4, tt, :,
                                   piece * P:(piece + 1) * P].rearrange(
                                       "j p c -> p j c")
                        sph.issue(g, kg, [(dstg, stg[:, piece])])

            @blk.tensor
            def _(t):
                def tr_group(it):
                    t.wait_ge(ctx.sem["vMain"], vb + 3 * it + 3)
                    for piece, src in ((0, hh), (1, hl)):
                        kg = it * 2 + piece
                        if kg >= 2:
                            t.wait_ge(ctx.sem["vEvac"], veb + kg - 1)
                        for jj in range(4):
                            ins = t.transpose(pst[:, kg % 2, jj * P:(jj + 1) * P],
                                              src[:, jj * P:(jj + 1) * P],
                                              ident[:])
                        ctx.inc(ins, "tTR")

                for it in range(NITER):
                    ic, tt = divmod(it, NTO)
                    if tt == 0:
                        wg_s.wait(t, ic)
                        wu_s.wait(t, ic)
                    ax_s.wait(t, it)
                    if it >= 2:
                        t.wait_ge(ctx.sem["aMain"], ab + it - 1)
                        t.wait_ge(ctx.sem["vMain"], vb + 3 * it - 5)
                    for f in range(FC):
                        ah = axb[:, it % 2, f, 0:P]
                        al = axb[:, it % 2, f, P:2 * P]
                        t.matmul(psG[:, it % 2], ah, wgb[:, ic % 2, f, 0:512],
                                 start=(f == 0), stop=False)
                        t.matmul(psG[:, it % 2], ah, wgb[:, ic % 2, f, 512:1024],
                                 start=False, stop=False)
                        t.matmul(psU[:, it % 2], ah, wub[:, ic % 2, f, 0:512],
                                 start=(f == 0), stop=False)
                        t.matmul(psU[:, it % 2], ah, wub[:, ic % 2, f, 512:1024],
                                 start=False, stop=False)
                        lg = t.matmul(psG[:, it % 2], al,
                                      wgb[:, ic % 2, f, 0:512], start=False,
                                      stop=(f == FC - 1))
                        lu = t.matmul(psU[:, it % 2], al,
                                      wub[:, ic % 2, f, 0:512], start=False,
                                      stop=(f == FC - 1))
                    ctx.inc(lg, "tMM")
                    ctx.inc(lu, "tMM")
                    if it >= 1:
                        tr_group(it - 1)
                if NITER >= 1:
                    tr_group(NITER - 1)

            @blk.scalar
            def _(sc):
                for it in range(NITER):
                    sc.wait_ge(ctx.sem["tMM"], tb_ + 2 * it + 1)
                    if it >= 1:
                        sc.wait_ge(ctx.sem["vMain"], vb + 3 * (it - 1) + 1)
                    ctx.inc(sc.activation(gs[:], psG[:, it % 2], AF.Silu), "aMain")

            @blk.vector
            def _(v):
                for it in range(NITER):
                    v.wait_ge(ctx.sem["aMain"], ab + it + 1)
                    v.wait_ge(ctx.sem["tMM"], tb_ + 2 * it + 2)
                    ctx.inc(v.tensor_tensor(hb[:], gs[:], psU[:, it % 2], OP.mult),
                            "vMain")
                    v.drain()
                    h3 = hb.rearrange("p (g b) -> p g b", b=32)
                    v.tensor_reduce(ma[:], h3, AX.X, OP.max,
                                    apply_absolute_value=True)
                    v.drain()
                    v.tensor_scalar(ma[:], ma[:], 2.0 / 448.0, 2e-30, OP.mult,
                                    OP.max)
                    v.drain()
                    v.reciprocal(r2[:], ma[:])
                    v.drain()
                    v.tensor_tensor(q8.rearrange("p (g b) -> p g b", b=32), h3,
                                    bcast_ap(r2, 32), OP.mult)
                    v.drain()
                    v.tensor_tensor(dq.rearrange("p (g b) -> p g b", b=32),
                                    q8.rearrange("p (g b) -> p g b", b=32),
                                    bcast_ap(ma, 32), OP.mult)
                    v.drain()
                    if it >= 1:
                        v.wait_ge(ctx.sem["tTR"], tr_ + 2 * it)
                    ctx.inc(v.tensor_copy(hh[:], dq[:]), "vMain")
                    v.drain()
                    ctx.inc(v.tensor_tensor(hl[:], dq[:], hh[:], OP.subtract),
                            "vMain")
                    for piece in range(2):
                        kg = it * 2 + piece
                        v.wait_ge(ctx.sem["tTR"], tr_ + kg + 1)
                        if kg >= 2:
                            sph.wait(v, kg - 2)
                        ctx.inc(v.tensor_copy(stg[:, piece],
                                              pst[:, kg % 2, 0:512]), "vEvac")

        with nc.Block() as blk:
            @blk.sync
            def _(s):
                for d in (wg_s, wu_s, ax_s, sph):
                    d.wait_all(s)

    masks_cm.__exit__(None, None, None)
    if "d1" in phases:
        emit_d1()

    def emit_d2():
        wd_s = DBuf(ctx, ["q0", "q1"])
        h_s = DBuf(ctx, ["q2", "q3"])
        xp_s = DBuf(ctx, ["q4", "q5"])
        out_s = DBuf(ctx, ["g0", "g1"])
        with (
            nc.sbuf_tensor("d2_wd", [P, NIS, 2 * OCW], BF16) as wdb,
            nc.sbuf_tensor("d2_h", [P, 2, NIS, 2 * P], BF16) as hbf,
            nc.sbuf_tensor("d2_xp", [P, 2, OCW], F32) as xpb,
            nc.sbuf_tensor("d2_yo", [P, 2, OCW], F32) as yob,
            nc.psum_tensor("d2_ps", [P, 2, 512], F32) as psD,
            nc.Block() as blk,
        ):
            tb_ = ctx.cnt["tMM"]
            vb = ctx.cnt["vMain"]

            @blk.sync
            def _(s):
                for oc in range(NOCH):
                    if oc >= 1:
                        s.wait_ge(ctx.sem["tMM"], tb_ + oc * NTO)
                    wd_s.issue(s, oc, [(wdb[:, isb], s_wd[isb, oc])
                                       for isb in range(NIS)])
                    for tt in range(NTO):
                        it = oc * NTO + tt
                        if it >= 2:
                            s.wait_ge(ctx.sem["tMM"], tb_ + it - 1)
                        half = NIS // 4
                        pairs = []
                        for j0 in range(0, NIS, half):
                            src = s_h[j0:j0 + half, tt].rearrange("f p c -> p f c")
                            pairs.append((hbf[:, it % 2, j0:j0 + half], src))
                        h_s.issue(s, it, pairs)
                        if it >= 2:
                            s.wait_ge(ctx.sem["vMain"], vb + it - 1)
                        xp_s.issue(s, it, [(xpb[:, it % 2],
                                            s_xp[tt, :, oc * OCW:(oc + 1) * OCW])])

            @blk.gpsimd
            def _(g):
                for oc in range(NOCH):
                    for tt in range(NTO):
                        it = oc * NTO + tt
                        g.wait_ge(ctx.sem["vMain"], vb + it + 1)
                        out_s.issue(g, it, [(y_out[tt * P:(tt + 1) * P,
                                                   oc * OCW:(oc + 1) * OCW],
                                             yob[:, it % 2])])

            @blk.tensor
            def _(t):
                for oc in range(NOCH):
                    for tt in range(NTO):
                        it = oc * NTO + tt
                        if tt == 0:
                            wd_s.wait(t, oc)
                        h_s.wait(t, it)
                        if it >= 2:
                            t.wait_ge(ctx.sem["vMain"], vb + it - 1)
                        last = None
                        for isb in range(NIS):
                            hh_ = hbf[:, it % 2, isb, 0:P]
                            hl_ = hbf[:, it % 2, isb, P:2 * P]
                            wh = wdb[:, isb, 0:OCW]
                            wl = wdb[:, isb, OCW:2 * OCW]
                            t.matmul(psD[:, it % 2, 0:OCW], hh_, wh, start=(isb == 0),
                                     stop=False)
                            t.matmul(psD[:, it % 2, 0:OCW], hh_, wl, start=False,
                                     stop=False)
                            last = t.matmul(psD[:, it % 2, 0:OCW], hl_, wh, start=False,
                                            stop=(isb == NIS - 1))
                        ctx.inc(last, "tMM")

            @blk.vector
            def _(v):
                for oc in range(NOCH):
                    for tt in range(NTO):
                        it = oc * NTO + tt
                        v.wait_ge(ctx.sem["tMM"], tb_ + it + 1)
                        xp_s.wait(v, it)
                        if it >= 2:
                            out_s.wait(v, it - 2)
                        ctx.inc(v.tensor_tensor(yob[:, it % 2], psD[:, it % 2, 0:OCW],
                                                xpb[:, it % 2], OP.add), "vMain")

        with nc.Block() as blk:
            @blk.sync
            def _(s):
                for d in (wd_s, h_s, xp_s, out_s):
                    d.wait_all(s)

    if "d2" in phases:
        emit_d2()

    with nc.Block() as blk:
        @blk.sync
        def _(s):
            for nm_ in ctx.sem:
                if nm_.startswith(("q", "g")) or nm_ == "dS":
                    ctx.wait(s, nm_)

    return nc, ctx



def _shard_inputs(inputs, cfg):
    D, H, T, TO, QW = (cfg[k] for k in ("D", "H", "T", "TO", "QW"))
    NSLOT = TO // QW
    ZONE = 2 * QW // P
    x = np.ascontiguousarray(np.asarray(inputs["L0_attn_x"], dtype=np.float32))
    B = x.shape[0]
    shared = dict(
        wq=np.ascontiguousarray(np.asarray(inputs["L0_attn_wq"], np.float32)),
        wk=np.ascontiguousarray(np.asarray(inputs["L0_attn_wk"], np.float32)),
        wv=np.ascontiguousarray(np.asarray(inputs["L0_attn_wv"], np.float32)),
        wo=np.ascontiguousarray(np.asarray(inputs["L0_attn_wo"], np.float32)),
        gate_w=np.ascontiguousarray(np.asarray(inputs["L0_ffn_gate_w"], np.float32)),
        up_w=np.ascontiguousarray(np.asarray(inputs["L0_ffn_up_w"], np.float32)),
        down_w=np.ascontiguousarray(np.asarray(inputs["L0_ffn_down_w"], np.float32)),
    )
    in_maps = []
    for c in range(2 * B):
        b, c2 = divmod(c, 2)
        xb = x[b]
        x_own = np.concatenate([xb[(2 * s + c2) * QW:(2 * s + c2 + 1) * QW]
                                for s in range(NSLOT)], 0)
        m = np.zeros((NSLOT, ZONE, P, QW), np.float32)
        for s in range(NSLOT):
            Es = (2 * s + 2) * QW
            qs = (2 * s + c2) * QW
            for z in range(ZONE):
                kt = Es // P - ZONE + z
                kglob = kt * P + np.arange(P)[:, None]
                qglob = qs + np.arange(QW)[None, :]
                m[s, z] = np.where(kglob <= qglob, 0.0, -240.0)
        in_maps.append(dict(shared,
                            x_batch=np.ascontiguousarray(xb),
                            x_own=np.ascontiguousarray(x_own),
                            cmask=m.astype(ml_dtypes.float8_e4m3)))
    return in_maps


def _assemble_output(results, cfg, B):
    D, TO, QW, T = cfg["D"], cfg["TO"], cfg["QW"], cfg["T"]
    NSLOT = TO // QW
    out = np.empty((B, T, D), np.float32)
    for c, r in enumerate(results):
        b, c2 = divmod(c, 2)
        yo = np.asarray(r["y_out"]).reshape(TO, D)
        for s in range(NSLOT):
            out[b, (2 * s + c2) * QW:(2 * s + c2 + 1) * QW] = \
                yo[s * QW:(s + 1) * QW]
    return out


def kernel(**inputs):
    cfg = full_cfg()
    x = np.asarray(inputs["L0_attn_x"])
    B = x.shape[0]
    nc, _ = build_program(cfg)
    in_maps = _shard_inputs(inputs, cfg)
    res = run_bass_kernel_spmd(nc, in_maps, core_ids=list(range(len(in_maps))))
    return _assemble_output(res.results, cfg, B)


# version 16
# speedup vs baseline: 1.0263x; 1.0263x over previous
import math
import numpy as np
import ml_dtypes
import concourse.bass as bass
import concourse.mybir as mybir
from concourse.bass_utils import run_bass_kernel_spmd
from concourse.alu_op_type import AluOpType
from bass_rust import ReduceOp
from concourse import library_config

F32 = mybir.dt.float32
BF16 = mybir.dt.bfloat16
FP8 = mybir.dt.float8e4
AF = mybir.ActivationFunctionType
AX = mybir.AxisListType
OP = AluOpType
P = 128


def full_cfg():
    return dict(D=2048, H=16, T=4096, TO=2048, QW=512, DFF=8192)


class Ctx:
    def __init__(self, nc):
        self.nc = nc
        self.sem = {}
        self.cnt = {}

    def add(self, name):
        self.sem[name] = self.nc.alloc_semaphore(name=name)
        self.cnt[name] = 0

    def inc(self, instr, name, n=1):
        instr.then_inc(self.sem[name], n)
        self.cnt[name] += n
        return self.cnt[name]

    def dma(self, eng, out, in_, name):
        eng.dma_start(out=out, in_=in_).then_inc(self.sem[name], 16)
        self.cnt[name] += 16
        return self.cnt[name]

    def wait(self, eng, name, target=None):
        eng.wait_ge(self.sem[name], self.cnt[name] if target is None else target)


class DBuf:

    def __init__(self, ctx, semnames):
        self.ctx = ctx
        self.names = list(semnames)
        self.n = len(self.names)
        self.tgt = {}

    def issue(self, eng, i, pairs):
        nm = self.names[i % self.n]
        for out, in_ in pairs:
            self.tgt[i] = self.ctx.dma(eng, out, in_, nm)
        return self.tgt[i]

    def wait(self, eng, i):
        eng.wait_ge(self.ctx.sem[self.names[i % self.n]], self.tgt[i])

    def wait_all(self, eng):
        for nm in self.names:
            self.ctx.wait(eng, nm)


def bcast_ap(t, inner):
    a = t if isinstance(t, bass.AP) else t.ap()
    return bass.AP(a.tensor, a.offset, [[a.ap[0][0], a.ap[0][1]]] +
                   [list(x) for x in a.ap[1:]] + [[0, inner]])


def build_program(cfg, phases=("w1", "w2", "a", "b", "d1", "d2")):
    D, H, T, TO, QW, DFF = (cfg[k] for k in ("D", "H", "T", "TO", "QW", "DFF"))
    FC = D // P
    NTT = T // P
    NTO = TO // P
    NSLOT = TO // QW
    ZONE = 2 * QW // P
    NI = DFF // 512
    NIS = DFF // P
    OCW = min(D, 512)
    NOCH = D // OCW
    QT = QW // P
    NDG = max(D // 512, 1)
    DG = min(D, 512)
    NCH = T // (4 * P)
    NCHO = TO // (4 * P)
    NG = D // 32
    NGI = 512 // 32
    assert D % P == 0 and H * P == D and QW % P == 0 and QW <= 512

    nc = bass.Bass("TRN2", target_bir_lowering=False)

    x_batch = nc.dram_tensor("x_batch", [T, D], F32, kind="ExternalInput").ap()
    x_own = nc.dram_tensor("x_own", [TO, D], F32, kind="ExternalInput").ap()
    w_in = {n: nc.dram_tensor("w" + n, [D, D], F32, kind="ExternalInput").ap()
            for n in "qkvo"}
    gate_w = nc.dram_tensor("gate_w", [DFF, D], F32, kind="ExternalInput").ap()
    up_w = nc.dram_tensor("up_w", [DFF, D], F32, kind="ExternalInput").ap()
    down_w = nc.dram_tensor("down_w", [D, DFF], F32, kind="ExternalInput").ap()
    cmask = nc.dram_tensor("cmask", [NSLOT, ZONE, P, QW], FP8,
                           kind="ExternalInput").ap()
    y_out = nc.dram_tensor("y_out", [TO, D], F32, kind="ExternalOutput").ap()

    s_wT = {n: nc.dram_tensor(f"s_w{n}T", [FC, P, D], BF16, kind="Internal").ap()
            for n in "qkvo"}
    s_kh = nc.dram_tensor("s_kh", [H, P, T], BF16, kind="Internal").ap()
    s_kl = nc.dram_tensor("s_kl", [H, P, T], BF16, kind="Internal").ap()
    s_v = nc.dram_tensor("s_v", [H, NTT, P, 2 * P], BF16, kind="Internal").ap()
    s_qh = nc.dram_tensor("s_qh", [H, P, TO], BF16, kind="Internal").ap()
    s_ql = nc.dram_tensor("s_ql", [H, P, TO], BF16, kind="Internal").ap()
    s_xp = nc.dram_tensor("s_xp", [NTO, P, D], F32, kind="Internal").ap()
    s_ax = nc.dram_tensor("s_ax", [FC, NTO, P, 2 * P], BF16, kind="Internal").ap()
    s_wg = nc.dram_tensor("s_wg", [FC, NI, P, 1024], BF16, kind="Internal").ap()
    s_wu = nc.dram_tensor("s_wu", [FC, NI, P, 1024], BF16, kind="Internal").ap()
    s_wd = nc.dram_tensor("s_wd", [NIS, NOCH, P, 2 * OCW], BF16, kind="Internal").ap()
    s_h = nc.dram_tensor("s_h", [NIS, NTO, P, 2 * P], BF16, kind="Internal").ap()
    s_fma = nc.dram_tensor("s_fma", [T // 512, 2, FC, P, 512], BF16,
                           kind="Internal").ap()
    s_fmq = nc.dram_tensor("s_fmq", [TO // 512, 2, FC, P, 512], BF16,
                           kind="Internal").ap()

    ctx = Ctx(nc)
    for s in ([f"q{i}" for i in range(6)] + [f"g{i}" for i in range(6)] +
              ["dS", "tMM", "tTR", "vMain", "vEvac", "vAux", "aMain", "gMain"]):
        ctx.add(s)

    ident = nc.sbuf_tensor("ident", [P, P], BF16).__enter__()
    ones_bf = nc.sbuf_tensor("ones_bf", [P, 1], BF16).__enter__()
    gam = {n: nc.sbuf_tensor(f"gam_{n}", [P, 1], F32).__enter__() for n in "qkvo"}
    invg = {n: nc.sbuf_tensor(f"invg_{n}", [P, 1], F32).__enter__() for n in "qkvo"}
    _CONSTS = {"neg_half": -0.5, "neg40": -40.0, "eps": 1e-5}
    cpool = nc.sbuf_tensor("cpool", [P, len(_CONSTS)], F32).__enter__()
    CP = {k: cpool[:, i:i + 1] for i, k in enumerate(_CONSTS)}
    onesf_c = nc.sbuf_tensor("onesf_c", [P, 1], F32).__enter__()
    onesf_r = nc.sbuf_tensor("onesf_r", [1, P], F32).__enter__()
    grow = nc.sbuf_tensor("grow", [1, 2], F32).__enter__()
    masks_cm = nc.sbuf_tensor("masks_sb", [P, NSLOT * ZONE * QW], FP8)
    masks_sb = masks_cm.__enter__()

    with nc.Block() as blk:
        @blk.vector
        def _(v):
            v.memset(ident[:], 0.0)
            v.memset(ones_bf[:], 1.0)
            v.memset(onesf_c[:], 1.0)
            v.memset(onesf_r[:], 1.0)
            for i, val in enumerate(_CONSTS.values()):
                v.memset(cpool[:, i:i + 1], val)

        @blk.sync
        def _(s):
            for sl in range(NSLOT):
                for z in range(ZONE):
                    ctx.dma(s, masks_sb[:, (sl * ZONE + z) * QW:
                                        (sl * ZONE + z + 1) * QW], cmask[sl, z], "dS")
            ctx.wait(s, "dS")

    with nc.Block() as blk:
        @blk.gpsimd
        def _(g):
            g.affine_select(out=ident[:], in_=ident[:], compare_op=OP.not_equal,
                            fill=1.0, base=0, pattern=[[-1, P]], channel_multiplier=1)

    def emit_bitlinear_prep(w_ap, wT_ap, gam_ap, invg_ap, tag, extra_scale=None):
        ntile = D // P
        nsub = D // P
        with (
            nc.sbuf_tensor(f"w1_wbuf_{tag}", [P, 2, D], F32) as wbuf,
            nc.sbuf_tensor(f"w1_abuf_{tag}", [P, D], F32) as abuf,
            nc.sbuf_tensor(f"w1_bbuf_{tag}", [P, D], F32) as bbuf,
            nc.sbuf_tensor(f"w1_part_{tag}", [P, ntile], F32) as part,
            nc.sbuf_tensor(f"w1_col_{tag}", [P, 1], F32) as col,
        ):
            lds = DBuf(ctx, ["q0", "q1"])
            vb = ctx.cnt["vMain"]
            with nc.Block() as blk:
                @blk.sync
                def _(s):
                    for i in range(ntile):
                        if i >= 2:
                            s.wait_ge(ctx.sem["vMain"], vb + i - 1)
                        lds.issue(s, i, [(wbuf[:, i % 2], w_ap[i * P:(i + 1) * P, :])])

                @blk.vector
                def _(v):
                    for i in range(ntile):
                        lds.wait(v, i)
                        ctx.inc(v.tensor_reduce(part[:, i:i + 1], wbuf[:, i % 2],
                                                AX.X, OP.add,
                                                apply_absolute_value=True), "vMain")
            with nc.Block() as blk:
                @blk.vector
                def _(v):
                    v.tensor_reduce(col[:], part[:], AX.X, OP.add)
            with nc.psum_tensor(f"w1_psg_{tag}", [P, 512], F32) as psg:
                with nc.Block() as blk:
                    @blk.tensor
                    def _(t):
                        t.matmul(psg[0:1, 0:1], col[:], onesf_c[:], start=True,
                                 stop=True)
                with nc.Block() as blk:
                    @blk.vector
                    def _(v):
                        v.tensor_scalar(grow[0:1, 0:1], psg[0:1, 0:1],
                                        1.0 / (D * D), 1e-5, OP.mult, OP.max)
                        v.drain()
                        v.reciprocal(grow[0:1, 1:2], grow[0:1, 0:1])
                        if extra_scale is not None:
                            v.drain()
                            v.tensor_scalar(grow[0:1, 0:1], grow[0:1, 0:1],
                                            extra_scale, None, OP.mult)
                with nc.Block() as blk:
                    @blk.tensor
                    def _(t):
                        t.matmul(psg[:, 0:2], onesf_r[:], grow[:], start=True,
                                 stop=True)
                with nc.Block() as blk:
                    @blk.vector
                    def _(v):
                        v.tensor_copy(gam_ap[:], psg[:, 0:1])
                        v.tensor_copy(invg_ap[:], psg[:, 1:2])

            lds2 = DBuf(ctx, ["q0", "q1"])
            spl = DBuf(ctx, ["g0", "g1", "g2", "g3"])
            ab = ctx.cnt["aMain"]
            vqb = ctx.cnt["vMain"]
            veb = ctx.cnt["vEvac"]
            tb_ = ctx.cnt["tTR"]
            with (
                nc.sbuf_tensor(f"w1_sgn_{tag}", [P, 2, D], BF16) as sgn,
                nc.sbuf_tensor(f"w1_rl_{tag}", [P, 2, D], BF16) as rl,
                nc.sbuf_tensor(f"w1_q_{tag}", [P, 2, D], BF16) as qb,
                nc.sbuf_tensor(f"w1_tb_{tag}", [P, 4, 512], BF16) as tbuf,
                nc.psum_tensor(f"w1_ps_{tag}", [P, 2, 1024], BF16) as ps,
                nc.Block() as blk,
            ):
                @blk.sync
                def _(s):
                    for i in range(ntile):
                        if i >= 2:
                            s.wait_ge(ctx.sem["aMain"], ab + i - 1)
                        lds2.issue(s, i, [(wbuf[:, i % 2], w_ap[i * P:(i + 1) * P, :])])

                @blk.gpsimd
                def _(g):
                    for i in range(ntile):
                        for jg in range(nsub // 4):
                            kg = i * (nsub // 4) + jg
                            g.wait_ge(ctx.sem["vEvac"], veb + kg + 1)
                            dstg = wT_ap[jg * 4:(jg + 1) * 4, :,
                                         i * P:(i + 1) * P].rearrange(
                                             "j p c -> p j c")
                            spl.issue(g, kg, [(dstg, tbuf[:, kg % 4])])
                    spl.wait_all(g)

                @blk.scalar
                def _(sc):
                    for i in range(ntile):
                        lds2.wait(sc, i)
                        if i >= 2:
                            sc.wait_ge(ctx.sem["vMain"], vqb + i - 1)
                        sc.activation(abuf[:], wbuf[:, i % 2], AF.Abs,
                                      scale=invg_ap[:])
                        sc.drain()
                        sc.activation(bbuf[:], abuf[:], AF.Sign, bias=CP["neg_half"])
                        sc.drain()
                        sc.activation(rl[:, i % 2], bbuf[:], AF.Relu)
                        sc.drain()
                        ctx.inc(sc.activation(sgn[:, i % 2], wbuf[:, i % 2], AF.Sign),
                                "aMain")

                @blk.tensor
                def _(t):
                    for i in range(ntile):
                        t.wait_ge(ctx.sem["vMain"], vqb + i + 1)
                        for jg in range(nsub // 4):
                            kg = i * (nsub // 4) + jg
                            if kg >= 2:
                                t.wait_ge(ctx.sem["vEvac"], veb + kg - 1)
                            for jj in range(4):
                                j = jg * 4 + jj
                                ins = t.transpose(
                                    ps[:, kg % 2, jj * P:(jj + 1) * P],
                                    qb[:, i % 2, j * P:(j + 1) * P], ident[:])
                            ctx.inc(ins, "tTR")

                @blk.vector
                def _(v):
                    ng = nsub // 4
                    for i in range(ntile + 1):
                        if i < ntile:
                            v.wait_ge(ctx.sem["aMain"], ab + i + 1)
                            if i >= 2:
                                v.wait_ge(ctx.sem["tTR"], tb_ + (i - 1) * ng)
                            ctx.inc(v.tensor_tensor(qb[:, i % 2], rl[:, i % 2],
                                                    sgn[:, i % 2], OP.mult), "vMain")
                        if i >= 1:
                            for jg in range(ng):
                                kg = (i - 1) * ng + jg
                                v.wait_ge(ctx.sem["tTR"], tb_ + kg + 1)
                                if kg >= 4:
                                    spl.wait(v, kg - 4)
                                ctx.inc(v.tensor_copy(tbuf[:, kg % 4],
                                                      ps[:, kg % 2, 0:512]),
                                        "vEvac")

    if "w1" in phases:
        for tag in "qkvo":
            emit_bitlinear_prep(w_in[tag], s_wT[tag], gam[tag], invg[tag], tag,
                                extra_scale=(1.0 / math.sqrt(P) if tag == "q"
                                             else None))

    def emit_mx_weight_prep(w_ap, nrow, ncol, dst, dst_kind, tag):
        ntile = nrow // P
        nsub = ncol // P
        ngrp = ncol // 32
        pair = dst_kind == "gu"
        ndq = 2 if pair else 1
        nwb = 4 if pair else 2
        lds = DBuf(ctx, ["q0", "q1"])
        spl = DBuf(ctx, ["q2", "q3", "q4", "q5"])
        ab = ctx.cnt["aMain"]
        vqb = ctx.cnt["vMain"]
        veb = ctx.cnt["vEvac"]
        tb_ = ctx.cnt["tTR"]
        gb = ctx.cnt["gMain"]
        with (
            nc.sbuf_tensor(f"w2_wbuf_{tag}", [P, nwb, ncol], F32) as wbuf,
            nc.sbuf_tensor(f"w2_ma_{tag}", [P, 2 if pair else 1, 2, ngrp],
                           F32) as ma,
            nc.sbuf_tensor(f"w2_r2_{tag}", [P, 2 if pair else 1, 2, ngrp],
                           F32) as r2,
            nc.sbuf_tensor(f"w2_q8_{tag}", [P, 2 if pair else 1, ncol],
                           FP8) as q8,
            nc.sbuf_tensor(f"w2_dq_{tag}", [P, ndq, ncol], F32) as dq,
            nc.sbuf_tensor(f"w2_hi_{tag}", [P, 2, ncol], BF16) as hib,
            nc.sbuf_tensor(f"w2_lo_{tag}", [P, 2, ncol], BF16) as lob,
            nc.sbuf_tensor(f"w2_tb_{tag}", [P, 4, 1024], BF16) as tbuf,
            nc.psum_tensor(f"w2_ps_{tag}", [P, 2, 1024], BF16) as ps,
            nc.Block() as blk,
        ):
            gsz = max(g for g in (8, 4, 2, 1) if nsub % g == 0)
            ng = nsub // gsz

            def dst_group(rt, jg, piece):
                if dst_kind == "gu":
                    ic, pos = (rt * P) // 512, rt % 4
                    off = (0 if piece == 0 else 512) + pos * P
                    d = dst[jg * gsz:(jg + 1) * gsz, ic, :, off:off + P]
                else:
                    och, pos = (rt * P) // OCW, rt % (OCW // P)
                    off = (0 if piece == 0 else OCW) + pos * P
                    d = dst[jg * gsz:(jg + 1) * gsz, och, :, off:off + P]
                return d.rearrange("j p c -> p j c")

            @blk.sync
            def _(s):
                for i in range(ntile):
                    if i >= 2:
                        lds.wait(s, i - 2)
                    if i >= nwb:
                        s.wait_ge(ctx.sem["vMain"], vqb + i - nwb + 1)
                    lds.issue(s, i, [(wbuf[:, i % nwb],
                                      w_ap[i * P:(i + 1) * P, :])])

            @blk.scalar
            def _(sc):
                def evacs(i):
                    for piece in range(2):
                        for jg in range(ng):
                            kg = (i * 2 + piece) * ng + jg
                            sc.wait_ge(ctx.sem["tTR"], tb_ + kg + 1)
                            if kg >= 4:
                                spl.wait(sc, kg - 4)
                            ctx.inc(sc.activation(tbuf[:, kg % 4, 0:gsz * P],
                                                  ps[:, kg % 2, 0:gsz * P],
                                                  AF.Identity), "vEvac")
                            if kg >= 1:
                                sc.wait_ge(ctx.sem["vEvac"], veb + kg)
                                ig, r = divmod(kg - 1, 2 * ng)
                                spl.issue(sc, kg - 1,
                                          [(dst_group(ig, r % ng, r // ng),
                                            tbuf[:, (kg - 1) % 4, 0:gsz * P])])
                for i in range(ntile):
                    sc.wait_ge(ctx.sem["vMain"], vqb + i + 1)
                    if i >= 2:
                        sc.wait_ge(ctx.sem["tTR"], tb_ + 2 * ng * (i - 1))
                    ctx.inc(sc.activation(hib[:, i % 2], dq[:, i % ndq],
                                          AF.Identity), "aMain")
                    if i >= 1:
                        evacs(i - 1)
                evacs(ntile - 1)
                kg = 2 * ng * ntile - 1
                sc.wait_ge(ctx.sem["vEvac"], veb + kg + 1)
                ig, r = divmod(kg, 2 * ng)
                spl.issue(sc, kg, [(dst_group(ig, r % ng, r // ng),
                                    tbuf[:, kg % 4, 0:gsz * P])])
                spl.wait_all(sc)

            @blk.tensor
            def _(t):
                for i in range(ntile):
                    t.wait_ge(ctx.sem["aMain"], ab + i + 1)
                    for piece, srcb in ((0, hib), (1, lob)):
                        if piece == 1:
                            t.wait_ge(ctx.sem["gMain"], gb + i + 1)
                        for jg in range(ng):
                            kg = (i * 2 + piece) * ng + jg
                            if kg >= 2:
                                t.wait_ge(ctx.sem["vEvac"], veb + kg - 1)
                            for jj in range(gsz):
                                j = jg * gsz + jj
                                ins = t.transpose(
                                    ps[:, kg % 2, jj * P:(jj + 1) * P],
                                    srcb[:, i % 2, j * P:(j + 1) * P], ident[:])
                            ctx.inc(ins, "tTR")

            @blk.gpsimd
            def _(g):
                for i in range(ntile):
                    g.wait_ge(ctx.sem["vMain"], vqb + i + 1)
                    g.wait_ge(ctx.sem["aMain"], ab + i + 1)
                    if i >= 2:
                        g.wait_ge(ctx.sem["tTR"], tb_ + 2 * ng * (i - 1))
                    ctx.inc(g.tensor_tensor(lob[:, i % 2], dq[:, i % ndq],
                                            hib[:, i % 2], OP.subtract), "gMain")

            @blk.vector
            def _(v):
                step = 2 if pair else 1
                for i0 in range(0, ntile, step):
                    tiles = list(range(i0, min(i0 + step, ntile)))
                    pj = (i0 // 2) % 2

                    def mslot(i):
                        return (pj, i % 2) if pair else (0, i % 2)

                    for i in tiles:
                        lds.wait(v, i)
                        w3 = wbuf[:, i % nwb].rearrange("p (g b) -> p g b", b=32)
                        v.tensor_reduce(ma[:, mslot(i)[0], mslot(i)[1]], w3,
                                        AX.X, OP.max, apply_absolute_value=True)
                    v.drain()
                    if pair and len(tiles) == 2:
                        mboth = ma[:, pj].rearrange("p s g -> p (s g)")
                        rboth = r2[:, pj].rearrange("p s g -> p (s g)")
                        v.tensor_scalar(mboth, mboth, 2.0 / 448.0, 2e-30,
                                        OP.mult, OP.max)
                        v.drain()
                        v.reciprocal(rboth, mboth)
                    else:
                        i = tiles[0]
                        sl = mslot(i)
                        v.tensor_scalar(ma[:, sl[0], sl[1]], ma[:, sl[0], sl[1]],
                                        2.0 / 448.0, 2e-30, OP.mult, OP.max)
                        v.drain()
                        v.reciprocal(r2[:, sl[0], sl[1]], ma[:, sl[0], sl[1]])
                    v.drain()
                    for i in tiles:
                        w3 = wbuf[:, i % nwb].rearrange("p (g b) -> p g b", b=32)
                        sl = mslot(i)
                        v.tensor_tensor(
                            q8[:, i % 2 if pair else 0].rearrange(
                                "p (g b) -> p g b", b=32), w3,
                            bcast_ap(r2[:, sl[0], sl[1]], 32), OP.mult)
                    v.drain()
                    for i in tiles:
                        if i >= ndq:
                            v.wait_ge(ctx.sem["aMain"], ab + i - ndq + 1)
                            v.wait_ge(ctx.sem["gMain"], gb + i - ndq + 1)
                        sl = mslot(i)
                        ctx.inc(v.tensor_tensor(
                            dq[:, i % ndq].rearrange("p (g b) -> p g b", b=32),
                            q8[:, i % 2 if pair else 0].rearrange(
                                "p (g b) -> p g b", b=32),
                            bcast_ap(ma[:, sl[0], sl[1]], 32), OP.mult), "vMain")

    if "w2" in phases:
        emit_mx_weight_prep(gate_w, DFF, D, s_wg, "gu", "g")
        emit_mx_weight_prep(up_w, DFF, D, s_wu, "gu", "u")
        emit_mx_weight_prep(down_w, D, DFF, s_wd, "d", "d")

    def spidx(qc, kk, np_):
        return qc * NDG * np_ + kk

    def emit_ln_spill(x_ap, nchunk, fm_dst, tagp):
        NT4 = 4
        NGRP = D // 512
        ngf = FC // 4
        NTILE = nchunk * NT4
        lds = DBuf(ctx, ["q0", "q1"])
        spl = DBuf(ctx, ["q2", "q3"])
        ab = ctx.cnt["aMain"]
        vb = ctx.cnt["vMain"]
        xb = ctx.cnt["vAux"]
        veb = ctx.cnt["vEvac"]
        tb_ = ctx.cnt["tTR"]
        gb = ctx.cnt["gMain"]
        order = [("a", 0), ("a", 1)] + \
            [x for t in range(NT4 - 2) for x in (("b", t), ("a", t + 2))] + \
            [("b", NT4 - 2), ("b", NT4 - 1)]
        act_order = [("s", 0), ("s", 1)] + \
            [x for t in range(NT4 - 2) for x in (("x", t), ("s", t + 2))] + \
            [("x", NT4 - 2), ("x", NT4 - 1)]
        pos_nm, pos_lo = {}, {}
        for qc in range(nchunk):
            for n_, (kind_, t_) in enumerate(order):
                (pos_nm if kind_ == "a" else pos_lo)[qc * NT4 + t_] = \
                    vb + qc * 8 + n_ + 1
        pos_sq, pos_x = {}, {}
        for qc in range(nchunk):
            for n_, (kind_, t_) in enumerate(act_order):
                (pos_sq if kind_ == "s" else pos_x)[qc * NT4 + t_] = \
                    ab + qc * 8 + n_ + 1
        pos_ag = {gi: xb + gi // NT4 * 4 + gi % NT4 + 1 for gi in range(NTILE)}
        pos_xh = {gi: gb + gi + 1 for gi in range(NTILE)}

        with (
            nc.sbuf_tensor(f"a_x_{tagp}", [P, 2, D], F32) as xbuf,
            nc.sbuf_tensor(f"a_stat_{tagp}", [P, 4, NGRP * 6], F32) as bns,
            nc.sbuf_tensor(f"a_mv_{tagp}", [P, 4, 2], F32) as mv,
            nc.sbuf_tensor(f"a_rv_{tagp}", [P, 4], F32) as rv,
            nc.sbuf_tensor(f"a_sq_{tagp}", [P, 4], F32) as sq,
            nc.sbuf_tensor(f"a_nm_{tagp}", [P, 4], F32) as nm,
            nc.sbuf_tensor(f"a_xln_{tagp}", [P, 2, D], F32) as xlnb,
            nc.sbuf_tensor(f"a_xh_{tagp}", [P, 3, D], BF16) as xh,
            nc.sbuf_tensor(f"a_xl_{tagp}", [P, 3, D], BF16) as xl,
            nc.sbuf_tensor(f"a_fmh_{tagp}", [P, 2, FC, 4 * P], BF16) as fmh,
            nc.sbuf_tensor(f"a_fml_{tagp}", [P, 2, FC, 4 * P], BF16) as fml,
            nc.psum_tensor(f"a_pst_{tagp}", [P, 4, 1024], BF16) as pst,
            nc.Block() as blk,
        ):
            @blk.sync
            def _(s):
                def spills(qc):
                    for piece, buf in ((0, fmh), (1, fml)):
                        idx = qc * 2 + piece
                        if idx >= 2:
                            spl.wait(s, idx - 2)
                        s.wait_ge(ctx.sem["vEvac"],
                                  veb + (qc + 1) * NT4 * 2 * ngf)
                        spl.issue(s, idx,
                                  [(fm_dst[qc, piece].rearrange(
                                      "f p c -> p f c"), buf[:, qc % 2])])
                for qc in range(nchunk):
                    for t4 in range(NT4):
                        gi = qc * NT4 + t4
                        if gi >= 2:
                            lds.wait(s, gi - 2)
                            s.wait_ge(ctx.sem["aMain"], pos_x[gi - 2])
                        lds.issue(s, gi, [(xbuf[:, gi % 2],
                                           x_ap[gi * P:(gi + 1) * P, :])])
                    if qc >= 1:
                        spills(qc - 1)
                spills(nchunk - 1)

            @blk.scalar
            def _(sc):
                for qc in range(nchunk):
                    for kind, t4 in act_order:
                        gi = qc * NT4 + t4
                        if kind == "s":
                            sc.wait_ge(ctx.sem["vAux"], pos_ag[gi])
                            if qc >= 1:
                                sc.wait_ge(ctx.sem["vMain"],
                                           pos_nm[gi - NT4])
                            ctx.inc(sc.activation(sq[:, t4:t4 + 1],
                                                  mv[:, t4, 1:2], AF.Sqrt,
                                                  bias=CP["eps"]), "aMain")
                        else:
                            sc.wait_ge(ctx.sem["vMain"], pos_nm[gi])
                            if gi >= 2:
                                sc.wait_ge(ctx.sem["vMain"], pos_lo[gi - 2])
                                sc.wait_ge(ctx.sem["gMain"], pos_xh[gi - 2])
                            ctx.inc(sc.activation(xlnb[:, gi % 2],
                                                  xbuf[:, gi % 2],
                                                  AF.Identity,
                                                  bias=nm[:, t4:t4 + 1],
                                                  scale=rv[:, t4:t4 + 1]),
                                    "aMain")
                            if qc >= 2 and t4 == 0:
                                spl.wait(sc, (qc - 2) * 2)
                                spl.wait(sc, (qc - 2) * 2 + 1)
                            for piece in range(2):
                                for jg in range(ngf):
                                    kg = gi * 2 * ngf + piece * ngf + jg
                                    sc.wait_ge(ctx.sem["tTR"], tb_ + kg + 1)
                                    dstp = fmh if piece == 0 else fml
                                    ctx.inc(sc.activation(
                                        dstp[:, qc % 2, jg * 4:(jg + 1) * 4,
                                             t4 * P:(t4 + 1) * P],
                                        pst[:, kg % 4, 0:512],
                                        AF.Identity), "vEvac")

            @blk.gpsimd
            def _(g):
                for gi in range(NTILE):
                    g.wait_ge(ctx.sem["aMain"], pos_x[gi])
                    if gi >= 3:
                        g.wait_ge(ctx.sem["tTR"], tb_ + (gi - 2) * 2 * ngf)
                    ctx.inc(g.tensor_copy(xh[:, gi % 3], xlnb[:, gi % 2]),
                            "gMain")

            @blk.tensor
            def _(t):
                for gi in range(NTILE):
                    qc, t4 = divmod(gi, NT4)
                    t.wait_ge(ctx.sem["gMain"], pos_xh[gi])
                    for piece, srcb in ((0, xh), (1, xl)):
                        if piece == 1:
                            t.wait_ge(ctx.sem["vMain"], pos_lo[gi])
                        for jg in range(ngf):
                            kg = gi * 2 * ngf + piece * ngf + jg
                            if kg >= 4:
                                t.wait_ge(ctx.sem["vEvac"], veb + kg - 3)
                            for jj in range(4):
                                j = jg * 4 + jj
                                ins = t.transpose(
                                    pst[:, kg % 4, jj * P:(jj + 1) * P],
                                    srcb[:, gi % 3, j * P:(j + 1) * P],
                                    ident[:])
                            ctx.inc(ins, "tTR")

            @blk.vector
            def _(v):
                for qc in range(nchunk):
                    for kind, t4 in order:
                        gi = qc * NT4 + t4
                        if kind == "a":
                            lds.wait(v, gi)
                            if qc >= 1:
                                v.wait_ge(ctx.sem["aMain"], pos_x[gi - NT4])
                            for gr in range(NGRP):
                                v.bn_stats(bns[:, t4, gr * 6:gr * 6 + 6],
                                           xbuf[:, gi % 2,
                                                gr * 512:(gr + 1) * 512])
                            v.drain()
                            ctx.inc(v.bn_aggr(mv[:, t4], bns[:, t4]), "vAux")
                            v.wait_ge(ctx.sem["aMain"], pos_sq[gi])
                            v.reciprocal(rv[:, t4:t4 + 1], sq[:, t4:t4 + 1])
                            v.drain()
                            ctx.inc(v.scalar_tensor_tensor(
                                nm[:, t4:t4 + 1], mv[:, t4, 0:1], -1.0,
                                rv[:, t4:t4 + 1], OP.mult, OP.mult), "vMain")
                        else:
                            v.wait_ge(ctx.sem["gMain"], pos_xh[gi])
                            if gi >= 3:
                                v.wait_ge(ctx.sem["tTR"],
                                          tb_ + (gi - 2) * 2 * ngf)
                            ctx.inc(v.tensor_tensor(xl[:, gi % 3],
                                                    xlnb[:, gi % 2],
                                                    xh[:, gi % 3],
                                                    OP.subtract), "vMain")

        with nc.Block() as blk:
            @blk.sync
            def _(s):
                for d in (lds, spl):
                    d.wait_all(s)

    def emit_proj_all(nchunk, mode, fm_src, tagp):
        np_ = 8 if mode == "kv" else 4
        lds = DBuf(ctx, ["q0", "q1"])
        wk_s = DBuf(ctx, ["q2", "q3"])
        wv_s = DBuf(ctx, ["q4", "q5"])
        spk = DBuf(ctx, ["g0", "g1"])
        spv = DBuf(ctx, ["g2", "g3"])
        tb2 = ctx.cnt["tMM"]
        ab2 = ctx.cnt["aMain"]
        vb2 = ctx.cnt["vMain"]
        CU = NDG * np_
        with (
            nc.sbuf_tensor(f"a_fmh_{tagp}", [P, 2, FC, 4 * P], BF16) as fmh,
            nc.sbuf_tensor(f"a_fml_{tagp}", [P, 2, FC, 4 * P], BF16) as fml,
            nc.sbuf_tensor(f"a_wkb_{tagp}", [P, 2, FC, DG], BF16) as wkb,
            nc.sbuf_tensor(f"a_wvb_{tagp}", [P, 2, FC, DG], BF16) as wvb,
            nc.sbuf_tensor(f"a_keh_{tagp}", [P, 2, DG], BF16) as keh,
            nc.sbuf_tensor(f"a_kel_{tagp}", [P, 2, DG], BF16) as kel,
            nc.psum_tensor(f"a_psk_{tagp}", [P, 2, DG], F32) as psk,
            nc.Block() as blk,
        ):
            @blk.sync
            def _(s):
                for qc in range(nchunk):
                    if qc >= 2:
                        lds.wait(s, qc - 2)
                        s.wait_ge(ctx.sem["tMM"], tb2 + (qc - 1) * CU)
                    lds.issue(s, qc, [(fmh[:, qc % 2],
                                       fm_src[qc, 0].rearrange(
                                           "f p c -> p f c")),
                                      (fml[:, qc % 2],
                                       fm_src[qc, 1].rearrange(
                                           "f p c -> p f c"))])
                    for dg in range(NDG):
                        i = qc * NDG + dg
                        if i >= 2:
                            s.wait_ge(ctx.sem["tMM"], tb2 + np_ * (i - 1))
                        wk_s.issue(s, i,
                                   [(wkb[:, i % 2, f],
                                     s_wT["k" if mode == "kv" else "q"]
                                     [f, :, dg * DG:(dg + 1) * DG])
                                    for f in range(FC)])
                        if mode == "kv":
                            wv_s.issue(s, i,
                                       [(wvb[:, i % 2, f],
                                         s_wT["v"][f, :, dg * DG:(dg + 1) * DG])
                                        for f in range(FC)])

            @blk.gpsimd
            def _(g):
                for qc in range(nchunk):
                    for dg in range(NDG):
                        for u in range(np_):
                            kk = dg * np_ + u
                            K = qc * CU + kk
                            g.wait_ge(ctx.sem["vMain"], vb2 + K + 1)
                            if mode == "q":
                                h = dg * 4 + u
                                spk.issue(g, spidx(qc, kk, np_),
                                          [(s_qh[h, :, qc * 512:(qc + 1) * 512],
                                            keh[:, K % 2]),
                                           (s_ql[h, :, qc * 512:(qc + 1) * 512],
                                            kel[:, K % 2])])
                            elif u < 4:
                                h = dg * 4 + u
                                spk.issue(g, spidx(qc, kk, np_),
                                          [(s_kh[h, :, qc * 512:(qc + 1) * 512],
                                            keh[:, K % 2]),
                                           (s_kl[h, :, qc * 512:(qc + 1) * 512],
                                            kel[:, K % 2])])
                            else:
                                tt4 = u - 4
                                ttg = qc * 4 + tt4
                                pairs = []
                                for hh in range(4):
                                    h = dg * 4 + hh
                                    pairs.append(
                                        (s_v[h, ttg, :, 0:P],
                                         keh[:, K % 2, hh * P:(hh + 1) * P]))
                                    pairs.append(
                                        (s_v[h, ttg, :, P:2 * P],
                                         kel[:, K % 2, hh * P:(hh + 1) * P]))
                                spv.issue(g, spidx(qc, kk, np_), pairs)

            def spill_wait(eng, qc, kk):
                gk = qc * CU + kk
                if gk < 2:
                    return
                kkp, qcp = divmod(gk - 2, CU)[1], (gk - 2) // CU
                st = spk if (mode == "q" or kkp % np_ < 4) else spv
                st.wait(eng, spidx(qcp, kkp, np_))

            @blk.tensor
            def _(t):
                for qc in range(nchunk):
                    lds.wait(t, qc)
                    for dg in range(NDG):
                        i = qc * NDG + dg
                        wk_s.wait(t, i)
                        if mode == "kv":
                            wv_s.wait(t, i)
                        for u in range(np_):
                            kk = dg * np_ + u
                            K = qc * CU + kk
                            if K >= 2:
                                t.wait_ge(ctx.sem["vMain"], vb2 + K - 1)
                            last = None
                            if mode == "q" or u < 4:
                                dch = u
                                for f in range(FC):
                                    lt = wkb[:, i % 2, f,
                                             dch * P:(dch + 1) * P]
                                    t.matmul(psk[:, K % 2], lt,
                                             fmh[:, qc % 2, f],
                                             start=(f == 0), stop=False)
                                    last = t.matmul(psk[:, K % 2], lt,
                                                    fml[:, qc % 2, f],
                                                    start=False,
                                                    stop=(f == FC - 1))
                            else:
                                tt4 = u - 4
                                for f in range(FC):
                                    t.matmul(psk[:, K % 2],
                                             fmh[:, qc % 2, f,
                                                 tt4 * P:(tt4 + 1) * P],
                                             wvb[:, i % 2, f],
                                             start=(f == 0), stop=False)
                                    last = t.matmul(
                                        psk[:, K % 2],
                                        fml[:, qc % 2, f,
                                            tt4 * P:(tt4 + 1) * P],
                                        wvb[:, i % 2, f], start=False,
                                        stop=(f == FC - 1))
                            ctx.inc(last, "tMM")

            @blk.scalar
            def _(sc):
                for qc in range(nchunk):
                    for kk in range(CU):
                        K = qc * CU + kk
                        sc.wait_ge(ctx.sem["tMM"], tb2 + K + 1)
                        spill_wait(sc, qc, kk)
                        gm = gam[("q" if mode == "q" else
                                  ("k" if kk % np_ < 4 else "v"))]
                        ctx.inc(sc.activation(keh[:, K % 2], psk[:, K % 2],
                                              AF.Copy, scale=gm[:]), "aMain")

            @blk.vector
            def _(v):
                for qc in range(nchunk):
                    for kk in range(CU):
                        K = qc * CU + kk
                        v.wait_ge(ctx.sem["aMain"], ab2 + K + 1)
                        spill_wait(v, qc, kk)
                        gm = gam[("q" if mode == "q" else
                                  ("k" if kk % np_ < 4 else "v"))]
                        ctx.inc(v.scalar_tensor_tensor(
                            kel[:, K % 2], psk[:, K % 2], gm[:],
                            keh[:, K % 2], OP.mult, OP.subtract), "vMain")

        with nc.Block() as blk:
            @blk.sync
            def _(s):
                for d in (lds, wk_s, wv_s, spk, spv):
                    d.wait_all(s)

    if "a" in phases:
        emit_ln_spill(x_batch, NCH, s_fma, "a")
        emit_ln_spill(x_own, NCHO, s_fmq, "a2")
        emit_proj_all(NCH, "kv", s_fma, "pa")
        emit_proj_all(NCHO, "q", s_fmq, "pa2")

    def mmseq(Nkt):
        seq = []
        for kt in range(Nkt + 2):
            if kt < Nkt:
                seq.append(("S", kt))
            if kt >= 2:
                seq.append(("A", kt - 2))
        posS = {kt: seq.index(("S", kt)) + 1 for kt in range(Nkt)}
        posA = {kt: seq.index(("A", kt)) + 1 for kt in range(Nkt)}
        return seq, posS, posA

    def emit_attention(s_idx, avh, avl):
        Es = (2 * s_idx + 2) * QW
        Nkt = Es // P
        zone0 = Nkt - ZONE
        seq, posS, posA = mmseq(Nkt)
        L = len(seq)
        kb = DBuf(ctx, ["q0", "q1"])
        vb_ = DBuf(ctx, ["q2", "q3"])
        qb_ = DBuf(ctx, ["q4", "q5"])
        tS0 = ctx.cnt["tMM"]
        vC0 = ctx.cnt["vEvac"]
        eR0 = ctx.cnt["vAux"]
        aE0 = ctx.cnt["aMain"]
        hD0 = ctx.cnt["vMain"]
        tB0 = ctx.cnt["tTR"]
        zcum = {}
        lastz = {}
        zprev = {}
        zc = vC0
        for h in range(H):
            for kt in range(Nkt):
                kidx = h * Nkt + kt
                if kt >= zone0:
                    zc += 1
                    zcum[kidx] = zc
                    zprev[kidx] = lastz.get(kidx % 3)
                    lastz[kidx % 3] = kidx

        def t_S(h, kt):
            return tS0 + h * L + posS[kt]

        def t_A(h, kt):
            return tS0 + h * L + posA[kt]

        with (
            nc.sbuf_tensor(f"b_k_{s_idx}", [P, 2, 2, Es], BF16) as kbf,
            nc.sbuf_tensor(f"b_v_{s_idx}", [P, 2, (Es // P) * 2 * P], BF16) as vbf,
            nc.sbuf_tensor(f"b_q_{s_idx}", [P, 2, 2, QW], BF16) as qbf,
            nc.sbuf_tensor(f"b_ein_{s_idx}", [P, 3, QW], F32) as ein,
            nc.sbuf_tensor(f"b_ef_{s_idx}", [P, 3, QW], F32) as ef,
            nc.sbuf_tensor(f"b_eh_{s_idx}", [P, 3, QW], BF16) as ehb,
            nc.sbuf_tensor(f"b_el_{s_idx}", [P, 3, QW], BF16) as elb,
            nc.sbuf_tensor(f"b_rc_{s_idx}", [1, QW], F32) as rc,
            nc.sbuf_tensor(f"b_rcb_{s_idx}", [P, QW], F32) as rcb,
            nc.sbuf_tensor(f"b_avt_{s_idx}", [P, QW], F32) as avt,
            nc.psum_tensor(f"b_ps_{s_idx}", [P, 3, 512], F32) as psS,
            nc.psum_tensor(f"b_pav_{s_idx}", [P, 2, 512], F32) as psAV,
            nc.psum_tensor(f"b_pcs_{s_idx}", [P, 2, 512], F32) as psCS,
            nc.psum_tensor(f"b_pb_{s_idx}", [P, 512], F32) as psB,
            nc.Block() as blk,
        ):
            @blk.sync
            def _(s):
                for h in range(H):
                    if h >= 2:
                        s.wait_ge(ctx.sem["tMM"], tS0 + (h - 1) * L)
                    kb.issue(s, h, [(kbf[:, h % 2, 0, :], s_kh[h, :, 0:Es]),
                                    (kbf[:, h % 2, 1, :], s_kl[h, :, 0:Es])])
                    ntt = Es // P
                    half = max(1, ntt // 2)
                    pairs = []
                    for j0 in range(0, ntt, half):
                        j1 = min(ntt, j0 + half)
                        src_v = s_v[h, j0:j1].rearrange("t p c -> p t c")
                        dst_v = vbf[:, h % 2, j0 * 2 * P:j1 * 2 * P].rearrange(
                            "p (t c) -> p t c", c=2 * P)
                        pairs.append((dst_v, src_v))
                    vb_.issue(s, h, pairs)
                    qb_.issue(s, h, [(qbf[:, h % 2, 0, :],
                                      s_qh[h, :, s_idx * QW:(s_idx + 1) * QW]),
                                     (qbf[:, h % 2, 1, :],
                                      s_ql[h, :, s_idx * QW:(s_idx + 1) * QW])])

            @blk.tensor
            def _(t):
                for h in range(H):
                    kb.wait(t, h)
                    vb_.wait(t, h)
                    qb_.wait(t, h)
                    if h >= 2:
                        t.wait_ge(ctx.sem["vMain"], hD0 + 3 * (h - 2) + 2)
                    for kind, kt in seq:
                        kidx = h * Nkt + kt
                        if kind == "S":
                            kprev = kidx - 3
                            if kprev >= 0:
                                hp, ktp = divmod(kprev, Nkt)
                                if ktp >= zone0:
                                    t.wait_ge(ctx.sem["vEvac"], zcum[kprev])
                                else:
                                    t.wait_ge(ctx.sem["aMain"],
                                              aE0 + 2 * kprev + 2)
                            kh = kbf[:, h % 2, 0, kt * P:(kt + 1) * P]
                            kl = kbf[:, h % 2, 1, kt * P:(kt + 1) * P]
                            t.matmul(psS[:, kt % 3, 0:QW], kh, qbf[:, h % 2, 0, :],
                                     start=True, stop=False)
                            t.matmul(psS[:, kt % 3, 0:QW], kh, qbf[:, h % 2, 1, :],
                                     start=False, stop=False)
                            ctx.inc(t.matmul(psS[:, kt % 3, 0:QW], kl,
                                             qbf[:, h % 2, 0, :],
                                             start=False, stop=True), "tMM")
                        else:
                            k0 = kt
                            ki = h * Nkt + k0
                            t.wait_ge(ctx.sem["vAux"], eR0 + ki + 1)
                            vh = vbf[:, h % 2, k0 * 2 * P:k0 * 2 * P + P]
                            vl = vbf[:, h % 2, k0 * 2 * P + P:(k0 + 1) * 2 * P]
                            st = (k0 == 0)
                            sp = (k0 == Nkt - 1)
                            t.matmul(psAV[:, h % 2, 0:QW], vh, ehb[:, ki % 3],
                                     start=st, stop=False)
                            t.matmul(psAV[:, h % 2, 0:QW], vh, elb[:, ki % 3],
                                     start=False, stop=False)
                            t.matmul(psAV[:, h % 2, 0:QW], vl, ehb[:, ki % 3],
                                     start=False, stop=sp)
                            t.matmul(psCS[0:1, h % 2, 0:QW], ones_bf[:], ehb[:, ki % 3],
                                     start=st, stop=False)
                            ctx.inc(t.matmul(psCS[0:1, h % 2, 0:QW], ones_bf[:],
                                             elb[:, ki % 3], start=False, stop=sp),
                                    "tMM")
                    t.wait_ge(ctx.sem["vMain"], hD0 + 3 * h + 1)
                    ctx.inc(t.matmul(psB[:, 0:QW], onesf_r[:], rc[:], start=True,
                                     stop=True), "tTR")

            @blk.scalar
            def _(sc):
                for h in range(H):
                    for kt in range(Nkt):
                        kidx = h * Nkt + kt
                        if kidx >= 3:
                            sc.wait_ge(ctx.sem["vAux"], eR0 + kidx - 2)
                            hp, ktp = divmod(kidx - 3, Nkt)
                            sc.wait_ge(ctx.sem["tMM"], t_A(hp, ktp))
                        if kt >= zone0:
                            sc.wait_ge(ctx.sem["vEvac"], zcum[kidx])
                            src = ein[:, kidx % 3]
                        else:
                            sc.wait_ge(ctx.sem["tMM"], t_S(h, kt))
                            src = psS[:, kt % 3, 0:QW]
                        ctx.inc(sc.activation(ef[:, kidx % 3], src, AF.Exp,
                                              bias=CP["neg40"]), "aMain")
                        ctx.inc(sc.activation(ehb[:, kidx % 3], src, AF.Exp,
                                              bias=CP["neg40"]), "aMain")

            @blk.vector
            def _(v):
                for h in range(H):
                    for kt in range(Nkt):
                        kidx = h * Nkt + kt
                        if kt >= zone0:
                            v.wait_ge(ctx.sem["tMM"], t_S(h, kt))
                            if zprev.get(kidx) is not None:
                                v.wait_ge(ctx.sem["aMain"],
                                          aE0 + 2 * zprev[kidx] + 2)
                            z = kt - zone0
                            moff = (s_idx * ZONE + z) * QW
                            ctx.inc(v.tensor_tensor(ein[:, kidx % 3],
                                                    psS[:, kt % 3, 0:QW],
                                                    masks_sb[:, moff:moff + QW],
                                                    OP.add), "vEvac")
                        v.wait_ge(ctx.sem["aMain"], aE0 + 2 * kidx + 2)
                        if kidx >= 3:
                            hp, ktp = divmod(kidx - 3, Nkt)
                            v.wait_ge(ctx.sem["tMM"], t_A(hp, ktp))
                        ctx.inc(v.tensor_tensor(elb[:, kidx % 3], ef[:, kidx % 3],
                                                ehb[:, kidx % 3], OP.subtract),
                                "vAux")
                    v.wait_ge(ctx.sem["tMM"], tS0 + (h + 1) * L)
                    ctx.inc(v.reciprocal(rc[:], psCS[0:1, h % 2, 0:QW]), "vMain")
                    v.wait_ge(ctx.sem["tTR"], tB0 + h + 1)
                    v.tensor_copy(rcb[:], psB[:, 0:QW])
                    v.drain()
                    ctx.inc(v.tensor_tensor(avt[:], psAV[:, h % 2, 0:QW], rcb[:],
                                            OP.mult), "vMain")
                    v.drain()
                    v.tensor_copy(avh[:, h * QW:(h + 1) * QW], avt[:])
                    v.drain()
                    ctx.inc(v.tensor_tensor(avl[:, h * QW:(h + 1) * QW], avt[:],
                                            avh[:, h * QW:(h + 1) * QW],
                                            OP.subtract), "vMain")

        with nc.Block() as blk:
            @blk.sync
            def _(s):
                for d in (kb, vb_, qb_):
                    d.wait_all(s)

    def emit_outproj_c(s_idx, avh, avl, xp):
        wo_s = DBuf(ctx, ["q2", "q3"])
        xo_s = DBuf(ctx, ["q4", "q5"])
        spx = DBuf(ctx, ["g0", "g1"])
        spa = DBuf(ctx, ["g2", "g3", "g4", "g5"])
        NGRP = D // 512
        ngf = FC // 4
        tb2 = ctx.cnt["tMM"]
        vb2 = ctx.cnt["vMain"]
        with (
            nc.sbuf_tensor(f"c_wo_{s_idx}", [P, 2, FC, DG], BF16) as wob,
            nc.psum_tensor(f"c_pso_{s_idx}", [P, 2, DG], F32) as psO,
            nc.Block() as blk,
        ):
            @blk.scalar
            def _(sc):
                for qs in range(QT):
                    xo_s.issue(sc, qs,
                               [(xp[:, qs],
                                 x_own[(s_idx * QT + qs) * P:
                                       (s_idx * QT + qs + 1) * P, :])])
                for dg in range(NDG):
                    if dg >= 2:
                        sc.wait_ge(ctx.sem["tMM"], tb2 + QT * (dg - 1))
                    wo_s.issue(sc, dg,
                               [(wob[:, dg % 2, f],
                                 s_wT["o"][f, :, dg * DG:(dg + 1) * DG])
                                for f in range(FC)])

            @blk.tensor
            def _(t):
                for dg in range(NDG):
                    wo_s.wait(t, dg)
                    for qs in range(QT):
                        kk = dg * QT + qs
                        if kk >= 2:
                            t.wait_ge(ctx.sem["vMain"], vb2 + kk - 1)
                        last = None
                        for f in range(FC):
                            lt_h = avh[:, f * QW + qs * P:f * QW + (qs + 1) * P]
                            lt_l = avl[:, f * QW + qs * P:f * QW + (qs + 1) * P]
                            t.matmul(psO[:, kk % 2], lt_h, wob[:, dg % 2, f],
                                     start=(f == 0), stop=False)
                            last = t.matmul(psO[:, kk % 2], lt_l, wob[:, dg % 2, f],
                                            start=False, stop=(f == FC - 1))
                        ctx.inc(last, "tMM")

            @blk.vector
            def _(v):
                for dg in range(NDG):
                    for qs in range(QT):
                        kk = dg * QT + qs
                        v.wait_ge(ctx.sem["tMM"], tb2 + kk + 1)
                        if dg == 0:
                            xo_s.wait(v, qs)
                        ctx.inc(v.scalar_tensor_tensor(
                            xp[:, qs, dg * DG:(dg + 1) * DG], psO[:, kk % 2],
                            gam["o"][:], xp[:, qs, dg * DG:(dg + 1) * DG],
                            OP.mult, OP.add), "vMain")

        ab = ctx.cnt["aMain"]
        xb3 = ctx.cnt["vAux"]
        veb = ctx.cnt["vEvac"]
        tb3 = ctx.cnt["tTR"]
        vb3 = ctx.cnt["vMain"]
        stages = []
        for qs in range(QT):
            for st in ("bn", "nmrec", "red", "tsr", "q8", "dq", "axl"):
                stages.append((st, qs))
        pos_nm, pos_q8, pos_dq, pos_axl, pos_aggr = {}, {}, {}, {}, {}
        vm, va = vb3, xb3
        for st, qs in stages:
            if st == "bn":
                va += 1
                pos_aggr[qs] = va
            elif st in ("nmrec", "q8", "dq", "axl"):
                vm += 1
                {"nmrec": pos_nm, "q8": pos_q8, "dq": pos_dq,
                 "axl": pos_axl}[st][qs] = vm
        pos_sq = {qs: ab + 3 * qs + 1 for qs in range(QT)}
        pos_xln = {qs: ab + 3 * qs + 2 for qs in range(QT)}
        pos_axh = {qs: ab + 3 * qs + 3 for qs in range(QT)}
        pos_t = {(qs, kg): tb3 + qs * 2 * ngf + kg + 1
                 for qs in range(QT) for kg in range(2 * ngf)}
        pos_e = {(qs, kg): veb + qs * 2 * ngf + kg + 1
                 for qs in range(QT) for kg in range(2 * ngf)}
        with (
            nc.sbuf_tensor(f"c_bns_{s_idx}", [P, NGRP * 6], F32) as bns,
            nc.sbuf_tensor(f"c_mv_{s_idx}", [P, QT, 2], F32) as mv,
            nc.sbuf_tensor(f"c_rv_{s_idx}", [P, QT], F32) as rv,
            nc.sbuf_tensor(f"c_sq_{s_idx}", [P, QT], F32) as sq,
            nc.sbuf_tensor(f"c_nm_{s_idx}", [P, QT], F32) as nm,
            nc.sbuf_tensor(f"c_xln_{s_idx}", [P, 2, D], F32) as xlnb,
            nc.sbuf_tensor(f"c_ma_{s_idx}", [P, 2, NG], F32) as ma,
            nc.sbuf_tensor(f"c_r2_{s_idx}", [P, 2, NG], F32) as r2,
            nc.sbuf_tensor(f"c_q8_{s_idx}", [P, D], FP8) as q8,
            nc.sbuf_tensor(f"c_dq_{s_idx}", [P, D], F32) as dq,
            nc.sbuf_tensor(f"c_axh_{s_idx}", [P, D], BF16) as axh,
            nc.sbuf_tensor(f"c_axl_{s_idx}", [P, D], BF16) as axl,
            nc.sbuf_tensor(f"c_stg_{s_idx}", [P, 4, 512], BF16) as stg,
            nc.psum_tensor(f"c_pst_{s_idx}", [P, 2, 1024], BF16) as pst,
            nc.Block() as blk,
        ):
            @blk.gpsimd
            def _(g):
                for qs in range(QT):
                    tile = s_idx * QT + qs
                    spx.issue(g, tile, [(s_xp[tile], xp[:, qs])])
                for qs in range(QT):
                    tile = s_idx * QT + qs
                    for kg in range(2 * ngf):
                        gk = tile * 2 * ngf + kg
                        g.wait_ge(ctx.sem["vEvac"], pos_e[(qs, kg)])
                        jg, piece = kg % ngf, kg // ngf
                        dstg = s_ax[jg * 4:(jg + 1) * 4, tile, :,
                                    piece * P:(piece + 1) * P].rearrange(
                                        "j p c -> p j c")
                        spa.issue(g, gk, [(dstg, stg[:, gk % 4])])

            @blk.scalar
            def _(sc):
                def evacs(qs):
                    tile = s_idx * QT + qs
                    for kg in range(2 * ngf):
                        qk = qs * 2 * ngf + kg
                        gk = tile * 2 * ngf + kg
                        sc.wait_ge(ctx.sem["tTR"], pos_t[(qs, kg)])
                        if qk >= 4:
                            spa.wait(sc, gk - 4)
                        ctx.inc(sc.activation(stg[:, gk % 4],
                                              pst[:, qk % 2, 0:512],
                                              AF.Identity), "vEvac")
                for qs in range(QT):
                    sc.wait_ge(ctx.sem["vAux"], pos_aggr[qs])
                    ctx.inc(sc.activation(sq[:, qs:qs + 1], mv[:, qs, 1:2],
                                          AF.Sqrt, bias=CP["eps"]), "aMain")
                    sc.wait_ge(ctx.sem["vMain"], pos_nm[qs])
                    if qs >= 2:
                        sc.wait_ge(ctx.sem["vMain"], pos_q8[qs - 2])
                    ctx.inc(sc.activation(xlnb[:, qs % 2], xp[:, qs],
                                          AF.Identity, bias=nm[:, qs:qs + 1],
                                          scale=rv[:, qs:qs + 1]), "aMain")
                    if qs >= 1:
                        evacs(qs - 1)
                    sc.wait_ge(ctx.sem["vMain"], pos_dq[qs])
                    if qs >= 1:
                        sc.wait_ge(ctx.sem["tTR"], pos_t[(qs - 1, 2 * ngf - 1)])
                    ctx.inc(sc.activation(axh[:], dq[:], AF.Identity), "aMain")
                evacs(QT - 1)

            @blk.tensor
            def _(t):
                for qs in range(QT):
                    t.wait_ge(ctx.sem["aMain"], pos_axh[qs])
                    for piece, src in ((0, axh), (1, axl)):
                        if piece == 1:
                            t.wait_ge(ctx.sem["vMain"], pos_axl[qs])
                        for jg in range(ngf):
                            kg = piece * ngf + jg
                            qk = qs * 2 * ngf + kg
                            if qk >= 2:
                                t.wait_ge(ctx.sem["vEvac"], veb + qk - 1)
                            for jj in range(4):
                                j = jg * 4 + jj
                                ins = t.transpose(
                                    pst[:, qk % 2, jj * P:(jj + 1) * P],
                                    src[:, j * P:(j + 1) * P], ident[:])
                            ctx.inc(ins, "tTR")

            @blk.vector
            def _(v):
                for st, qs in stages:
                    if st == "bn":
                        for gi in range(NGRP):
                            v.bn_stats(bns[:, gi * 6:gi * 6 + 6],
                                       xp[:, qs, gi * 512:(gi + 1) * 512])
                        v.drain()
                        ctx.inc(v.bn_aggr(mv[:, qs], bns[:]), "vAux")
                    elif st == "nmrec":
                        v.wait_ge(ctx.sem["aMain"], pos_sq[qs])
                        v.reciprocal(rv[:, qs:qs + 1], sq[:, qs:qs + 1])
                        v.drain()
                        ctx.inc(v.scalar_tensor_tensor(
                            nm[:, qs:qs + 1], mv[:, qs, 0:1], -1.0,
                            rv[:, qs:qs + 1], OP.mult, OP.mult), "vMain")
                    elif st == "red":
                        v.wait_ge(ctx.sem["aMain"], pos_xln[qs])
                        x3 = xlnb[:, qs % 2].rearrange("p (g b) -> p g b", b=32)
                        v.tensor_reduce(ma[:, qs % 2], x3, AX.X, OP.max,
                                        apply_absolute_value=True)
                        v.drain()
                    elif st == "tsr":
                        v.tensor_scalar(ma[:, qs % 2], ma[:, qs % 2],
                                        2.0 / 448.0, 2e-30, OP.mult, OP.max)
                        v.drain()
                        v.reciprocal(r2[:, qs % 2], ma[:, qs % 2])
                        v.drain()
                    elif st == "q8":
                        x3 = xlnb[:, qs % 2].rearrange("p (g b) -> p g b", b=32)
                        ctx.inc(v.tensor_tensor(
                            q8.rearrange("p (g b) -> p g b", b=32), x3,
                            bcast_ap(r2[:, qs % 2], 32), OP.mult), "vMain")
                        v.drain()
                    elif st == "dq":
                        if qs >= 1:
                            v.wait_ge(ctx.sem["aMain"], pos_axh[qs - 1])
                        ctx.inc(v.tensor_tensor(
                            dq.rearrange("p (g b) -> p g b", b=32),
                            q8.rearrange("p (g b) -> p g b", b=32),
                            bcast_ap(ma[:, qs % 2], 32), OP.mult), "vMain")
                    elif st == "axl":
                        v.wait_ge(ctx.sem["aMain"], pos_axh[qs])
                        if qs >= 1:
                            v.wait_ge(ctx.sem["tTR"],
                                      pos_t[(qs - 1, 2 * ngf - 1)])
                        ctx.inc(v.tensor_tensor(axl[:], dq[:], axh[:],
                                                OP.subtract), "vMain")

        with nc.Block() as blk:
            @blk.sync
            def _(s):
                for d in (wo_s, xo_s, spx, spa):
                    d.wait_all(s)

    for s_idx in (range(NSLOT) if "b" in phases else ()):
        with (
            nc.sbuf_tensor(f"b_avh_{s_idx}", [P, H * QW], BF16) as avh,
            nc.sbuf_tensor(f"b_avl_{s_idx}", [P, H * QW], BF16) as avl,
        ):
            emit_attention(s_idx, avh, avl)
            with nc.sbuf_tensor(f"b_xp_{s_idx}", [P, QT, D], F32) as xp:
                emit_outproj_c(s_idx, avh, avl, xp)

    def emit_d1():
        wg_s = DBuf(ctx, ["q0", "q1"])
        wu_s = DBuf(ctx, ["q2", "q3"])
        ax_s = DBuf(ctx, ["q4", "q5"])
        sph = DBuf(ctx, ["g0", "g1", "g2", "g3"])
        NITER = NI * NTO
        with (
            nc.sbuf_tensor("d1_wg", [P, 2, FC, 1024], BF16) as wgb,
            nc.sbuf_tensor("d1_wu", [P, 2, FC, 1024], BF16) as wub,
            nc.sbuf_tensor("d1_ax", [P, 2, FC, 2 * P], BF16) as axb,
            nc.sbuf_tensor("d1_gs", [P, 512], F32) as gs,
            nc.sbuf_tensor("d1_h", [P, 512], F32) as hb,
            nc.sbuf_tensor("d1_ma", [P, NGI], F32) as ma,
            nc.sbuf_tensor("d1_r2", [P, NGI], F32) as r2,
            nc.sbuf_tensor("d1_q8", [P, 512], FP8) as q8,
            nc.sbuf_tensor("d1_dq", [P, 512], F32) as dq,
            nc.sbuf_tensor("d1_hh", [P, 512], BF16) as hh,
            nc.sbuf_tensor("d1_hl", [P, 512], BF16) as hl,
            nc.sbuf_tensor("d1_stg", [P, 2, 512], BF16) as stg,
            nc.psum_tensor("d1_psg", [P, 2, 512], F32) as psG,
            nc.psum_tensor("d1_psu", [P, 2, 512], F32) as psU,
            nc.psum_tensor("d1_pst", [P, 2, 1024], BF16) as pst,
            nc.Block() as blk,
        ):
            tb_ = ctx.cnt["tMM"]
            tr_ = ctx.cnt["tTR"]
            ab = ctx.cnt["aMain"]
            vb = ctx.cnt["vMain"]
            veb = ctx.cnt["vEvac"]

            @blk.sync
            def _(s):
                for ic in range(NI):
                    if ic >= 2:
                        s.wait_ge(ctx.sem["tMM"], tb_ + 2 * NTO * (ic - 1))
                    wg_s.issue(s, ic, [(wgb[:, ic % 2, f], s_wg[f, ic])
                                       for f in range(FC)])
                    wu_s.issue(s, ic, [(wub[:, ic % 2, f], s_wu[f, ic])
                                       for f in range(FC)])
                    for tt in range(NTO):
                        it = ic * NTO + tt
                        if it >= 2:
                            s.wait_ge(ctx.sem["tMM"], tb_ + 2 * (it - 1))
                        half = FC // 2
                        pairs = []
                        for j0 in range(0, FC, half):
                            src = s_ax[j0:j0 + half, tt].rearrange("f p c -> p f c")
                            pairs.append((axb[:, it % 2, j0:j0 + half], src))
                        ax_s.issue(s, it, pairs)

            @blk.gpsimd
            def _(g):
                for it in range(NITER):
                    ic, tt = divmod(it, NTO)
                    for piece in range(2):
                        kg = it * 2 + piece
                        g.wait_ge(ctx.sem["vEvac"], veb + kg + 1)
                        dstg = s_h[ic * 4:(ic + 1) * 4, tt, :,
                                   piece * P:(piece + 1) * P].rearrange(
                                       "j p c -> p j c")
                        sph.issue(g, kg, [(dstg, stg[:, piece])])

            @blk.tensor
            def _(t):
                def tr_group(it):
                    t.wait_ge(ctx.sem["vMain"], vb + 3 * it + 3)
                    for piece, src in ((0, hh), (1, hl)):
                        kg = it * 2 + piece
                        if kg >= 2:
                            t.wait_ge(ctx.sem["vEvac"], veb + kg - 1)
                        for jj in range(4):
                            ins = t.transpose(pst[:, kg % 2, jj * P:(jj + 1) * P],
                                              src[:, jj * P:(jj + 1) * P],
                                              ident[:])
                        ctx.inc(ins, "tTR")

                for it in range(NITER):
                    ic, tt = divmod(it, NTO)
                    if tt == 0:
                        wg_s.wait(t, ic)
                        wu_s.wait(t, ic)
                    ax_s.wait(t, it)
                    if it >= 2:
                        t.wait_ge(ctx.sem["aMain"], ab + it - 1)
                        t.wait_ge(ctx.sem["vMain"], vb + 3 * it - 5)
                    for f in range(FC):
                        ah = axb[:, it % 2, f, 0:P]
                        al = axb[:, it % 2, f, P:2 * P]
                        t.matmul(psG[:, it % 2], ah, wgb[:, ic % 2, f, 0:512],
                                 start=(f == 0), stop=False)
                        t.matmul(psG[:, it % 2], ah, wgb[:, ic % 2, f, 512:1024],
                                 start=False, stop=False)
                        t.matmul(psU[:, it % 2], ah, wub[:, ic % 2, f, 0:512],
                                 start=(f == 0), stop=False)
                        t.matmul(psU[:, it % 2], ah, wub[:, ic % 2, f, 512:1024],
                                 start=False, stop=False)
                        lg = t.matmul(psG[:, it % 2], al,
                                      wgb[:, ic % 2, f, 0:512], start=False,
                                      stop=(f == FC - 1))
                        lu = t.matmul(psU[:, it % 2], al,
                                      wub[:, ic % 2, f, 0:512], start=False,
                                      stop=(f == FC - 1))
                    ctx.inc(lg, "tMM")
                    ctx.inc(lu, "tMM")
                    if it >= 1:
                        tr_group(it - 1)
                if NITER >= 1:
                    tr_group(NITER - 1)

            @blk.scalar
            def _(sc):
                for it in range(NITER):
                    sc.wait_ge(ctx.sem["tMM"], tb_ + 2 * it + 1)
                    if it >= 1:
                        sc.wait_ge(ctx.sem["vMain"], vb + 3 * (it - 1) + 1)
                    ctx.inc(sc.activation(gs[:], psG[:, it % 2], AF.Silu), "aMain")

            @blk.vector
            def _(v):
                for it in range(NITER):
                    v.wait_ge(ctx.sem["aMain"], ab + it + 1)
                    v.wait_ge(ctx.sem["tMM"], tb_ + 2 * it + 2)
                    ctx.inc(v.tensor_tensor(hb[:], gs[:], psU[:, it % 2], OP.mult),
                            "vMain")
                    v.drain()
                    h3 = hb.rearrange("p (g b) -> p g b", b=32)
                    v.tensor_reduce(ma[:], h3, AX.X, OP.max,
                                    apply_absolute_value=True)
                    v.drain()
                    v.tensor_scalar(ma[:], ma[:], 2.0 / 448.0, 2e-30, OP.mult,
                                    OP.max)
                    v.drain()
                    v.reciprocal(r2[:], ma[:])
                    v.drain()
                    v.tensor_tensor(q8.rearrange("p (g b) -> p g b", b=32), h3,
                                    bcast_ap(r2, 32), OP.mult)
                    v.drain()
                    v.tensor_tensor(dq.rearrange("p (g b) -> p g b", b=32),
                                    q8.rearrange("p (g b) -> p g b", b=32),
                                    bcast_ap(ma, 32), OP.mult)
                    v.drain()
                    if it >= 1:
                        v.wait_ge(ctx.sem["tTR"], tr_ + 2 * it)
                    ctx.inc(v.tensor_copy(hh[:], dq[:]), "vMain")
                    v.drain()
                    ctx.inc(v.tensor_tensor(hl[:], dq[:], hh[:], OP.subtract),
                            "vMain")
                    for piece in range(2):
                        kg = it * 2 + piece
                        v.wait_ge(ctx.sem["tTR"], tr_ + kg + 1)
                        if kg >= 2:
                            sph.wait(v, kg - 2)
                        ctx.inc(v.tensor_copy(stg[:, piece],
                                              pst[:, kg % 2, 0:512]), "vEvac")

        with nc.Block() as blk:
            @blk.sync
            def _(s):
                for d in (wg_s, wu_s, ax_s, sph):
                    d.wait_all(s)

    masks_cm.__exit__(None, None, None)
    if "d1" in phases:
        emit_d1()

    def emit_d2():
        wd_s = DBuf(ctx, ["q0", "q1"])
        h_s = DBuf(ctx, ["q2", "q3"])
        xp_s = DBuf(ctx, ["q4", "q5"])
        out_s = DBuf(ctx, ["g0", "g1"])
        with (
            nc.sbuf_tensor("d2_wd", [P, NIS, 2 * OCW], BF16) as wdb,
            nc.sbuf_tensor("d2_h", [P, 2, NIS, 2 * P], BF16) as hbf,
            nc.sbuf_tensor("d2_xp", [P, 2, OCW], F32) as xpb,
            nc.sbuf_tensor("d2_yo", [P, 2, OCW], F32) as yob,
            nc.psum_tensor("d2_ps", [P, 2, 512], F32) as psD,
            nc.Block() as blk,
        ):
            tb_ = ctx.cnt["tMM"]
            vb = ctx.cnt["vMain"]

            @blk.sync
            def _(s):
                for oc in range(NOCH):
                    if oc >= 1:
                        s.wait_ge(ctx.sem["tMM"], tb_ + oc * NTO)
                    wd_s.issue(s, oc, [(wdb[:, isb], s_wd[isb, oc])
                                       for isb in range(NIS)])
                    for tt in range(NTO):
                        it = oc * NTO + tt
                        if it >= 2:
                            s.wait_ge(ctx.sem["tMM"], tb_ + it - 1)
                        half = NIS // 4
                        pairs = []
                        for j0 in range(0, NIS, half):
                            src = s_h[j0:j0 + half, tt].rearrange("f p c -> p f c")
                            pairs.append((hbf[:, it % 2, j0:j0 + half], src))
                        h_s.issue(s, it, pairs)
                        if it >= 2:
                            s.wait_ge(ctx.sem["vMain"], vb + it - 1)
                        xp_s.issue(s, it, [(xpb[:, it % 2],
                                            s_xp[tt, :, oc * OCW:(oc + 1) * OCW])])

            @blk.gpsimd
            def _(g):
                for oc in range(NOCH):
                    for tt in range(NTO):
                        it = oc * NTO + tt
                        g.wait_ge(ctx.sem["vMain"], vb + it + 1)
                        out_s.issue(g, it, [(y_out[tt * P:(tt + 1) * P,
                                                   oc * OCW:(oc + 1) * OCW],
                                             yob[:, it % 2])])

            @blk.tensor
            def _(t):
                for oc in range(NOCH):
                    for tt in range(NTO):
                        it = oc * NTO + tt
                        if tt == 0:
                            wd_s.wait(t, oc)
                        h_s.wait(t, it)
                        if it >= 2:
                            t.wait_ge(ctx.sem["vMain"], vb + it - 1)
                        last = None
                        for isb in range(NIS):
                            hh_ = hbf[:, it % 2, isb, 0:P]
                            hl_ = hbf[:, it % 2, isb, P:2 * P]
                            wh = wdb[:, isb, 0:OCW]
                            wl = wdb[:, isb, OCW:2 * OCW]
                            t.matmul(psD[:, it % 2, 0:OCW], hh_, wh, start=(isb == 0),
                                     stop=False)
                            t.matmul(psD[:, it % 2, 0:OCW], hh_, wl, start=False,
                                     stop=False)
                            last = t.matmul(psD[:, it % 2, 0:OCW], hl_, wh, start=False,
                                            stop=(isb == NIS - 1))
                        ctx.inc(last, "tMM")

            @blk.vector
            def _(v):
                for oc in range(NOCH):
                    for tt in range(NTO):
                        it = oc * NTO + tt
                        v.wait_ge(ctx.sem["tMM"], tb_ + it + 1)
                        xp_s.wait(v, it)
                        if it >= 2:
                            out_s.wait(v, it - 2)
                        ctx.inc(v.tensor_tensor(yob[:, it % 2], psD[:, it % 2, 0:OCW],
                                                xpb[:, it % 2], OP.add), "vMain")

        with nc.Block() as blk:
            @blk.sync
            def _(s):
                for d in (wd_s, h_s, xp_s, out_s):
                    d.wait_all(s)

    if "d2" in phases:
        emit_d2()

    with nc.Block() as blk:
        @blk.sync
        def _(s):
            for nm_ in ctx.sem:
                if nm_.startswith(("q", "g")) or nm_ == "dS":
                    ctx.wait(s, nm_)

    return nc, ctx



def _shard_inputs(inputs, cfg):
    D, H, T, TO, QW = (cfg[k] for k in ("D", "H", "T", "TO", "QW"))
    NSLOT = TO // QW
    ZONE = 2 * QW // P
    x = np.ascontiguousarray(np.asarray(inputs["L0_attn_x"], dtype=np.float32))
    B = x.shape[0]
    shared = dict(
        wq=np.ascontiguousarray(np.asarray(inputs["L0_attn_wq"], np.float32)),
        wk=np.ascontiguousarray(np.asarray(inputs["L0_attn_wk"], np.float32)),
        wv=np.ascontiguousarray(np.asarray(inputs["L0_attn_wv"], np.float32)),
        wo=np.ascontiguousarray(np.asarray(inputs["L0_attn_wo"], np.float32)),
        gate_w=np.ascontiguousarray(np.asarray(inputs["L0_ffn_gate_w"], np.float32)),
        up_w=np.ascontiguousarray(np.asarray(inputs["L0_ffn_up_w"], np.float32)),
        down_w=np.ascontiguousarray(np.asarray(inputs["L0_ffn_down_w"], np.float32)),
    )
    in_maps = []
    for c in range(2 * B):
        b, c2 = divmod(c, 2)
        xb = x[b]
        x_own = np.concatenate([xb[(2 * s + c2) * QW:(2 * s + c2 + 1) * QW]
                                for s in range(NSLOT)], 0)
        m = np.zeros((NSLOT, ZONE, P, QW), np.float32)
        for s in range(NSLOT):
            Es = (2 * s + 2) * QW
            qs = (2 * s + c2) * QW
            for z in range(ZONE):
                kt = Es // P - ZONE + z
                kglob = kt * P + np.arange(P)[:, None]
                qglob = qs + np.arange(QW)[None, :]
                m[s, z] = np.where(kglob <= qglob, 0.0, -240.0)
        in_maps.append(dict(shared,
                            x_batch=np.ascontiguousarray(xb),
                            x_own=np.ascontiguousarray(x_own),
                            cmask=m.astype(ml_dtypes.float8_e4m3)))
    return in_maps


def _assemble_output(results, cfg, B):
    D, TO, QW, T = cfg["D"], cfg["TO"], cfg["QW"], cfg["T"]
    NSLOT = TO // QW
    out = np.empty((B, T, D), np.float32)
    for c, r in enumerate(results):
        b, c2 = divmod(c, 2)
        yo = np.asarray(r["y_out"]).reshape(TO, D)
        for s in range(NSLOT):
            out[b, (2 * s + c2) * QW:(2 * s + c2 + 1) * QW] = \
                yo[s * QW:(s + 1) * QW]
    return out


def kernel(**inputs):
    cfg = full_cfg()
    x = np.asarray(inputs["L0_attn_x"])
    B = x.shape[0]
    nc, _ = build_program(cfg)
    in_maps = _shard_inputs(inputs, cfg)
    res = run_bass_kernel_spmd(nc, in_maps, core_ids=list(range(len(in_maps))))
    return _assemble_output(res.results, cfg, B)


# version 17
# speedup vs baseline: 1.1123x; 1.0837x over previous
import math
import numpy as np
import ml_dtypes
import concourse.bass as bass
import concourse.mybir as mybir
from concourse.bass_utils import run_bass_kernel_spmd
from concourse.alu_op_type import AluOpType
from bass_rust import ReduceOp
from concourse import library_config

F32 = mybir.dt.float32
BF16 = mybir.dt.bfloat16
FP8 = mybir.dt.float8e4
AF = mybir.ActivationFunctionType
AX = mybir.AxisListType
OP = AluOpType
P = 128


def full_cfg():
    return dict(D=2048, H=16, T=4096, TO=2048, QW=512, DFF=8192)


class Ctx:
    def __init__(self, nc):
        self.nc = nc
        self.sem = {}
        self.cnt = {}

    def add(self, name):
        self.sem[name] = self.nc.alloc_semaphore(name=name)
        self.cnt[name] = 0

    def inc(self, instr, name, n=1):
        instr.then_inc(self.sem[name], n)
        self.cnt[name] += n
        return self.cnt[name]

    def dma(self, eng, out, in_, name):
        eng.dma_start(out=out, in_=in_).then_inc(self.sem[name], 16)
        self.cnt[name] += 16
        return self.cnt[name]

    def wait(self, eng, name, target=None):
        eng.wait_ge(self.sem[name], self.cnt[name] if target is None else target)


class DBuf:

    def __init__(self, ctx, semnames):
        self.ctx = ctx
        self.names = list(semnames)
        self.n = len(self.names)
        self.tgt = {}

    def issue(self, eng, i, pairs):
        nm = self.names[i % self.n]
        for out, in_ in pairs:
            self.tgt[i] = self.ctx.dma(eng, out, in_, nm)
        return self.tgt[i]

    def wait(self, eng, i):
        eng.wait_ge(self.ctx.sem[self.names[i % self.n]], self.tgt[i])

    def wait_all(self, eng):
        for nm in self.names:
            self.ctx.wait(eng, nm)


def bcast_ap(t, inner):
    a = t if isinstance(t, bass.AP) else t.ap()
    return bass.AP(a.tensor, a.offset, [[a.ap[0][0], a.ap[0][1]]] +
                   [list(x) for x in a.ap[1:]] + [[0, inner]])


def build_program(cfg, phases=("w1", "w2", "a", "b", "d1", "d2")):
    D, H, T, TO, QW, DFF = (cfg[k] for k in ("D", "H", "T", "TO", "QW", "DFF"))
    FC = D // P
    NTT = T // P
    NTO = TO // P
    NSLOT = TO // QW
    ZONE = 2 * QW // P
    NI = DFF // 512
    NIS = DFF // P
    OCW = min(D, 512)
    NOCH = D // OCW
    QT = QW // P
    NDG = max(D // 512, 1)
    DG = min(D, 512)
    NCH = T // (4 * P)
    NCHO = TO // (4 * P)
    NG = D // 32
    NGI = 512 // 32
    assert D % P == 0 and H * P == D and QW % P == 0 and QW <= 512

    nc = bass.Bass("TRN2", target_bir_lowering=False)

    x_batch = nc.dram_tensor("x_batch", [T, D], F32, kind="ExternalInput").ap()
    x_own = nc.dram_tensor("x_own", [TO, D], F32, kind="ExternalInput").ap()
    w_in = {n: nc.dram_tensor("w" + n, [D, D], F32, kind="ExternalInput").ap()
            for n in "qkvo"}
    gate_w = nc.dram_tensor("gate_w", [DFF, D], F32, kind="ExternalInput").ap()
    up_w = nc.dram_tensor("up_w", [DFF, D], F32, kind="ExternalInput").ap()
    down_w = nc.dram_tensor("down_w", [D, DFF], F32, kind="ExternalInput").ap()
    cmask = nc.dram_tensor("cmask", [NSLOT, ZONE, P, QW], FP8,
                           kind="ExternalInput").ap()
    y_out = nc.dram_tensor("y_out", [TO, D], F32, kind="ExternalOutput").ap()

    s_wT = {n: nc.dram_tensor(f"s_w{n}T", [FC, P, D], BF16, kind="Internal").ap()
            for n in "qkvo"}
    s_kh = nc.dram_tensor("s_kh", [H, P, T], BF16, kind="Internal").ap()
    s_kl = nc.dram_tensor("s_kl", [H, P, T], BF16, kind="Internal").ap()
    s_v = nc.dram_tensor("s_v", [H, NTT, P, 2 * P], BF16, kind="Internal").ap()
    s_qh = nc.dram_tensor("s_qh", [H, P, TO], BF16, kind="Internal").ap()
    s_ql = nc.dram_tensor("s_ql", [H, P, TO], BF16, kind="Internal").ap()
    s_xp = nc.dram_tensor("s_xp", [NTO, P, D], F32, kind="Internal").ap()
    s_ax = nc.dram_tensor("s_ax", [FC, NTO, P, 2 * P], BF16, kind="Internal").ap()
    s_wg = nc.dram_tensor("s_wg", [FC, NI, P, 1024], BF16, kind="Internal").ap()
    s_wu = nc.dram_tensor("s_wu", [FC, NI, P, 1024], BF16, kind="Internal").ap()
    s_wd = nc.dram_tensor("s_wd", [NIS, NOCH, P, 2 * OCW], BF16, kind="Internal").ap()
    s_h = nc.dram_tensor("s_h", [NIS, NTO, P, 2 * P], BF16, kind="Internal").ap()
    s_fma = nc.dram_tensor("s_fma", [T // 512, 2, FC, P, 512], BF16,
                           kind="Internal").ap()
    s_fmq = nc.dram_tensor("s_fmq", [TO // 512, 2, FC, P, 512], BF16,
                           kind="Internal").ap()

    ctx = Ctx(nc)
    for s in ([f"q{i}" for i in range(6)] + [f"g{i}" for i in range(6)] +
              ["dS", "tMM", "tTR", "vMain", "vEvac", "vAux", "aMain", "gMain"]):
        ctx.add(s)

    ident = nc.sbuf_tensor("ident", [P, P], BF16).__enter__()
    ones_bf = nc.sbuf_tensor("ones_bf", [P, 1], BF16).__enter__()
    gam = {n: nc.sbuf_tensor(f"gam_{n}", [P, 1], F32).__enter__() for n in "qkvo"}
    invg = {n: nc.sbuf_tensor(f"invg_{n}", [P, 1], F32).__enter__() for n in "qkvo"}
    _CONSTS = {"neg_half": -0.5, "neg40": -40.0, "eps": 1e-5}
    cpool = nc.sbuf_tensor("cpool", [P, len(_CONSTS)], F32).__enter__()
    CP = {k: cpool[:, i:i + 1] for i, k in enumerate(_CONSTS)}
    onesf_c = nc.sbuf_tensor("onesf_c", [P, 1], F32).__enter__()
    onesf_r = nc.sbuf_tensor("onesf_r", [1, P], F32).__enter__()
    grow = nc.sbuf_tensor("grow", [1, 2], F32).__enter__()
    masks_cm = nc.sbuf_tensor("masks_sb", [P, NSLOT * ZONE * QW], FP8)
    masks_sb = masks_cm.__enter__()

    with nc.Block() as blk:
        @blk.vector
        def _(v):
            v.memset(ident[:], 0.0)
            v.memset(ones_bf[:], 1.0)
            v.memset(onesf_c[:], 1.0)
            v.memset(onesf_r[:], 1.0)
            for i, val in enumerate(_CONSTS.values()):
                v.memset(cpool[:, i:i + 1], val)

        @blk.sync
        def _(s):
            for sl in range(NSLOT):
                for z in range(ZONE):
                    ctx.dma(s, masks_sb[:, (sl * ZONE + z) * QW:
                                        (sl * ZONE + z + 1) * QW], cmask[sl, z], "dS")
            ctx.wait(s, "dS")

    with nc.Block() as blk:
        @blk.gpsimd
        def _(g):
            g.affine_select(out=ident[:], in_=ident[:], compare_op=OP.not_equal,
                            fill=1.0, base=0, pattern=[[-1, P]], channel_multiplier=1)

    def emit_bitlinear_prep(w_ap, wT_ap, gam_ap, invg_ap, tag, extra_scale=None):
        ntile = D // P
        nsub = D // P
        with (
            nc.sbuf_tensor(f"w1_wbuf_{tag}", [P, 2, D], F32) as wbuf,
            nc.sbuf_tensor(f"w1_abuf_{tag}", [P, D], F32) as abuf,
            nc.sbuf_tensor(f"w1_bbuf_{tag}", [P, D], F32) as bbuf,
            nc.sbuf_tensor(f"w1_part_{tag}", [P, ntile], F32) as part,
            nc.sbuf_tensor(f"w1_col_{tag}", [P, 1], F32) as col,
        ):
            lds = DBuf(ctx, ["q0", "q1"])
            vb = ctx.cnt["vMain"]
            with nc.Block() as blk:
                @blk.sync
                def _(s):
                    for i in range(ntile):
                        if i >= 2:
                            s.wait_ge(ctx.sem["vMain"], vb + i - 1)
                        lds.issue(s, i, [(wbuf[:, i % 2], w_ap[i * P:(i + 1) * P, :])])

                @blk.vector
                def _(v):
                    for i in range(ntile):
                        lds.wait(v, i)
                        ctx.inc(v.tensor_reduce(part[:, i:i + 1], wbuf[:, i % 2],
                                                AX.X, OP.add,
                                                apply_absolute_value=True), "vMain")
            with nc.Block() as blk:
                @blk.vector
                def _(v):
                    v.tensor_reduce(col[:], part[:], AX.X, OP.add)
            with nc.psum_tensor(f"w1_psg_{tag}", [P, 512], F32) as psg:
                with nc.Block() as blk:
                    @blk.tensor
                    def _(t):
                        t.matmul(psg[0:1, 0:1], col[:], onesf_c[:], start=True,
                                 stop=True)
                with nc.Block() as blk:
                    @blk.vector
                    def _(v):
                        v.tensor_scalar(grow[0:1, 0:1], psg[0:1, 0:1],
                                        1.0 / (D * D), 1e-5, OP.mult, OP.max)
                        v.drain()
                        v.reciprocal(grow[0:1, 1:2], grow[0:1, 0:1])
                        if extra_scale is not None:
                            v.drain()
                            v.tensor_scalar(grow[0:1, 0:1], grow[0:1, 0:1],
                                            extra_scale, None, OP.mult)
                with nc.Block() as blk:
                    @blk.tensor
                    def _(t):
                        t.matmul(psg[:, 0:2], onesf_r[:], grow[:], start=True,
                                 stop=True)
                with nc.Block() as blk:
                    @blk.vector
                    def _(v):
                        v.tensor_copy(gam_ap[:], psg[:, 0:1])
                        v.tensor_copy(invg_ap[:], psg[:, 1:2])

            lds2 = DBuf(ctx, ["q0", "q1"])
            spl = DBuf(ctx, ["g0", "g1", "g2", "g3"])
            ab = ctx.cnt["aMain"]
            vqb = ctx.cnt["vMain"]
            veb = ctx.cnt["vEvac"]
            tb_ = ctx.cnt["tTR"]
            with (
                nc.sbuf_tensor(f"w1_sgn_{tag}", [P, 2, D], BF16) as sgn,
                nc.sbuf_tensor(f"w1_rl_{tag}", [P, 2, D], BF16) as rl,
                nc.sbuf_tensor(f"w1_q_{tag}", [P, 2, D], BF16) as qb,
                nc.sbuf_tensor(f"w1_tb_{tag}", [P, 4, 512], BF16) as tbuf,
                nc.psum_tensor(f"w1_ps_{tag}", [P, 2, 1024], BF16) as ps,
                nc.Block() as blk,
            ):
                @blk.sync
                def _(s):
                    for i in range(ntile):
                        if i >= 2:
                            s.wait_ge(ctx.sem["aMain"], ab + i - 1)
                        lds2.issue(s, i, [(wbuf[:, i % 2], w_ap[i * P:(i + 1) * P, :])])

                @blk.gpsimd
                def _(g):
                    for i in range(ntile):
                        for jg in range(nsub // 4):
                            kg = i * (nsub // 4) + jg
                            g.wait_ge(ctx.sem["vEvac"], veb + kg + 1)
                            dstg = wT_ap[jg * 4:(jg + 1) * 4, :,
                                         i * P:(i + 1) * P].rearrange(
                                             "j p c -> p j c")
                            spl.issue(g, kg, [(dstg, tbuf[:, kg % 4])])
                    spl.wait_all(g)

                @blk.scalar
                def _(sc):
                    for i in range(ntile):
                        lds2.wait(sc, i)
                        if i >= 2:
                            sc.wait_ge(ctx.sem["vMain"], vqb + i - 1)
                        sc.activation(abuf[:], wbuf[:, i % 2], AF.Abs,
                                      scale=invg_ap[:])
                        sc.drain()
                        sc.activation(bbuf[:], abuf[:], AF.Sign, bias=CP["neg_half"])
                        sc.drain()
                        sc.activation(rl[:, i % 2], bbuf[:], AF.Relu)
                        sc.drain()
                        ctx.inc(sc.activation(sgn[:, i % 2], wbuf[:, i % 2], AF.Sign),
                                "aMain")

                @blk.tensor
                def _(t):
                    for i in range(ntile):
                        t.wait_ge(ctx.sem["vMain"], vqb + i + 1)
                        for jg in range(nsub // 4):
                            kg = i * (nsub // 4) + jg
                            if kg >= 2:
                                t.wait_ge(ctx.sem["vEvac"], veb + kg - 1)
                            for jj in range(4):
                                j = jg * 4 + jj
                                ins = t.transpose(
                                    ps[:, kg % 2, jj * P:(jj + 1) * P],
                                    qb[:, i % 2, j * P:(j + 1) * P], ident[:])
                            ctx.inc(ins, "tTR")

                @blk.vector
                def _(v):
                    ng = nsub // 4
                    for i in range(ntile + 1):
                        if i < ntile:
                            v.wait_ge(ctx.sem["aMain"], ab + i + 1)
                            if i >= 2:
                                v.wait_ge(ctx.sem["tTR"], tb_ + (i - 1) * ng)
                            ctx.inc(v.tensor_tensor(qb[:, i % 2], rl[:, i % 2],
                                                    sgn[:, i % 2], OP.mult), "vMain")
                        if i >= 1:
                            for jg in range(ng):
                                kg = (i - 1) * ng + jg
                                v.wait_ge(ctx.sem["tTR"], tb_ + kg + 1)
                                if kg >= 4:
                                    spl.wait(v, kg - 4)
                                ctx.inc(v.tensor_copy(tbuf[:, kg % 4],
                                                      ps[:, kg % 2, 0:512]),
                                        "vEvac")

    if "w1" in phases:
        for tag in "qkvo":
            emit_bitlinear_prep(w_in[tag], s_wT[tag], gam[tag], invg[tag], tag,
                                extra_scale=(1.0 / math.sqrt(P) if tag == "q"
                                             else None))

    def emit_mx_weight_prep(w_ap, nrow, ncol, dst, dst_kind, tag):
        ntile = nrow // P
        nsub = ncol // P
        ngrp = ncol // 32
        pair = dst_kind == "gu"
        ndq = 3 if pair else 1
        nhl = 3 if pair else 2
        nwb = 4 if pair else 2
        lds = DBuf(ctx, ["q0", "q1"])
        spl = DBuf(ctx, ["q2", "q3", "q4", "q5"])
        ab = ctx.cnt["aMain"]
        vqb = ctx.cnt["vMain"]
        veb = ctx.cnt["vEvac"]
        tb_ = ctx.cnt["tTR"]
        gb = ctx.cnt["gMain"]
        with (
            nc.sbuf_tensor(f"w2_wbuf_{tag}", [P, nwb, ncol], F32) as wbuf,
            nc.sbuf_tensor(f"w2_ma_{tag}", [P, 2 if pair else 1, 2, ngrp],
                           F32) as ma,
            nc.sbuf_tensor(f"w2_r2_{tag}", [P, 2 if pair else 1, 2, ngrp],
                           F32) as r2,
            nc.sbuf_tensor(f"w2_q8_{tag}", [P, 2 if pair else 1, ncol],
                           FP8) as q8,
            nc.sbuf_tensor(f"w2_dq_{tag}", [P, ndq, ncol], F32) as dq,
            nc.sbuf_tensor(f"w2_hi_{tag}", [P, nhl, ncol], BF16) as hib,
            nc.sbuf_tensor(f"w2_lo_{tag}", [P, nhl, ncol], BF16) as lob,
            nc.sbuf_tensor(f"w2_tb_{tag}", [P, 4, 1024], BF16) as tbuf,
            nc.psum_tensor(f"w2_ps_{tag}", [P, 2, 1024], BF16) as ps,
            nc.Block() as blk,
        ):
            gsz = max(g for g in (8, 4, 2, 1) if nsub % g == 0)
            ng = nsub // gsz

            def dst_group(rt, jg, piece):
                if dst_kind == "gu":
                    ic, pos = (rt * P) // 512, rt % 4
                    off = (0 if piece == 0 else 512) + pos * P
                    d = dst[jg * gsz:(jg + 1) * gsz, ic, :, off:off + P]
                else:
                    och, pos = (rt * P) // OCW, rt % (OCW // P)
                    off = (0 if piece == 0 else OCW) + pos * P
                    d = dst[jg * gsz:(jg + 1) * gsz, och, :, off:off + P]
                return d.rearrange("j p c -> p j c")

            @blk.sync
            def _(s):
                for i in range(ntile):
                    if i >= 2:
                        lds.wait(s, i - 2)
                    if i >= nwb:
                        s.wait_ge(ctx.sem["vMain"], vqb + i - nwb + 1)
                    lds.issue(s, i, [(wbuf[:, i % nwb],
                                      w_ap[i * P:(i + 1) * P, :])])

            @blk.scalar
            def _(sc):
                def evacs(i):
                    for piece in range(2):
                        for jg in range(ng):
                            kg = (i * 2 + piece) * ng + jg
                            sc.wait_ge(ctx.sem["tTR"], tb_ + kg + 1)
                            if kg >= 4:
                                spl.wait(sc, kg - 4)
                            ctx.inc(sc.activation(tbuf[:, kg % 4, 0:gsz * P],
                                                  ps[:, kg % 2, 0:gsz * P],
                                                  AF.Identity), "vEvac")
                            if kg >= 1:
                                sc.wait_ge(ctx.sem["vEvac"], veb + kg)
                                ig, r = divmod(kg - 1, 2 * ng)
                                spl.issue(sc, kg - 1,
                                          [(dst_group(ig, r % ng, r // ng),
                                            tbuf[:, (kg - 1) % 4, 0:gsz * P])])
                for i in range(ntile):
                    sc.wait_ge(ctx.sem["vMain"], vqb + i + 1)
                    if i >= nhl:
                        sc.wait_ge(ctx.sem["tTR"],
                                   tb_ + 2 * ng * (i - nhl + 1))
                    ctx.inc(sc.activation(hib[:, i % nhl], dq[:, i % ndq],
                                          AF.Identity), "aMain")
                    if i >= 1:
                        evacs(i - 1)
                evacs(ntile - 1)
                kg = 2 * ng * ntile - 1
                sc.wait_ge(ctx.sem["vEvac"], veb + kg + 1)
                ig, r = divmod(kg, 2 * ng)
                spl.issue(sc, kg, [(dst_group(ig, r % ng, r // ng),
                                    tbuf[:, kg % 4, 0:gsz * P])])
                spl.wait_all(sc)

            @blk.tensor
            def _(t):
                for i in range(ntile):
                    t.wait_ge(ctx.sem["aMain"], ab + i + 1)
                    for piece, srcb in ((0, hib), (1, lob)):
                        if piece == 1:
                            t.wait_ge(ctx.sem["gMain"], gb + i + 1)
                        for jg in range(ng):
                            kg = (i * 2 + piece) * ng + jg
                            if kg >= 2:
                                t.wait_ge(ctx.sem["vEvac"], veb + kg - 1)
                            for jj in range(gsz):
                                j = jg * gsz + jj
                                ins = t.transpose(
                                    ps[:, kg % 2, jj * P:(jj + 1) * P],
                                    srcb[:, i % nhl, j * P:(j + 1) * P],
                                    ident[:])
                            ctx.inc(ins, "tTR")

            @blk.gpsimd
            def _(g):
                for i in range(ntile):
                    g.wait_ge(ctx.sem["vMain"], vqb + i + 1)
                    g.wait_ge(ctx.sem["aMain"], ab + i + 1)
                    if i >= nhl:
                        g.wait_ge(ctx.sem["tTR"],
                                  tb_ + 2 * ng * (i - nhl + 1))
                    ctx.inc(g.tensor_tensor(lob[:, i % nhl], dq[:, i % ndq],
                                            hib[:, i % nhl], OP.subtract),
                            "gMain")

            @blk.vector
            def _(v):
                step = 2 if pair else 1
                for i0 in range(0, ntile, step):
                    tiles = list(range(i0, min(i0 + step, ntile)))
                    pj = (i0 // 2) % 2

                    def mslot(i):
                        return (pj, i % 2) if pair else (0, i % 2)

                    for i in tiles:
                        lds.wait(v, i)
                        w3 = wbuf[:, i % nwb].rearrange("p (g b) -> p g b", b=32)
                        v.tensor_reduce(ma[:, mslot(i)[0], mslot(i)[1]], w3,
                                        AX.X, OP.max, apply_absolute_value=True)
                    v.drain()
                    if pair and len(tiles) == 2:
                        mboth = ma[:, pj].rearrange("p s g -> p (s g)")
                        rboth = r2[:, pj].rearrange("p s g -> p (s g)")
                        v.tensor_scalar(mboth, mboth, 2.0 / 448.0, 2e-30,
                                        OP.mult, OP.max)
                        v.drain()
                        v.reciprocal(rboth, mboth)
                    else:
                        i = tiles[0]
                        sl = mslot(i)
                        v.tensor_scalar(ma[:, sl[0], sl[1]], ma[:, sl[0], sl[1]],
                                        2.0 / 448.0, 2e-30, OP.mult, OP.max)
                        v.drain()
                        v.reciprocal(r2[:, sl[0], sl[1]], ma[:, sl[0], sl[1]])
                    v.drain()
                    for i in tiles:
                        w3 = wbuf[:, i % nwb].rearrange("p (g b) -> p g b", b=32)
                        sl = mslot(i)
                        v.tensor_tensor(
                            q8[:, i % 2 if pair else 0].rearrange(
                                "p (g b) -> p g b", b=32), w3,
                            bcast_ap(r2[:, sl[0], sl[1]], 32), OP.mult)
                    v.drain()
                    for i in tiles:
                        if i >= ndq:
                            v.wait_ge(ctx.sem["aMain"], ab + i - ndq + 1)
                            v.wait_ge(ctx.sem["gMain"], gb + i - ndq + 1)
                        sl = mslot(i)
                        ctx.inc(v.tensor_tensor(
                            dq[:, i % ndq].rearrange("p (g b) -> p g b", b=32),
                            q8[:, i % 2 if pair else 0].rearrange(
                                "p (g b) -> p g b", b=32),
                            bcast_ap(ma[:, sl[0], sl[1]], 32), OP.mult), "vMain")

    if "w2" in phases:
        emit_mx_weight_prep(gate_w, DFF, D, s_wg, "gu", "g")
        emit_mx_weight_prep(up_w, DFF, D, s_wu, "gu", "u")
        emit_mx_weight_prep(down_w, D, DFF, s_wd, "d", "d")

    def spidx(qc, kk, np_):
        return qc * NDG * np_ + kk

    def emit_ln_spill(x_ap, nchunk, fm_dst, tagp):
        NT4 = 4
        NGRP = D // 512
        ngf = FC // 4
        NTILE = nchunk * NT4
        lds = DBuf(ctx, ["q0", "q1"])
        spl = DBuf(ctx, ["q2", "q3"])
        ab = ctx.cnt["aMain"]
        vb = ctx.cnt["vMain"]
        xb = ctx.cnt["vAux"]
        veb = ctx.cnt["vEvac"]
        tb_ = ctx.cnt["tTR"]
        gb = ctx.cnt["gMain"]
        order = [("a", 0), ("a", 1)] + \
            [x for t in range(NT4 - 2) for x in (("b", t), ("a", t + 2))] + \
            [("b", NT4 - 2), ("b", NT4 - 1)]
        act_order = [("s", 0), ("s", 1)] + \
            [x for t in range(NT4 - 2) for x in (("x", t), ("s", t + 2))] + \
            [("x", NT4 - 2), ("x", NT4 - 1)]
        pos_nm, pos_lo = {}, {}
        for qc in range(nchunk):
            for n_, (kind_, t_) in enumerate(order):
                (pos_nm if kind_ == "a" else pos_lo)[qc * NT4 + t_] = \
                    vb + qc * 8 + n_ + 1
        pos_sq, pos_x = {}, {}
        for qc in range(nchunk):
            for n_, (kind_, t_) in enumerate(act_order):
                (pos_sq if kind_ == "s" else pos_x)[qc * NT4 + t_] = \
                    ab + qc * 8 + n_ + 1
        pos_ag = {gi: xb + gi // NT4 * 4 + gi % NT4 + 1 for gi in range(NTILE)}
        pos_xh = {gi: gb + gi + 1 for gi in range(NTILE)}

        with (
            nc.sbuf_tensor(f"a_x_{tagp}", [P, 2, D], F32) as xbuf,
            nc.sbuf_tensor(f"a_stat_{tagp}", [P, 4, NGRP * 6], F32) as bns,
            nc.sbuf_tensor(f"a_mv_{tagp}", [P, 4, 2], F32) as mv,
            nc.sbuf_tensor(f"a_rv_{tagp}", [P, 4], F32) as rv,
            nc.sbuf_tensor(f"a_sq_{tagp}", [P, 4], F32) as sq,
            nc.sbuf_tensor(f"a_nm_{tagp}", [P, 4], F32) as nm,
            nc.sbuf_tensor(f"a_xln_{tagp}", [P, 2, D], F32) as xlnb,
            nc.sbuf_tensor(f"a_xh_{tagp}", [P, 3, D], BF16) as xh,
            nc.sbuf_tensor(f"a_xl_{tagp}", [P, 3, D], BF16) as xl,
            nc.sbuf_tensor(f"a_fmh_{tagp}", [P, 2, FC, 4 * P], BF16) as fmh,
            nc.sbuf_tensor(f"a_fml_{tagp}", [P, 2, FC, 4 * P], BF16) as fml,
            nc.psum_tensor(f"a_pst_{tagp}", [P, 4, 1024], BF16) as pst,
            nc.Block() as blk,
        ):
            @blk.sync
            def _(s):
                def spills(qc):
                    for piece, buf in ((0, fmh), (1, fml)):
                        idx = qc * 2 + piece
                        if idx >= 2:
                            spl.wait(s, idx - 2)
                        s.wait_ge(ctx.sem["vEvac"],
                                  veb + (qc + 1) * NT4 * 2 * ngf)
                        spl.issue(s, idx,
                                  [(fm_dst[qc, piece].rearrange(
                                      "f p c -> p f c"), buf[:, qc % 2])])
                for qc in range(nchunk):
                    for t4 in range(NT4):
                        gi = qc * NT4 + t4
                        if gi >= 2:
                            lds.wait(s, gi - 2)
                            s.wait_ge(ctx.sem["aMain"], pos_x[gi - 2])
                        lds.issue(s, gi, [(xbuf[:, gi % 2],
                                           x_ap[gi * P:(gi + 1) * P, :])])
                    if qc >= 1:
                        spills(qc - 1)
                spills(nchunk - 1)

            @blk.scalar
            def _(sc):
                for qc in range(nchunk):
                    for kind, t4 in act_order:
                        gi = qc * NT4 + t4
                        if kind == "s":
                            sc.wait_ge(ctx.sem["vAux"], pos_ag[gi])
                            if qc >= 1:
                                sc.wait_ge(ctx.sem["vMain"],
                                           pos_nm[gi - NT4])
                            ctx.inc(sc.activation(sq[:, t4:t4 + 1],
                                                  mv[:, t4, 1:2], AF.Sqrt,
                                                  bias=CP["eps"]), "aMain")
                        else:
                            sc.wait_ge(ctx.sem["vMain"], pos_nm[gi])
                            if gi >= 2:
                                sc.wait_ge(ctx.sem["vMain"], pos_lo[gi - 2])
                                sc.wait_ge(ctx.sem["gMain"], pos_xh[gi - 2])
                            ctx.inc(sc.activation(xlnb[:, gi % 2],
                                                  xbuf[:, gi % 2],
                                                  AF.Identity,
                                                  bias=nm[:, t4:t4 + 1],
                                                  scale=rv[:, t4:t4 + 1]),
                                    "aMain")
                            if qc >= 2 and t4 == 0:
                                spl.wait(sc, (qc - 2) * 2)
                                spl.wait(sc, (qc - 2) * 2 + 1)
                            for piece in range(2):
                                for jg in range(ngf):
                                    kg = gi * 2 * ngf + piece * ngf + jg
                                    sc.wait_ge(ctx.sem["tTR"], tb_ + kg + 1)
                                    dstp = fmh if piece == 0 else fml
                                    ctx.inc(sc.activation(
                                        dstp[:, qc % 2, jg * 4:(jg + 1) * 4,
                                             t4 * P:(t4 + 1) * P],
                                        pst[:, kg % 4, 0:512],
                                        AF.Identity), "vEvac")

            @blk.gpsimd
            def _(g):
                for gi in range(NTILE):
                    g.wait_ge(ctx.sem["aMain"], pos_x[gi])
                    if gi >= 3:
                        g.wait_ge(ctx.sem["tTR"], tb_ + (gi - 2) * 2 * ngf)
                    ctx.inc(g.tensor_copy(xh[:, gi % 3], xlnb[:, gi % 2]),
                            "gMain")

            @blk.tensor
            def _(t):
                for gi in range(NTILE):
                    qc, t4 = divmod(gi, NT4)
                    t.wait_ge(ctx.sem["gMain"], pos_xh[gi])
                    for piece, srcb in ((0, xh), (1, xl)):
                        if piece == 1:
                            t.wait_ge(ctx.sem["vMain"], pos_lo[gi])
                        for jg in range(ngf):
                            kg = gi * 2 * ngf + piece * ngf + jg
                            if kg >= 4:
                                t.wait_ge(ctx.sem["vEvac"], veb + kg - 3)
                            for jj in range(4):
                                j = jg * 4 + jj
                                ins = t.transpose(
                                    pst[:, kg % 4, jj * P:(jj + 1) * P],
                                    srcb[:, gi % 3, j * P:(j + 1) * P],
                                    ident[:])
                            ctx.inc(ins, "tTR")

            @blk.vector
            def _(v):
                for qc in range(nchunk):
                    for kind, t4 in order:
                        gi = qc * NT4 + t4
                        if kind == "a":
                            lds.wait(v, gi)
                            if qc >= 1:
                                v.wait_ge(ctx.sem["aMain"], pos_x[gi - NT4])
                            for gr in range(NGRP):
                                v.bn_stats(bns[:, t4, gr * 6:gr * 6 + 6],
                                           xbuf[:, gi % 2,
                                                gr * 512:(gr + 1) * 512])
                            v.drain()
                            ctx.inc(v.bn_aggr(mv[:, t4], bns[:, t4]), "vAux")
                            v.wait_ge(ctx.sem["aMain"], pos_sq[gi])
                            v.reciprocal(rv[:, t4:t4 + 1], sq[:, t4:t4 + 1])
                            v.drain()
                            ctx.inc(v.scalar_tensor_tensor(
                                nm[:, t4:t4 + 1], mv[:, t4, 0:1], -1.0,
                                rv[:, t4:t4 + 1], OP.mult, OP.mult), "vMain")
                        else:
                            v.wait_ge(ctx.sem["gMain"], pos_xh[gi])
                            if gi >= 3:
                                v.wait_ge(ctx.sem["tTR"],
                                          tb_ + (gi - 2) * 2 * ngf)
                            ctx.inc(v.tensor_tensor(xl[:, gi % 3],
                                                    xlnb[:, gi % 2],
                                                    xh[:, gi % 3],
                                                    OP.subtract), "vMain")

        with nc.Block() as blk:
            @blk.sync
            def _(s):
                for d in (lds, spl):
                    d.wait_all(s)

    def emit_proj_all(nchunk, mode, fm_src, tagp):
        np_ = 8 if mode == "kv" else 4
        lds = DBuf(ctx, ["q0", "q1"])
        wk_s = DBuf(ctx, ["q2", "q3"])
        wv_s = DBuf(ctx, ["q4", "q5"])
        spk = DBuf(ctx, ["g0", "g1"])
        spv = DBuf(ctx, ["g2", "g3"])
        tb2 = ctx.cnt["tMM"]
        ab2 = ctx.cnt["aMain"]
        vb2 = ctx.cnt["vMain"]
        CU = NDG * np_
        with (
            nc.sbuf_tensor(f"a_fmh_{tagp}", [P, 2, FC, 4 * P], BF16) as fmh,
            nc.sbuf_tensor(f"a_fml_{tagp}", [P, 2, FC, 4 * P], BF16) as fml,
            nc.sbuf_tensor(f"a_wkb_{tagp}", [P, 2, FC, DG], BF16) as wkb,
            nc.sbuf_tensor(f"a_wvb_{tagp}", [P, 2, FC, DG], BF16) as wvb,
            nc.sbuf_tensor(f"a_keh_{tagp}", [P, 2, DG], BF16) as keh,
            nc.sbuf_tensor(f"a_kel_{tagp}", [P, 2, DG], BF16) as kel,
            nc.psum_tensor(f"a_psk_{tagp}", [P, 2, DG], F32) as psk,
            nc.Block() as blk,
        ):
            @blk.sync
            def _(s):
                for qc in range(nchunk):
                    if qc >= 2:
                        lds.wait(s, qc - 2)
                        s.wait_ge(ctx.sem["tMM"], tb2 + (qc - 1) * CU)
                    lds.issue(s, qc, [(fmh[:, qc % 2],
                                       fm_src[qc, 0].rearrange(
                                           "f p c -> p f c")),
                                      (fml[:, qc % 2],
                                       fm_src[qc, 1].rearrange(
                                           "f p c -> p f c"))])
                    for dg in range(NDG):
                        i = qc * NDG + dg
                        if i >= 2:
                            s.wait_ge(ctx.sem["tMM"], tb2 + np_ * (i - 1))
                        wk_s.issue(s, i,
                                   [(wkb[:, i % 2, f],
                                     s_wT["k" if mode == "kv" else "q"]
                                     [f, :, dg * DG:(dg + 1) * DG])
                                    for f in range(FC)])
                        if mode == "kv":
                            wv_s.issue(s, i,
                                       [(wvb[:, i % 2, f],
                                         s_wT["v"][f, :, dg * DG:(dg + 1) * DG])
                                        for f in range(FC)])

            @blk.gpsimd
            def _(g):
                for qc in range(nchunk):
                    for dg in range(NDG):
                        for u in range(np_):
                            kk = dg * np_ + u
                            K = qc * CU + kk
                            g.wait_ge(ctx.sem["vMain"], vb2 + K + 1)
                            if mode == "q":
                                h = dg * 4 + u
                                spk.issue(g, spidx(qc, kk, np_),
                                          [(s_qh[h, :, qc * 512:(qc + 1) * 512],
                                            keh[:, K % 2]),
                                           (s_ql[h, :, qc * 512:(qc + 1) * 512],
                                            kel[:, K % 2])])
                            elif u < 4:
                                h = dg * 4 + u
                                spk.issue(g, spidx(qc, kk, np_),
                                          [(s_kh[h, :, qc * 512:(qc + 1) * 512],
                                            keh[:, K % 2]),
                                           (s_kl[h, :, qc * 512:(qc + 1) * 512],
                                            kel[:, K % 2])])
                            else:
                                tt4 = u - 4
                                ttg = qc * 4 + tt4
                                pairs = []
                                for hh in range(4):
                                    h = dg * 4 + hh
                                    pairs.append(
                                        (s_v[h, ttg, :, 0:P],
                                         keh[:, K % 2, hh * P:(hh + 1) * P]))
                                    pairs.append(
                                        (s_v[h, ttg, :, P:2 * P],
                                         kel[:, K % 2, hh * P:(hh + 1) * P]))
                                spv.issue(g, spidx(qc, kk, np_), pairs)

            def spill_wait(eng, qc, kk):
                gk = qc * CU + kk
                if gk < 2:
                    return
                kkp, qcp = divmod(gk - 2, CU)[1], (gk - 2) // CU
                st = spk if (mode == "q" or kkp % np_ < 4) else spv
                st.wait(eng, spidx(qcp, kkp, np_))

            @blk.tensor
            def _(t):
                for qc in range(nchunk):
                    lds.wait(t, qc)
                    for dg in range(NDG):
                        i = qc * NDG + dg
                        wk_s.wait(t, i)
                        if mode == "kv":
                            wv_s.wait(t, i)
                        for u in range(np_):
                            kk = dg * np_ + u
                            K = qc * CU + kk
                            if K >= 2:
                                t.wait_ge(ctx.sem["vMain"], vb2 + K - 1)
                            last = None
                            if mode == "q" or u < 4:
                                dch = u
                                for f in range(FC):
                                    lt = wkb[:, i % 2, f,
                                             dch * P:(dch + 1) * P]
                                    t.matmul(psk[:, K % 2], lt,
                                             fmh[:, qc % 2, f],
                                             start=(f == 0), stop=False)
                                    last = t.matmul(psk[:, K % 2], lt,
                                                    fml[:, qc % 2, f],
                                                    start=False,
                                                    stop=(f == FC - 1))
                            else:
                                tt4 = u - 4
                                for f in range(FC):
                                    t.matmul(psk[:, K % 2],
                                             fmh[:, qc % 2, f,
                                                 tt4 * P:(tt4 + 1) * P],
                                             wvb[:, i % 2, f],
                                             start=(f == 0), stop=False)
                                    last = t.matmul(
                                        psk[:, K % 2],
                                        fml[:, qc % 2, f,
                                            tt4 * P:(tt4 + 1) * P],
                                        wvb[:, i % 2, f], start=False,
                                        stop=(f == FC - 1))
                            ctx.inc(last, "tMM")

            @blk.scalar
            def _(sc):
                for qc in range(nchunk):
                    for kk in range(CU):
                        K = qc * CU + kk
                        sc.wait_ge(ctx.sem["tMM"], tb2 + K + 1)
                        spill_wait(sc, qc, kk)
                        gm = gam[("q" if mode == "q" else
                                  ("k" if kk % np_ < 4 else "v"))]
                        ctx.inc(sc.activation(keh[:, K % 2], psk[:, K % 2],
                                              AF.Copy, scale=gm[:]), "aMain")

            @blk.vector
            def _(v):
                for qc in range(nchunk):
                    for kk in range(CU):
                        K = qc * CU + kk
                        v.wait_ge(ctx.sem["aMain"], ab2 + K + 1)
                        spill_wait(v, qc, kk)
                        gm = gam[("q" if mode == "q" else
                                  ("k" if kk % np_ < 4 else "v"))]
                        ctx.inc(v.scalar_tensor_tensor(
                            kel[:, K % 2], psk[:, K % 2], gm[:],
                            keh[:, K % 2], OP.mult, OP.subtract), "vMain")

        with nc.Block() as blk:
            @blk.sync
            def _(s):
                for d in (lds, wk_s, wv_s, spk, spv):
                    d.wait_all(s)

    if "a" in phases:
        emit_ln_spill(x_batch, NCH, s_fma, "a")
        emit_ln_spill(x_own, NCHO, s_fmq, "a2")
        emit_proj_all(NCH, "kv", s_fma, "pa")
        emit_proj_all(NCHO, "q", s_fmq, "pa2")

    def mmseq(Nkt):
        seq = []
        for kt in range(Nkt + 2):
            if kt < Nkt:
                seq.append(("S", kt))
            if kt >= 2:
                seq.append(("A", kt - 2))
        posS = {kt: seq.index(("S", kt)) + 1 for kt in range(Nkt)}
        posA = {kt: seq.index(("A", kt)) + 1 for kt in range(Nkt)}
        return seq, posS, posA

    def emit_attention(s_idx, avh, avl):
        Es = (2 * s_idx + 2) * QW
        Nkt = Es // P
        zone0 = Nkt - ZONE
        seq, posS, posA = mmseq(Nkt)
        L = len(seq)
        kb = DBuf(ctx, ["q0", "q1"])
        vb_ = DBuf(ctx, ["q2", "q3"])
        qb_ = DBuf(ctx, ["q4", "q5"])
        tS0 = ctx.cnt["tMM"]
        vC0 = ctx.cnt["vEvac"]
        eR0 = ctx.cnt["vAux"]
        aE0 = ctx.cnt["aMain"]
        hD0 = ctx.cnt["vMain"]
        tB0 = ctx.cnt["tTR"]
        zcum = {}
        lastz = {}
        zprev = {}
        zc = vC0
        for h in range(H):
            for kt in range(Nkt):
                kidx = h * Nkt + kt
                if kt >= zone0:
                    zc += 1
                    zcum[kidx] = zc
                    zprev[kidx] = lastz.get(kidx % 3)
                    lastz[kidx % 3] = kidx

        def t_S(h, kt):
            return tS0 + h * L + posS[kt]

        def t_A(h, kt):
            return tS0 + h * L + posA[kt]

        with (
            nc.sbuf_tensor(f"b_k_{s_idx}", [P, 2, 2, Es], BF16) as kbf,
            nc.sbuf_tensor(f"b_v_{s_idx}", [P, 2, (Es // P) * 2 * P], BF16) as vbf,
            nc.sbuf_tensor(f"b_q_{s_idx}", [P, 2, 2, QW], BF16) as qbf,
            nc.sbuf_tensor(f"b_ein_{s_idx}", [P, 3, QW], F32) as ein,
            nc.sbuf_tensor(f"b_ef_{s_idx}", [P, 3, QW], F32) as ef,
            nc.sbuf_tensor(f"b_eh_{s_idx}", [P, 3, QW], BF16) as ehb,
            nc.sbuf_tensor(f"b_el_{s_idx}", [P, 3, QW], BF16) as elb,
            nc.sbuf_tensor(f"b_rc_{s_idx}", [1, QW], F32) as rc,
            nc.sbuf_tensor(f"b_rcb_{s_idx}", [P, QW], F32) as rcb,
            nc.sbuf_tensor(f"b_avt_{s_idx}", [P, QW], F32) as avt,
            nc.psum_tensor(f"b_ps_{s_idx}", [P, 3, 512], F32) as psS,
            nc.psum_tensor(f"b_pav_{s_idx}", [P, 2, 512], F32) as psAV,
            nc.psum_tensor(f"b_pcs_{s_idx}", [P, 2, 512], F32) as psCS,
            nc.psum_tensor(f"b_pb_{s_idx}", [P, 512], F32) as psB,
            nc.Block() as blk,
        ):
            @blk.sync
            def _(s):
                for h in range(H):
                    if h >= 2:
                        s.wait_ge(ctx.sem["tMM"], tS0 + (h - 1) * L)
                    kb.issue(s, h, [(kbf[:, h % 2, 0, :], s_kh[h, :, 0:Es]),
                                    (kbf[:, h % 2, 1, :], s_kl[h, :, 0:Es])])
                    ntt = Es // P
                    half = max(1, ntt // 2)
                    pairs = []
                    for j0 in range(0, ntt, half):
                        j1 = min(ntt, j0 + half)
                        src_v = s_v[h, j0:j1].rearrange("t p c -> p t c")
                        dst_v = vbf[:, h % 2, j0 * 2 * P:j1 * 2 * P].rearrange(
                            "p (t c) -> p t c", c=2 * P)
                        pairs.append((dst_v, src_v))
                    vb_.issue(s, h, pairs)
                    qb_.issue(s, h, [(qbf[:, h % 2, 0, :],
                                      s_qh[h, :, s_idx * QW:(s_idx + 1) * QW]),
                                     (qbf[:, h % 2, 1, :],
                                      s_ql[h, :, s_idx * QW:(s_idx + 1) * QW])])

            @blk.tensor
            def _(t):
                for h in range(H):
                    kb.wait(t, h)
                    vb_.wait(t, h)
                    qb_.wait(t, h)
                    if h >= 2:
                        t.wait_ge(ctx.sem["vMain"], hD0 + 3 * (h - 2) + 2)
                    for kind, kt in seq:
                        kidx = h * Nkt + kt
                        if kind == "S":
                            kprev = kidx - 3
                            if kprev >= 0:
                                hp, ktp = divmod(kprev, Nkt)
                                if ktp >= zone0:
                                    t.wait_ge(ctx.sem["vEvac"], zcum[kprev])
                                else:
                                    t.wait_ge(ctx.sem["aMain"],
                                              aE0 + 2 * kprev + 2)
                            kh = kbf[:, h % 2, 0, kt * P:(kt + 1) * P]
                            kl = kbf[:, h % 2, 1, kt * P:(kt + 1) * P]
                            t.matmul(psS[:, kt % 3, 0:QW], kh, qbf[:, h % 2, 0, :],
                                     start=True, stop=False)
                            t.matmul(psS[:, kt % 3, 0:QW], kh, qbf[:, h % 2, 1, :],
                                     start=False, stop=False)
                            ctx.inc(t.matmul(psS[:, kt % 3, 0:QW], kl,
                                             qbf[:, h % 2, 0, :],
                                             start=False, stop=True), "tMM")
                        else:
                            k0 = kt
                            ki = h * Nkt + k0
                            t.wait_ge(ctx.sem["vAux"], eR0 + ki + 1)
                            vh = vbf[:, h % 2, k0 * 2 * P:k0 * 2 * P + P]
                            vl = vbf[:, h % 2, k0 * 2 * P + P:(k0 + 1) * 2 * P]
                            st = (k0 == 0)
                            sp = (k0 == Nkt - 1)
                            t.matmul(psAV[:, h % 2, 0:QW], vh, ehb[:, ki % 3],
                                     start=st, stop=False)
                            t.matmul(psAV[:, h % 2, 0:QW], vh, elb[:, ki % 3],
                                     start=False, stop=False)
                            t.matmul(psAV[:, h % 2, 0:QW], vl, ehb[:, ki % 3],
                                     start=False, stop=sp)
                            t.matmul(psCS[0:1, h % 2, 0:QW], ones_bf[:], ehb[:, ki % 3],
                                     start=st, stop=False)
                            ctx.inc(t.matmul(psCS[0:1, h % 2, 0:QW], ones_bf[:],
                                             elb[:, ki % 3], start=False, stop=sp),
                                    "tMM")
                    t.wait_ge(ctx.sem["vMain"], hD0 + 3 * h + 1)
                    ctx.inc(t.matmul(psB[:, 0:QW], onesf_r[:], rc[:], start=True,
                                     stop=True), "tTR")

            @blk.scalar
            def _(sc):
                for h in range(H):
                    for kt in range(Nkt):
                        kidx = h * Nkt + kt
                        if kidx >= 3:
                            sc.wait_ge(ctx.sem["vAux"], eR0 + kidx - 2)
                            hp, ktp = divmod(kidx - 3, Nkt)
                            sc.wait_ge(ctx.sem["tMM"], t_A(hp, ktp))
                        if kt >= zone0:
                            sc.wait_ge(ctx.sem["vEvac"], zcum[kidx])
                            src = ein[:, kidx % 3]
                        else:
                            sc.wait_ge(ctx.sem["tMM"], t_S(h, kt))
                            src = psS[:, kt % 3, 0:QW]
                        ctx.inc(sc.activation(ef[:, kidx % 3], src, AF.Exp,
                                              bias=CP["neg40"]), "aMain")
                        ctx.inc(sc.activation(ehb[:, kidx % 3], src, AF.Exp,
                                              bias=CP["neg40"]), "aMain")

            @blk.vector
            def _(v):
                for h in range(H):
                    for kt in range(Nkt):
                        kidx = h * Nkt + kt
                        if kt >= zone0:
                            v.wait_ge(ctx.sem["tMM"], t_S(h, kt))
                            if zprev.get(kidx) is not None:
                                v.wait_ge(ctx.sem["aMain"],
                                          aE0 + 2 * zprev[kidx] + 2)
                            z = kt - zone0
                            moff = (s_idx * ZONE + z) * QW
                            ctx.inc(v.tensor_tensor(ein[:, kidx % 3],
                                                    psS[:, kt % 3, 0:QW],
                                                    masks_sb[:, moff:moff + QW],
                                                    OP.add), "vEvac")
                        v.wait_ge(ctx.sem["aMain"], aE0 + 2 * kidx + 2)
                        if kidx >= 3:
                            hp, ktp = divmod(kidx - 3, Nkt)
                            v.wait_ge(ctx.sem["tMM"], t_A(hp, ktp))
                        ctx.inc(v.tensor_tensor(elb[:, kidx % 3], ef[:, kidx % 3],
                                                ehb[:, kidx % 3], OP.subtract),
                                "vAux")
                    v.wait_ge(ctx.sem["tMM"], tS0 + (h + 1) * L)
                    ctx.inc(v.reciprocal(rc[:], psCS[0:1, h % 2, 0:QW]), "vMain")
                    v.wait_ge(ctx.sem["tTR"], tB0 + h + 1)
                    v.tensor_copy(rcb[:], psB[:, 0:QW])
                    v.drain()
                    ctx.inc(v.tensor_tensor(avt[:], psAV[:, h % 2, 0:QW], rcb[:],
                                            OP.mult), "vMain")
                    v.drain()
                    v.tensor_copy(avh[:, h * QW:(h + 1) * QW], avt[:])
                    v.drain()
                    ctx.inc(v.tensor_tensor(avl[:, h * QW:(h + 1) * QW], avt[:],
                                            avh[:, h * QW:(h + 1) * QW],
                                            OP.subtract), "vMain")

        with nc.Block() as blk:
            @blk.sync
            def _(s):
                for d in (kb, vb_, qb_):
                    d.wait_all(s)

    def emit_outproj_c(s_idx, avh, avl, xp):
        wo_s = DBuf(ctx, ["q2", "q3"])
        xo_s = DBuf(ctx, ["q4", "q5"])
        spx = DBuf(ctx, ["g0", "g1"])
        spa = DBuf(ctx, ["g2", "g3", "g4", "g5"])
        NGRP = D // 512
        ngf = FC // 4
        tb2 = ctx.cnt["tMM"]
        vb2 = ctx.cnt["vMain"]
        with (
            nc.sbuf_tensor(f"c_wo_{s_idx}", [P, 2, FC, DG], BF16) as wob,
            nc.psum_tensor(f"c_pso_{s_idx}", [P, 2, DG], F32) as psO,
            nc.Block() as blk,
        ):
            @blk.scalar
            def _(sc):
                for qs in range(QT):
                    xo_s.issue(sc, qs,
                               [(xp[:, qs],
                                 x_own[(s_idx * QT + qs) * P:
                                       (s_idx * QT + qs + 1) * P, :])])
                for dg in range(NDG):
                    if dg >= 2:
                        sc.wait_ge(ctx.sem["tMM"], tb2 + QT * (dg - 1))
                    wo_s.issue(sc, dg,
                               [(wob[:, dg % 2, f],
                                 s_wT["o"][f, :, dg * DG:(dg + 1) * DG])
                                for f in range(FC)])

            @blk.tensor
            def _(t):
                for dg in range(NDG):
                    wo_s.wait(t, dg)
                    for qs in range(QT):
                        kk = dg * QT + qs
                        if kk >= 2:
                            t.wait_ge(ctx.sem["vMain"], vb2 + kk - 1)
                        last = None
                        for f in range(FC):
                            lt_h = avh[:, f * QW + qs * P:f * QW + (qs + 1) * P]
                            lt_l = avl[:, f * QW + qs * P:f * QW + (qs + 1) * P]
                            t.matmul(psO[:, kk % 2], lt_h, wob[:, dg % 2, f],
                                     start=(f == 0), stop=False)
                            last = t.matmul(psO[:, kk % 2], lt_l, wob[:, dg % 2, f],
                                            start=False, stop=(f == FC - 1))
                        ctx.inc(last, "tMM")

            @blk.vector
            def _(v):
                for dg in range(NDG):
                    for qs in range(QT):
                        kk = dg * QT + qs
                        v.wait_ge(ctx.sem["tMM"], tb2 + kk + 1)
                        if dg == 0:
                            xo_s.wait(v, qs)
                        ctx.inc(v.scalar_tensor_tensor(
                            xp[:, qs, dg * DG:(dg + 1) * DG], psO[:, kk % 2],
                            gam["o"][:], xp[:, qs, dg * DG:(dg + 1) * DG],
                            OP.mult, OP.add), "vMain")

        ab = ctx.cnt["aMain"]
        xb3 = ctx.cnt["vAux"]
        veb = ctx.cnt["vEvac"]
        tb3 = ctx.cnt["tTR"]
        vb3 = ctx.cnt["vMain"]
        stages = []
        for qs in range(QT):
            for st in ("bn", "nmrec", "red", "tsr", "q8", "dq", "axl"):
                stages.append((st, qs))
        pos_nm, pos_q8, pos_dq, pos_axl, pos_aggr = {}, {}, {}, {}, {}
        vm, va = vb3, xb3
        for st, qs in stages:
            if st == "bn":
                va += 1
                pos_aggr[qs] = va
            elif st in ("nmrec", "q8", "dq", "axl"):
                vm += 1
                {"nmrec": pos_nm, "q8": pos_q8, "dq": pos_dq,
                 "axl": pos_axl}[st][qs] = vm
        pos_sq = {qs: ab + 3 * qs + 1 for qs in range(QT)}
        pos_xln = {qs: ab + 3 * qs + 2 for qs in range(QT)}
        pos_axh = {qs: ab + 3 * qs + 3 for qs in range(QT)}
        pos_t = {(qs, kg): tb3 + qs * 2 * ngf + kg + 1
                 for qs in range(QT) for kg in range(2 * ngf)}
        pos_e = {(qs, kg): veb + qs * 2 * ngf + kg + 1
                 for qs in range(QT) for kg in range(2 * ngf)}
        with (
            nc.sbuf_tensor(f"c_bns_{s_idx}", [P, NGRP * 6], F32) as bns,
            nc.sbuf_tensor(f"c_mv_{s_idx}", [P, QT, 2], F32) as mv,
            nc.sbuf_tensor(f"c_rv_{s_idx}", [P, QT], F32) as rv,
            nc.sbuf_tensor(f"c_sq_{s_idx}", [P, QT], F32) as sq,
            nc.sbuf_tensor(f"c_nm_{s_idx}", [P, QT], F32) as nm,
            nc.sbuf_tensor(f"c_xln_{s_idx}", [P, 2, D], F32) as xlnb,
            nc.sbuf_tensor(f"c_ma_{s_idx}", [P, 2, NG], F32) as ma,
            nc.sbuf_tensor(f"c_r2_{s_idx}", [P, 2, NG], F32) as r2,
            nc.sbuf_tensor(f"c_q8_{s_idx}", [P, D], FP8) as q8,
            nc.sbuf_tensor(f"c_dq_{s_idx}", [P, D], F32) as dq,
            nc.sbuf_tensor(f"c_axh_{s_idx}", [P, D], BF16) as axh,
            nc.sbuf_tensor(f"c_axl_{s_idx}", [P, D], BF16) as axl,
            nc.sbuf_tensor(f"c_stg_{s_idx}", [P, 4, 512], BF16) as stg,
            nc.psum_tensor(f"c_pst_{s_idx}", [P, 2, 1024], BF16) as pst,
            nc.Block() as blk,
        ):
            @blk.gpsimd
            def _(g):
                for qs in range(QT):
                    tile = s_idx * QT + qs
                    spx.issue(g, tile, [(s_xp[tile], xp[:, qs])])
                for qs in range(QT):
                    tile = s_idx * QT + qs
                    for kg in range(2 * ngf):
                        gk = tile * 2 * ngf + kg
                        g.wait_ge(ctx.sem["vEvac"], pos_e[(qs, kg)])
                        jg, piece = kg % ngf, kg // ngf
                        dstg = s_ax[jg * 4:(jg + 1) * 4, tile, :,
                                    piece * P:(piece + 1) * P].rearrange(
                                        "j p c -> p j c")
                        spa.issue(g, gk, [(dstg, stg[:, gk % 4])])

            @blk.scalar
            def _(sc):
                def evacs(qs):
                    tile = s_idx * QT + qs
                    for kg in range(2 * ngf):
                        qk = qs * 2 * ngf + kg
                        gk = tile * 2 * ngf + kg
                        sc.wait_ge(ctx.sem["tTR"], pos_t[(qs, kg)])
                        if qk >= 4:
                            spa.wait(sc, gk - 4)
                        ctx.inc(sc.activation(stg[:, gk % 4],
                                              pst[:, qk % 2, 0:512],
                                              AF.Identity), "vEvac")
                for qs in range(QT):
                    sc.wait_ge(ctx.sem["vAux"], pos_aggr[qs])
                    ctx.inc(sc.activation(sq[:, qs:qs + 1], mv[:, qs, 1:2],
                                          AF.Sqrt, bias=CP["eps"]), "aMain")
                    sc.wait_ge(ctx.sem["vMain"], pos_nm[qs])
                    if qs >= 2:
                        sc.wait_ge(ctx.sem["vMain"], pos_q8[qs - 2])
                    ctx.inc(sc.activation(xlnb[:, qs % 2], xp[:, qs],
                                          AF.Identity, bias=nm[:, qs:qs + 1],
                                          scale=rv[:, qs:qs + 1]), "aMain")
                    if qs >= 1:
                        evacs(qs - 1)
                    sc.wait_ge(ctx.sem["vMain"], pos_dq[qs])
                    if qs >= 1:
                        sc.wait_ge(ctx.sem["tTR"], pos_t[(qs - 1, 2 * ngf - 1)])
                    ctx.inc(sc.activation(axh[:], dq[:], AF.Identity), "aMain")
                evacs(QT - 1)

            @blk.tensor
            def _(t):
                for qs in range(QT):
                    t.wait_ge(ctx.sem["aMain"], pos_axh[qs])
                    for piece, src in ((0, axh), (1, axl)):
                        if piece == 1:
                            t.wait_ge(ctx.sem["vMain"], pos_axl[qs])
                        for jg in range(ngf):
                            kg = piece * ngf + jg
                            qk = qs * 2 * ngf + kg
                            if qk >= 2:
                                t.wait_ge(ctx.sem["vEvac"], veb + qk - 1)
                            for jj in range(4):
                                j = jg * 4 + jj
                                ins = t.transpose(
                                    pst[:, qk % 2, jj * P:(jj + 1) * P],
                                    src[:, j * P:(j + 1) * P], ident[:])
                            ctx.inc(ins, "tTR")

            @blk.vector
            def _(v):
                for st, qs in stages:
                    if st == "bn":
                        for gi in range(NGRP):
                            v.bn_stats(bns[:, gi * 6:gi * 6 + 6],
                                       xp[:, qs, gi * 512:(gi + 1) * 512])
                        v.drain()
                        ctx.inc(v.bn_aggr(mv[:, qs], bns[:]), "vAux")
                    elif st == "nmrec":
                        v.wait_ge(ctx.sem["aMain"], pos_sq[qs])
                        v.reciprocal(rv[:, qs:qs + 1], sq[:, qs:qs + 1])
                        v.drain()
                        ctx.inc(v.scalar_tensor_tensor(
                            nm[:, qs:qs + 1], mv[:, qs, 0:1], -1.0,
                            rv[:, qs:qs + 1], OP.mult, OP.mult), "vMain")
                    elif st == "red":
                        v.wait_ge(ctx.sem["aMain"], pos_xln[qs])
                        x3 = xlnb[:, qs % 2].rearrange("p (g b) -> p g b", b=32)
                        v.tensor_reduce(ma[:, qs % 2], x3, AX.X, OP.max,
                                        apply_absolute_value=True)
                        v.drain()
                    elif st == "tsr":
                        v.tensor_scalar(ma[:, qs % 2], ma[:, qs % 2],
                                        2.0 / 448.0, 2e-30, OP.mult, OP.max)
                        v.drain()
                        v.reciprocal(r2[:, qs % 2], ma[:, qs % 2])
                        v.drain()
                    elif st == "q8":
                        x3 = xlnb[:, qs % 2].rearrange("p (g b) -> p g b", b=32)
                        ctx.inc(v.tensor_tensor(
                            q8.rearrange("p (g b) -> p g b", b=32), x3,
                            bcast_ap(r2[:, qs % 2], 32), OP.mult), "vMain")
                        v.drain()
                    elif st == "dq":
                        if qs >= 1:
                            v.wait_ge(ctx.sem["aMain"], pos_axh[qs - 1])
                        ctx.inc(v.tensor_tensor(
                            dq.rearrange("p (g b) -> p g b", b=32),
                            q8.rearrange("p (g b) -> p g b", b=32),
                            bcast_ap(ma[:, qs % 2], 32), OP.mult), "vMain")
                    elif st == "axl":
                        v.wait_ge(ctx.sem["aMain"], pos_axh[qs])
                        if qs >= 1:
                            v.wait_ge(ctx.sem["tTR"],
                                      pos_t[(qs - 1, 2 * ngf - 1)])
                        ctx.inc(v.tensor_tensor(axl[:], dq[:], axh[:],
                                                OP.subtract), "vMain")

        with nc.Block() as blk:
            @blk.sync
            def _(s):
                for d in (wo_s, xo_s, spx, spa):
                    d.wait_all(s)

    for s_idx in (range(NSLOT) if "b" in phases else ()):
        with (
            nc.sbuf_tensor(f"b_avh_{s_idx}", [P, H * QW], BF16) as avh,
            nc.sbuf_tensor(f"b_avl_{s_idx}", [P, H * QW], BF16) as avl,
        ):
            emit_attention(s_idx, avh, avl)
            with nc.sbuf_tensor(f"b_xp_{s_idx}", [P, QT, D], F32) as xp:
                emit_outproj_c(s_idx, avh, avl, xp)

    def emit_d1():
        wg_s = DBuf(ctx, ["q0", "q1"])
        wu_s = DBuf(ctx, ["q2", "q3"])
        ax_s = DBuf(ctx, ["q4", "q5"])
        sph = DBuf(ctx, ["g0", "g1", "g2", "g3"])
        NITER = NI * NTO
        with (
            nc.sbuf_tensor("d1_wg", [P, 2, FC, 1024], BF16) as wgb,
            nc.sbuf_tensor("d1_wu", [P, 2, FC, 1024], BF16) as wub,
            nc.sbuf_tensor("d1_ax", [P, 2, FC, 2 * P], BF16) as axb,
            nc.sbuf_tensor("d1_gs", [P, 512], F32) as gs,
            nc.sbuf_tensor("d1_h", [P, 512], F32) as hb,
            nc.sbuf_tensor("d1_ma", [P, NGI], F32) as ma,
            nc.sbuf_tensor("d1_r2", [P, NGI], F32) as r2,
            nc.sbuf_tensor("d1_q8", [P, 512], FP8) as q8,
            nc.sbuf_tensor("d1_dq", [P, 512], F32) as dq,
            nc.sbuf_tensor("d1_hh", [P, 512], BF16) as hh,
            nc.sbuf_tensor("d1_hl", [P, 512], BF16) as hl,
            nc.sbuf_tensor("d1_stg", [P, 2, 512], BF16) as stg,
            nc.psum_tensor("d1_psg", [P, 2, 512], F32) as psG,
            nc.psum_tensor("d1_psu", [P, 2, 512], F32) as psU,
            nc.psum_tensor("d1_pst", [P, 2, 1024], BF16) as pst,
            nc.Block() as blk,
        ):
            tb_ = ctx.cnt["tMM"]
            tr_ = ctx.cnt["tTR"]
            ab = ctx.cnt["aMain"]
            vb = ctx.cnt["vMain"]
            veb = ctx.cnt["vEvac"]

            @blk.sync
            def _(s):
                for ic in range(NI):
                    if ic >= 2:
                        s.wait_ge(ctx.sem["tMM"], tb_ + 2 * NTO * (ic - 1))
                    wg_s.issue(s, ic, [(wgb[:, ic % 2, f], s_wg[f, ic])
                                       for f in range(FC)])
                    wu_s.issue(s, ic, [(wub[:, ic % 2, f], s_wu[f, ic])
                                       for f in range(FC)])
                    for tt in range(NTO):
                        it = ic * NTO + tt
                        if it >= 2:
                            s.wait_ge(ctx.sem["tMM"], tb_ + 2 * (it - 1))
                        half = FC // 2
                        pairs = []
                        for j0 in range(0, FC, half):
                            src = s_ax[j0:j0 + half, tt].rearrange("f p c -> p f c")
                            pairs.append((axb[:, it % 2, j0:j0 + half], src))
                        ax_s.issue(s, it, pairs)

            @blk.gpsimd
            def _(g):
                for it in range(NITER):
                    ic, tt = divmod(it, NTO)
                    for piece in range(2):
                        kg = it * 2 + piece
                        g.wait_ge(ctx.sem["vEvac"], veb + kg + 1)
                        dstg = s_h[ic * 4:(ic + 1) * 4, tt, :,
                                   piece * P:(piece + 1) * P].rearrange(
                                       "j p c -> p j c")
                        sph.issue(g, kg, [(dstg, stg[:, piece])])

            @blk.tensor
            def _(t):
                def tr_group(it):
                    t.wait_ge(ctx.sem["vMain"], vb + 3 * it + 3)
                    for piece, src in ((0, hh), (1, hl)):
                        kg = it * 2 + piece
                        if kg >= 2:
                            t.wait_ge(ctx.sem["vEvac"], veb + kg - 1)
                        for jj in range(4):
                            ins = t.transpose(pst[:, kg % 2, jj * P:(jj + 1) * P],
                                              src[:, jj * P:(jj + 1) * P],
                                              ident[:])
                        ctx.inc(ins, "tTR")

                for it in range(NITER):
                    ic, tt = divmod(it, NTO)
                    if tt == 0:
                        wg_s.wait(t, ic)
                        wu_s.wait(t, ic)
                    ax_s.wait(t, it)
                    if it >= 2:
                        t.wait_ge(ctx.sem["aMain"], ab + it - 1)
                        t.wait_ge(ctx.sem["vMain"], vb + 3 * it - 5)
                    for f in range(FC):
                        ah = axb[:, it % 2, f, 0:P]
                        al = axb[:, it % 2, f, P:2 * P]
                        t.matmul(psG[:, it % 2], ah, wgb[:, ic % 2, f, 0:512],
                                 start=(f == 0), stop=False)
                        t.matmul(psG[:, it % 2], ah, wgb[:, ic % 2, f, 512:1024],
                                 start=False, stop=False)
                        t.matmul(psU[:, it % 2], ah, wub[:, ic % 2, f, 0:512],
                                 start=(f == 0), stop=False)
                        t.matmul(psU[:, it % 2], ah, wub[:, ic % 2, f, 512:1024],
                                 start=False, stop=False)
                        lg = t.matmul(psG[:, it % 2], al,
                                      wgb[:, ic % 2, f, 0:512], start=False,
                                      stop=(f == FC - 1))
                        lu = t.matmul(psU[:, it % 2], al,
                                      wub[:, ic % 2, f, 0:512], start=False,
                                      stop=(f == FC - 1))
                    ctx.inc(lg, "tMM")
                    ctx.inc(lu, "tMM")
                    if it >= 1:
                        tr_group(it - 1)
                if NITER >= 1:
                    tr_group(NITER - 1)

            @blk.scalar
            def _(sc):
                for it in range(NITER):
                    sc.wait_ge(ctx.sem["tMM"], tb_ + 2 * it + 1)
                    if it >= 1:
                        sc.wait_ge(ctx.sem["vMain"], vb + 3 * (it - 1) + 1)
                    ctx.inc(sc.activation(gs[:], psG[:, it % 2], AF.Silu), "aMain")

            @blk.vector
            def _(v):
                for it in range(NITER):
                    v.wait_ge(ctx.sem["aMain"], ab + it + 1)
                    v.wait_ge(ctx.sem["tMM"], tb_ + 2 * it + 2)
                    ctx.inc(v.tensor_tensor(hb[:], gs[:], psU[:, it % 2], OP.mult),
                            "vMain")
                    v.drain()
                    h3 = hb.rearrange("p (g b) -> p g b", b=32)
                    v.tensor_reduce(ma[:], h3, AX.X, OP.max,
                                    apply_absolute_value=True)
                    v.drain()
                    v.tensor_scalar(ma[:], ma[:], 2.0 / 448.0, 2e-30, OP.mult,
                                    OP.max)
                    v.drain()
                    v.reciprocal(r2[:], ma[:])
                    v.drain()
                    v.tensor_tensor(q8.rearrange("p (g b) -> p g b", b=32), h3,
                                    bcast_ap(r2, 32), OP.mult)
                    v.drain()
                    v.tensor_tensor(dq.rearrange("p (g b) -> p g b", b=32),
                                    q8.rearrange("p (g b) -> p g b", b=32),
                                    bcast_ap(ma, 32), OP.mult)
                    v.drain()
                    if it >= 1:
                        v.wait_ge(ctx.sem["tTR"], tr_ + 2 * it)
                    ctx.inc(v.tensor_copy(hh[:], dq[:]), "vMain")
                    v.drain()
                    ctx.inc(v.tensor_tensor(hl[:], dq[:], hh[:], OP.subtract),
                            "vMain")
                    for piece in range(2):
                        kg = it * 2 + piece
                        v.wait_ge(ctx.sem["tTR"], tr_ + kg + 1)
                        if kg >= 2:
                            sph.wait(v, kg - 2)
                        ctx.inc(v.tensor_copy(stg[:, piece],
                                              pst[:, kg % 2, 0:512]), "vEvac")

        with nc.Block() as blk:
            @blk.sync
            def _(s):
                for d in (wg_s, wu_s, ax_s, sph):
                    d.wait_all(s)

    masks_cm.__exit__(None, None, None)
    if "d1" in phases:
        emit_d1()

    def emit_d2():
        wd_s = DBuf(ctx, ["q0", "q1"])
        h_s = DBuf(ctx, ["q2", "q3"])
        xp_s = DBuf(ctx, ["q4", "q5"])
        out_s = DBuf(ctx, ["g0", "g1"])
        with (
            nc.sbuf_tensor("d2_wd", [P, NIS, 2 * OCW], BF16) as wdb,
            nc.sbuf_tensor("d2_h", [P, 2, NIS, 2 * P], BF16) as hbf,
            nc.sbuf_tensor("d2_xp", [P, 2, OCW], F32) as xpb,
            nc.sbuf_tensor("d2_yo", [P, 4, OCW], F32) as yob,
            nc.psum_tensor("d2_ps", [P, 2, 512], F32) as psD,
            nc.Block() as blk,
        ):
            tb_ = ctx.cnt["tMM"]
            vb = ctx.cnt["vMain"]

            @blk.sync
            def _(s):
                for oc in range(NOCH):
                    if oc >= 1:
                        s.wait_ge(ctx.sem["tMM"], tb_ + oc * NTO)
                    wd_s.issue(s, oc, [(wdb[:, isb], s_wd[isb, oc])
                                       for isb in range(NIS)])
                    for tt in range(NTO):
                        it = oc * NTO + tt
                        if it >= 2:
                            s.wait_ge(ctx.sem["tMM"], tb_ + it - 1)
                        half = NIS // 4
                        pairs = []
                        for j0 in range(0, NIS, half):
                            src = s_h[j0:j0 + half, tt].rearrange("f p c -> p f c")
                            pairs.append((hbf[:, it % 2, j0:j0 + half], src))
                        h_s.issue(s, it, pairs)
                        if it >= 2:
                            s.wait_ge(ctx.sem["vMain"], vb + it - 1)
                        xp_s.issue(s, it, [(xpb[:, it % 2],
                                            s_xp[tt, :, oc * OCW:(oc + 1) * OCW])])

            @blk.gpsimd
            def _(g):
                for oc in range(NOCH):
                    for tt in range(NTO):
                        it = oc * NTO + tt
                        if it >= 2:
                            out_s.wait(g, it - 2)
                        g.wait_ge(ctx.sem["vMain"], vb + it + 1)
                        out_s.issue(g, it, [(y_out[tt * P:(tt + 1) * P,
                                                   oc * OCW:(oc + 1) * OCW],
                                             yob[:, it % 4])])

            @blk.tensor
            def _(t):
                for oc in range(NOCH):
                    for tt in range(NTO):
                        it = oc * NTO + tt
                        if tt == 0:
                            wd_s.wait(t, oc)
                        h_s.wait(t, it)
                        if it >= 2:
                            t.wait_ge(ctx.sem["vMain"], vb + it - 1)
                        last = None
                        for isb in range(NIS):
                            hh_ = hbf[:, it % 2, isb, 0:P]
                            hl_ = hbf[:, it % 2, isb, P:2 * P]
                            wh = wdb[:, isb, 0:OCW]
                            wl = wdb[:, isb, OCW:2 * OCW]
                            t.matmul(psD[:, it % 2, 0:OCW], hh_, wh, start=(isb == 0),
                                     stop=False)
                            t.matmul(psD[:, it % 2, 0:OCW], hh_, wl, start=False,
                                     stop=False)
                            last = t.matmul(psD[:, it % 2, 0:OCW], hl_, wh, start=False,
                                            stop=(isb == NIS - 1))
                        ctx.inc(last, "tMM")

            @blk.vector
            def _(v):
                for oc in range(NOCH):
                    for tt in range(NTO):
                        it = oc * NTO + tt
                        v.wait_ge(ctx.sem["tMM"], tb_ + it + 1)
                        xp_s.wait(v, it)
                        if it >= 4:
                            out_s.wait(v, it - 4)
                        ctx.inc(v.tensor_tensor(yob[:, it % 4], psD[:, it % 2, 0:OCW],
                                                xpb[:, it % 2], OP.add), "vMain")

        with nc.Block() as blk:
            @blk.sync
            def _(s):
                for d in (wd_s, h_s, xp_s, out_s):
                    d.wait_all(s)

    if "d2" in phases:
        emit_d2()

    with nc.Block() as blk:
        @blk.sync
        def _(s):
            for nm_ in ctx.sem:
                if nm_.startswith(("q", "g")) or nm_ == "dS":
                    ctx.wait(s, nm_)

    return nc, ctx



def _shard_inputs(inputs, cfg):
    D, H, T, TO, QW = (cfg[k] for k in ("D", "H", "T", "TO", "QW"))
    NSLOT = TO // QW
    ZONE = 2 * QW // P
    x = np.ascontiguousarray(np.asarray(inputs["L0_attn_x"], dtype=np.float32))
    B = x.shape[0]
    shared = dict(
        wq=np.ascontiguousarray(np.asarray(inputs["L0_attn_wq"], np.float32)),
        wk=np.ascontiguousarray(np.asarray(inputs["L0_attn_wk"], np.float32)),
        wv=np.ascontiguousarray(np.asarray(inputs["L0_attn_wv"], np.float32)),
        wo=np.ascontiguousarray(np.asarray(inputs["L0_attn_wo"], np.float32)),
        gate_w=np.ascontiguousarray(np.asarray(inputs["L0_ffn_gate_w"], np.float32)),
        up_w=np.ascontiguousarray(np.asarray(inputs["L0_ffn_up_w"], np.float32)),
        down_w=np.ascontiguousarray(np.asarray(inputs["L0_ffn_down_w"], np.float32)),
    )
    in_maps = []
    for c in range(2 * B):
        b, c2 = divmod(c, 2)
        xb = x[b]
        x_own = np.concatenate([xb[(2 * s + c2) * QW:(2 * s + c2 + 1) * QW]
                                for s in range(NSLOT)], 0)
        m = np.zeros((NSLOT, ZONE, P, QW), np.float32)
        for s in range(NSLOT):
            Es = (2 * s + 2) * QW
            qs = (2 * s + c2) * QW
            for z in range(ZONE):
                kt = Es // P - ZONE + z
                kglob = kt * P + np.arange(P)[:, None]
                qglob = qs + np.arange(QW)[None, :]
                m[s, z] = np.where(kglob <= qglob, 0.0, -240.0)
        in_maps.append(dict(shared,
                            x_batch=np.ascontiguousarray(xb),
                            x_own=np.ascontiguousarray(x_own),
                            cmask=m.astype(ml_dtypes.float8_e4m3)))
    return in_maps


def _assemble_output(results, cfg, B):
    D, TO, QW, T = cfg["D"], cfg["TO"], cfg["QW"], cfg["T"]
    NSLOT = TO // QW
    out = np.empty((B, T, D), np.float32)
    for c, r in enumerate(results):
        b, c2 = divmod(c, 2)
        yo = np.asarray(r["y_out"]).reshape(TO, D)
        for s in range(NSLOT):
            out[b, (2 * s + c2) * QW:(2 * s + c2 + 1) * QW] = \
                yo[s * QW:(s + 1) * QW]
    return out


def kernel(**inputs):
    cfg = full_cfg()
    x = np.asarray(inputs["L0_attn_x"])
    B = x.shape[0]
    nc, _ = build_program(cfg)
    in_maps = _shard_inputs(inputs, cfg)
    res = run_bass_kernel_spmd(nc, in_maps, core_ids=list(range(len(in_maps))))
    return _assemble_output(res.results, cfg, B)
